# version 01
# baseline (speedup 1.0000x reference)
import numpy as np
import ml_dtypes
import concourse.bass as bass
import concourse.mybir as mybir
from concourse.bass_utils import run_bass_kernel_spmd

F32 = mybir.dt.float32
BF16 = mybir.dt.bfloat16
AF = mybir.ActivationFunctionType
ALU = mybir.AluOpType
AX = mybir.AxisListType

B, S, D = 4, 4096, 2048
QUADS = [[0, 3, 4, 7], [1, 2, 5, 6]]
NKT = [8, 16, 24, 32]


def slot_tokens(role):
    return np.concatenate([np.arange(512 * q, 512 * (q + 1)) for q in QUADS[role]])


class Sem:
    def __init__(self, nc, name):
        self.h = nc.alloc_semaphore(name)


def rope_host_tables():
    inv = 1.0 / (10000.0 ** (np.arange(0, 128, 2, dtype=np.float64) / 128))
    pos = np.arange(S, dtype=np.float64)
    fr = pos[:, None] * inv[None, :]
    emb = np.concatenate([fr, fr], axis=-1)
    return np.cos(emb).astype(np.float32), np.sin(emb).astype(np.float32)


def pi_matrix():
    P = np.zeros((128, 128), dtype=np.float32)
    for i in range(64):
        P[i, 64 + i] = -1.0
        P[64 + i, i] = 1.0
    return P



def build_l0_proj():
    nc = bass.Bass("TRN2", target_bir_lowering=False, debug=False)
    uT = nc.dram_tensor("uT", [D, 2048], F32, kind="ExternalInput").ap()
    WqT = nc.dram_tensor("WqT", [D, 4096], F32, kind="ExternalInput").ap()
    WkT = nc.dram_tensor("WkT", [D, 1024], F32, kind="ExternalInput").ap()
    WvT = nc.dram_tensor("WvT", [D, 1024], F32, kind="ExternalInput").ap()
    ROPE = nc.dram_tensor("ROPE", [128, 4 * 2048], F32, kind="ExternalInput").ap()
    PIT = nc.dram_tensor("PIT", [128, 128], F32, kind="ExternalInput").ap()
    Qo = nc.dram_tensor("Qo", [4096, 2048], BF16, kind="ExternalOutput").ap()
    Ko = nc.dram_tensor("Ko", [1024, 2048], BF16, kind="ExternalOutput").ap()
    Vo = nc.dram_tensor("Vo", [1024, 2048], BF16, kind="ExternalOutput").ap()

    NK = 16
    ub = nc.alloc_sbuf_tensor("ub", [128, NK * 2048], BF16)
    wt = [nc.alloc_sbuf_tensor(f"wt{i}", [128, NK * 128], BF16) for i in range(2)]
    rope = nc.alloc_sbuf_tensor("rope", [128, 4 * 2048], F32)
    pit = nc.alloc_sbuf_tensor("pit", [128, 128], BF16)
    qraw = [nc.alloc_sbuf_tensor(f"qraw{i}", [128, 512], BF16) for i in range(2)]
    t1 = nc.alloc_sbuf_tensor("t1", [128, 512], F32)
    t2 = nc.alloc_sbuf_tensor("t2", [128, 512], F32)
    dum = nc.alloc_sbuf_tensor("dum", [1, 8], F32)
    dum2 = nc.alloc_sbuf_tensor("dum2", [1, 8], F32)
    stage = [nc.alloc_sbuf_tensor(f"stage{i}", [128, 2048], BF16) for i in range(2)]
    psA = nc.alloc_psum_tensor("psA", [128, 512], F32)
    psB = nc.alloc_psum_tensor("psB", [128, 512], F32)
    psC = nc.alloc_psum_tensor("psC", [128, 512], F32)

    wld = [Sem(nc, "wld0"), Sem(nc, "wld1")]; uld = Sem(nc, "uld"); mmA = Sem(nc, "mmA")
    veq = Sem(nc, "veq"); rot = Sem(nc, "rot"); gfin = Sem(nc, "gfin")
    sfl = Sem(nc, "sfl"); sto = [Sem(nc, "sto0"), Sem(nc, "sto1")]

    MTOT = 48
    NQK = 160
    kind = lambda im: ("q" if im < 32 else ("k" if im < 40 else "v"))

    def wsrc(im):
        if im < 32:
            return WqT, im
        if im < 40:
            return WkT, im - 32
        return WvT, im - 40

    with nc.Block() as block:
        @block.gpsimd
        def _(gpsimd):
            gpsimd.dma_start(out=ub[:].rearrange("p (k t) -> p k t", k=NK),
                             in_=uT.rearrange("(k p) t -> p k t", p=128)).then_inc(uld.h, 16)
            gpsimd.dma_start(out=rope[:], in_=ROPE).then_inc(uld.h, 16)
            gpsimd.dma_start(out=pit[:], in_=PIT).then_inc(uld.h, 16)
            for im in range(MTOT):
                Wd, m = wsrc(im)
                if im >= 2:
                    gpsimd.wait_ge(mmA.h, 4 * (im - 1))
                gpsimd.dma_start(
                    out=wt[im % 2][:].rearrange("p (k f) -> p k f", k=NK),
                    in_=Wd.rearrange("(k p) f -> p k f", p=128)[:, :, m * 128:(m + 1) * 128]
                ).then_inc(wld[im % 2].h, 16)

        @block.tensor
        def _(tensor):
            tensor.wait_ge(uld.h, 48)
            for im in range(MTOT):
                tensor.wait_ge(wld[im % 2].h, 16 * (im // 2 + 1))
                for n in range(4):
                    g = im * 4 + n
                    ps = psA if n % 2 == 0 else psB
                    if g >= 2:
                        if g - 2 >= NQK:
                            tensor.wait_ge(gfin.h, g - 1)
                        else:
                            tensor.wait_ge(veq.h, g - 1)
                    inst = None
                    for k in range(NK):
                        inst = tensor.matmul(
                            ps[:],
                            wt[im % 2][:, k * 128:(k + 1) * 128],
                            ub[:, k * 2048 + n * 512: k * 2048 + (n + 1) * 512],
                            start=(k == 0), stop=(k == NK - 1))
                    inst.then_inc(mmA.h, 1)
                    gp = g - 1
                    if 0 <= gp < NQK and g <= NQK:
                        tensor.wait_ge(veq.h, g + 1)
                        if gp >= 1:
                            tensor.wait_ge(gfin.h, gp)
                        tensor.matmul(psC[:], pit[:], qraw[gp % 2][:],
                                      start=True, stop=True).then_inc(rot.h, 1)

        @block.vector
        def _(vector):
            nrope = 0

            def rope_emit(gp):
                nonlocal nrope
                imp, np_ = divmod(gp, 4)
                knd = kind(imp)
                if np_ == 0 and imp >= 2:
                    vector.wait_ge(sto[imp % 2].h, 16 * (imp // 2))
                off = 0 if knd == "q" else 2 * 2048
                cos = rope[:, off + np_ * 512: off + (np_ + 1) * 512]
                sin = rope[:, off + 2048 + np_ * 512: off + 2048 + (np_ + 1) * 512]
                vector.tensor_tensor(t1[:], qraw[gp % 2][:], cos, ALU.mult)
                nrope += 1
                vector.wait_ge(rot.h, nrope)
                vector.tensor_tensor(t2[:], psC[:], sin, ALU.mult)
                vector.tensor_add(stage[imp % 2][:, np_ * 512:(np_ + 1) * 512],
                                  t1[:], t2[:]).then_inc(gfin.h, 1)
                if np_ == 3:
                    vector.tensor_copy(dum2[0:1, 0:8], pit[0:1, 0:8]).then_inc(sfl.h, 1)

            for g in range(NQK):
                vector.wait_ge(mmA.h, g + 1)
                ps = psA if g % 2 == 0 else psB
                vector.tensor_copy(qraw[g % 2][:], ps[:]).then_inc(veq.h, 1)
                if g == NQK - 1:
                    vector.tensor_copy(dum2[0:1, 0:8], pit[0:1, 0:8]).then_inc(veq.h, 1)
                if g >= 1:
                    rope_emit(g - 1)
            rope_emit(NQK - 1)
            for g in range(NQK, 192):
                im, n = divmod(g, 4)
                if n == 0 and im >= 2:
                    vector.wait_ge(sto[im % 2].h, 16 * (im // 2))
                vector.wait_ge(mmA.h, g + 1)
                ps = psA if g % 2 == 0 else psB
                vector.tensor_copy(stage[im % 2][:, n * 512:(n + 1) * 512], ps[:]
                                   ).then_inc(gfin.h, 1)
                if n == 3:
                    vector.tensor_copy(dum2[0:1, 0:8], pit[0:1, 0:8]).then_inc(sfl.h, 1)

        @block.sync
        def _(sync):
            for im in range(MTOT):
                knd = kind(im)
                sync.wait_ge(gfin.h, 4 * (im + 1))
                sync.wait_ge(sfl.h, im + 1)
                if knd == "q":
                    dst = Qo.rearrange("(m p) t -> p m t", p=128)[:, im]
                elif knd == "k":
                    dst = Ko.rearrange("(m p) t -> p m t", p=128)[:, im - 32]
                else:
                    dst = Vo.rearrange("(m p) t -> p m t", p=128)[:, im - 40]
                sync.dma_start(out=dst, in_=stage[im % 2][:]).then_inc(sto[im % 2].h, 16)
            sync.wait_ge(sto[0].h, 16 * (MTOT // 2))
            sync.wait_ge(sto[1].h, 16 * (MTOT // 2))

    return nc



def attn_tiles():
    tiles = []
    for g in range(4):
        for kt in range(NKT[g]):
            masked = kt >= NKT[g] - 8
            j = kt - (NKT[g] - 8)
            tiles.append((g, kt, masked, j))
    assert len(tiles) == 80
    return tiles




DBG_NH = 32
DBG_NOPDUM = False


def build_l0_attn():
    nc = bass.Bass("TRN2", target_bir_lowering=False, debug=False)
    Q = nc.dram_tensor("Q", [4096, 2048], BF16, kind="ExternalInput").ap()
    K = nc.dram_tensor("K", [1024, 4096], BF16, kind="ExternalInput").ap()
    Vt = nc.dram_tensor("Vt", [128, 8 * 32 * 128], BF16, kind="ExternalInput").ap()
    MSK = nc.dram_tensor("MSK", [128, 16 * 512], BF16, kind="ExternalInput").ap()
    ONE = nc.dram_tensor("ONE", [128, 1], BF16, kind="ExternalInput").ap()
    Oo = nc.dram_tensor("Oo", [4096, 2048], BF16, kind="ExternalOutput").ap()
    scr = nc.dram_tensor("scr", [4, 2048], F32, kind="Internal").ap()

    kb = nc.alloc_sbuf_tensor("kb", [128, 8 * 4096], BF16)
    vb = nc.alloc_sbuf_tensor("vb", [128, 8 * 32 * 128], BF16)
    msk = nc.alloc_sbuf_tensor("msk", [128, 16 * 512], BF16)
    one = nc.alloc_sbuf_tensor("one", [128, 1], BF16)
    qhb = [nc.alloc_sbuf_tensor(f"qhb{i}", [128, 2048], BF16) for i in range(2)]
    pt = [nc.alloc_sbuf_tensor(f"pt{i}", [128, 512], BF16) for i in range(4)]
    ptm = [nc.alloc_sbuf_tensor(f"ptm{i}", [128, 512], BF16) for i in range(4)]
    ost = [nc.alloc_sbuf_tensor(f"ost{i}", [128, 2048], BF16) for i in range(2)]
    rs = nc.alloc_sbuf_tensor("rs", [1, 2048], F32)
    rcpr = nc.alloc_sbuf_tensor("rcpr", [1, 2048], F32)
    warm = nc.alloc_sbuf_tensor("warm", [1, 32], F32)
    dum = nc.alloc_sbuf_tensor("dum", [1, 8], F32)
    dum2 = nc.alloc_sbuf_tensor("dum2", [1, 8], F32)
    recipb = [nc.alloc_sbuf_tensor(f"recipb{i}", [128, 2048], F32) for i in range(2)]
    psS = [nc.alloc_psum_tensor(f"psS{i}", [128, 512], F32) for i in range(3)]
    psO = [nc.alloc_psum_tensor(f"psO{i}", [128, 512], F32) for i in range(4)]
    psC = [nc.alloc_psum_tensor(f"psC{i}", [1, 512], F32) for i in range(1)]

    ld = Sem(nc, "ld"); qld = [Sem(nc, "qld0"), Sem(nc, "qld1")]; smm = Sem(nc, "smm"); wse = Sem(nc, "wse")
    exq = Sem(nc, "exq"); mskd = Sem(nc, "mskd"); mskp = Sem(nc, "mskp"); avm = Sem(nc, "avm")
    rcq = Sem(nc, "rcq")
    avq = Sem(nc, "avq"); cse = Sem(nc, "cse"); rcp = Sem(nc, "rcp")
    rw = Sem(nc, "rw"); rb = Sem(nc, "rb"); oev = Sem(nc, "oev")
    sto = [Sem(nc, "sto0"), Sem(nc, "sto1")]

    NH = DBG_NH
    tiles = attn_tiles()
    mreal_of = {}
    mreal = 0
    for h in range(NH):
        for i, (g, kt, masked, j) in enumerate(tiles):
            if masked:
                mreal_of[(h, i)] = mreal
                mreal += 1

    with nc.Block() as block:
        @block.gpsimd
        def _(gpsimd):
            gpsimd.dma_start(out=kb[:].rearrange("p (h t) -> p h t", h=8),
                             in_=K.rearrange("(h p) t -> p h t", p=128)).then_inc(ld.h, 16)
            gpsimd.dma_start(out=vb[:], in_=Vt).then_inc(ld.h, 16)
            gpsimd.dma_start(out=msk[:], in_=MSK).then_inc(ld.h, 16)
            gpsimd.dma_start(out=one[:], in_=ONE).then_inc(ld.h, 16)
            for h in range(NH):
                if h >= 2:
                    gpsimd.wait_ge(avq.h, (h - 1) * 4)
                gpsimd.dma_start(out=qhb[h % 2][:],
                                 in_=Q.rearrange("(m p) t -> p m t", p=128)[:, h]
                                 ).then_inc(qld[h % 2].h, 16)

        @block.tensor
        def _(tensor):
            tensor.wait_ge(ld.h, 64)
            LA = 3
            for h in range(NH):
                kvh = h // 4
                tensor.wait_ge(qld[h % 2].h, 16 * (h // 2 + 1))
                for i in range(80 + LA):
                    if i < 80:
                        g, kt, masked, j = tiles[i]
                        tot = h * 80 + i
                        if tot >= LA:
                            tp = tot - LA
                            tensor.wait_ge(exq.h, 81 * (tp // 80) + (tp % 80) + 1)
                        tensor.matmul(psS[tot % 3][:],
                                      kb[:, kvh * 4096 + kt * 128: kvh * 4096 + (kt + 1) * 128],
                                      qhb[h % 2][:, g * 512:(g + 1) * 512],
                                      start=True, stop=True).then_inc(smm.h, 1)
                    if i >= LA:
                        g, kt, masked, j = tiles[i - LA]
                        tot = h * 80 + i - LA
                        gi = h * 4 + g
                        if masked:
                            tensor.wait_ge(mskp.h, mreal_of[(h, i - LA)] + 1)
                            tensor.wait_ge(mskd.h, mreal_of[(h, i - LA)] + 1)
                            src = ptm[mreal_of[(h, i - LA)] % 4]
                        else:
                            tensor.wait_ge(exq.h, 81 * h + min((i - LA) + 2, 81))
                            src = pt[tot % 4]
                        if kt == 0 and h >= 1:
                            tensor.wait_ge(oev.h, 5 * h)
                        av = tensor.matmul(psO[g][:],
                                           vb[:, (kvh * 32 + kt) * 128:(kvh * 32 + kt + 1) * 128],
                                           src[:], start=(kt == 0), stop=(kt == NKT[g] - 1))
                        if masked:
                            av.then_inc(avm.h, 1)
                        if kt == 0 and gi >= 1:
                            tensor.wait_ge(cse.h, gi)
                        ci = tensor.matmul(psC[0][:], one[:], src[:],
                                           start=(kt == 0), stop=(kt == NKT[g] - 1))
                        if kt == NKT[g] - 1:
                            ci.then_inc(avq.h, 1)

        @block.scalar
        def _(scalar):
            wn = [0]

            def wdummy():
                scalar.wait_ge(wse.h, wn[0])
                wn[0] += 1
                if DBG_NOPDUM:
                    return scalar.nop().then_inc(wse.h, 1)
                return scalar.activation(warm[:, 0:16], warm[:, 16:32], AF.Exp
                                         ).then_inc(wse.h, 1)

            scalar.memzero(warm[:]).then_inc(wse.h, 1)
            wn[0] = 1
            wdummy()
            wdummy()
            for h in range(NH):
                for i in range(80):
                    tot = h * 80 + i
                    scalar.wait_ge(smm.h, tot + 1)
                    if tot >= 4:
                        h2, i2 = divmod(tot - 4, 80)
                        if tiles[i2][2]:
                            scalar.wait_ge(mskp.h, mreal_of[(h2, i2)] + 1)
                    scalar.activation(pt[tot % 4][:], psS[tot % 3][:], AF.Exp
                                      ).then_inc(exq.h, 1)
                scalar.wait_ge(wse.h, 3)
                scalar.wait_ge(exq.h, 81 * h + 80)
                if DBG_NOPDUM:
                    scalar.nop().then_inc(exq.h, 1)
                else:
                    scalar.activation(warm[:, 0:16], warm[:, 16:32], AF.Exp
                                      ).then_inc(exq.h, 1)

        @block.vector
        def _(vector):
            for h in range(NH):
                for i, (g, kt, masked, j) in enumerate(tiles):
                    tot = h * 80 + i
                    if masked:
                        mr = mreal_of[(h, i)]
                        vector.wait_ge(exq.h, 81 * h + min(i + 2, 81))
                        if mr >= 4:
                            vector.wait_ge(avm.h, mr - 3)
                        vector.tensor_tensor(ptm[mr % 4][:], pt[tot % 4][:],
                                             msk[:, ((g % 2) * 8 + j) * 512:((g % 2) * 8 + j + 1) * 512],
                                             ALU.mult).then_inc(mskp.h, 1)
                        (vector.nop() if DBG_NOPDUM else vector.tensor_copy(dum2[0:1, 0:1], one[0:1, 0:1])).then_inc(mskd.h, 1)
                    if kt == NKT[g] - 1:
                        gi = h * 4 + g
                        vector.wait_ge(avq.h, gi + 1)
                        vector.tensor_copy(rs[:, g * 512:(g + 1) * 512],
                                           psC[0][:]).then_inc(cse.h, 1)
                vector.wait_ge(cse.h, 4 * (h + 1))
                if h >= 1:
                    vector.wait_ge(rw.h, 16 * h)
                vector.reciprocal(rcpr[:], rs[:]).then_inc(rcq.h, 1)
                (vector.nop() if DBG_NOPDUM else vector.tensor_copy(dum2[0:1, 0:1], one[0:1, 0:1])).then_inc(rcp.h, 1)
                if h >= 2:
                    vector.wait_ge(sto[h % 2].h, 16 * (h // 2))
                vector.wait_ge(rb.h, 16 * (h + 1))
                for g in range(4):
                    vector.tensor_tensor(ost[h % 2][:, g * 512:(g + 1) * 512],
                                         psO[g][:], recipb[h % 2][:, g * 512:(g + 1) * 512],
                                         ALU.mult).then_inc(oev.h, 1)
                (vector.nop() if DBG_NOPDUM else vector.tensor_copy(dum2[0:1, 0:1], one[0:1, 0:1])).then_inc(oev.h, 1)

        @block.sync
        def _(sync):
            for h in range(NH):
                sync.wait_ge(rcq.h, h + 1)
                sync.wait_ge(rcp.h, h + 1)
                sync.dma_start(out=scr[h % 4], in_=rcpr[:]).then_inc(rw.h, 16)
                sync.wait_ge(rw.h, 16 * (h + 1))
                if h >= 2:
                    sync.wait_ge(oev.h, 5 * (h - 1))
                sync.dma_start(out=recipb[h % 2][:], in_=scr[h % 4].partition_broadcast(128)
                               ).then_inc(rb.h, 16)
                sync.wait_ge(oev.h, 5 * (h + 1))
                sync.dma_start(out=Oo.rearrange("(m p) t -> p m t", p=128)[:, h],
                               in_=ost[h % 2][:]).then_inc(sto[h % 2].h, 16)
            sync.wait_ge(sto[0].h, 16 * ((NH + 1) // 2))
            sync.wait_ge(sto[1].h, 16 * (NH // 2))

    return nc




DBG_NSL = 4


def build_l0_tail():
    nc = bass.Bass("TRN2", target_bir_lowering=False, debug=False)
    O = nc.dram_tensor("O", [4096, 2048], BF16, kind="ExternalInput").ap()
    WOB = nc.dram_tensor("WOB", [16, 32, 128, 128], F32, kind="ExternalInput").ap()
    UT = nc.dram_tensor("UT", [D, 2048], F32, kind="ExternalInput").ap()
    GWB = nc.dram_tensor("GWB", [44, 16, 128, 128], F32, kind="ExternalInput").ap()
    UWB = nc.dram_tensor("UWB", [44, 16, 128, 128], F32, kind="ExternalInput").ap()
    DWB = nc.dram_tensor("DWB", [16, 44, 128, 128], F32, kind="ExternalInput").ap()
    GS = nc.dram_tensor("GS", [128, 44], F32, kind="ExternalInput").ap()
    US = nc.dram_tensor("US", [128, 44], F32, kind="ExternalInput").ap()
    DS = nc.dram_tensor("DS", [128, 16], F32, kind="ExternalInput").ap()
    ONE = nc.dram_tensor("ONE", [128, 1], BF16, kind="ExternalInput").ap()
    X2 = nc.dram_tensor("X2", [D, 2048], F32, kind="ExternalOutput").ap()
    X1F = nc.dram_tensor("X1F", [D, 2048], F32, kind="Internal").ap()
    scr = nc.dram_tensor("scr", [8, 1024], F32, kind="Internal").ap()

    NSL = DBG_NSL

    onb = nc.alloc_sbuf_tensor("onb", [128, 32 * 512], BF16)
    wob = [nc.alloc_sbuf_tensor("wob0", [128, 32 * 128], BF16)] * 2
    y1raw = nc.alloc_sbuf_tensor("y1raw", [128, 16 * 512], F32)
    x1bn = nc.alloc_sbuf_tensor("x1bn", [128, 16 * 512], BF16)
    gwb = [nc.alloc_sbuf_tensor(f"gwb{i}", [128, 16 * 128], BF16) for i in range(2)]
    uwb = [nc.alloc_sbuf_tensor(f"uwb{i}", [128, 16 * 128], BF16) for i in range(2)]
    dwb = [nc.alloc_sbuf_tensor("dwb0", [128, 44 * 128], BF16)] * 2
    hdn = nc.alloc_sbuf_tensor("hdn", [128, 44 * 512], BF16)
    sumb = x1bn
    sqb = hdn
    gm = nc.alloc_sbuf_tensor("gm", [128, 512], BF16)
    um = nc.alloc_sbuf_tensor("um", [128, 512], BF16)
    x1fn = nc.alloc_sbuf_tensor("x1fn", [128, 16 * 512], F32)
    utn = x1fn
    y2raw = y1raw
    gs = nc.alloc_sbuf_tensor("gs", [128, 44], F32)
    us = nc.alloc_sbuf_tensor("us", [128, 44], F32)
    ds = nc.alloc_sbuf_tensor("ds", [128, 16], F32)
    one = nc.alloc_sbuf_tensor("one", [128, 1], BF16)
    mb = nc.alloc_sbuf_tensor("mb", [128, 1024], F32)
    srow = x1fn[0:1, 0:1024]
    tmps = onb[:, 0:1024].bitcast(F32)
    tmp5 = mb[0:1, 0:512]
    tmp6 = mb[0:1, 512:1024]
    warm = nc.alloc_sbuf_tensor("warm", [1, 32], F32)
    dum2 = nc.alloc_sbuf_tensor("dum2", [1, 8], F32)
    psA = nc.alloc_psum_tensor("psA", [128, 512], F32)
    psB = nc.alloc_psum_tensor("psB", [128, 512], F32)
    psM = nc.alloc_psum_tensor("psM", [1, 512], F32)
    psQ = nc.alloc_psum_tensor("psQ", [1, 512], F32)

    lres = Sem(nc, "lres")
    lon = Sem(nc, "lon"); lut = Sem(nc, "lut"); lx1 = Sem(nc, "lx1")
    lwm = [Sem(nc, "lwm0"), Sem(nc, "lwm1")]
    lgw = [Sem(nc, "lgw0"), Sem(nc, "lgw1")]
    ldw = [Sem(nc, "ldw0"), Sem(nc, "ldw1")]
    mmA = Sem(nc, "mmA"); mmB = Sem(nc, "mmB"); mmC = Sem(nc, "mmC"); mmS = Sem(nc, "mmS")
    evA = Sem(nc, "evA"); evAd = Sem(nc, "evAd")
    seB = Sem(nc, "seB"); evB = Sem(nc, "evB"); evBd = Sem(nc, "evBd")
    evC = Sem(nc, "evC"); evCd = Sem(nc, "evCd")
    cst = Sem(nc, "cst"); cstd = Sem(nc, "cstd"); vch = Sem(nc, "vch")
    sqq = Sem(nc, "sqq"); wse = Sem(nc, "wse")
    stq = Sem(nc, "stq"); rwq = Sem(nc, "rwq"); rbq = Sem(nc, "rbq")
    apq = Sem(nc, "apq"); apqd = Sem(nc, "apqd")
    stX1 = Sem(nc, "stX1"); stX2 = Sem(nc, "stX2")

    with nc.Block() as block:
        @block.gpsimd
        def _(gpsimd):
            gpsimd.dma_start(out=gs[:], in_=GS).then_inc(lres.h, 16)
            gpsimd.dma_start(out=us[:], in_=US).then_inc(lres.h, 16)
            gpsimd.dma_start(out=ds[:], in_=DS).then_inc(lres.h, 16)
            gpsimd.dma_start(out=one[:], in_=ONE).then_inc(lres.h, 16)
            for n in range(NSL):
                if n >= 1:
                    gpsimd.wait_ge(stX2.h, 16 * n)
                gpsimd.dma_start(out=onb[:].rearrange("p (k t) -> p k t", k=32),
                                 in_=O.rearrange("(k p) t -> p k t", p=128)[:, :, n * 512:(n + 1) * 512]
                                 ).then_inc(lon.h, 16)
                gpsimd.dma_start(out=utn[:].rearrange("p (k t) -> p k t", k=16),
                                 in_=UT.rearrange("(k p) t -> p k t", p=128)[:, :, n * 512:(n + 1) * 512]
                                 ).then_inc(lut.h, 16)
                for m in range(16):
                    if n * 16 + m >= 1:
                        gpsimd.wait_ge(mmA.h, 16 * n + m)
                    gpsimd.dma_start(out=wob[0][:].rearrange("p (k f) -> p k f", k=32),
                                     in_=WOB[m].rearrange("k p f -> p k f")
                                     ).then_inc(lwm[0].h, 16)
                for m in range(44):
                    if n * 44 + m >= 2:
                        gpsimd.wait_ge(mmB.h, 2 * (44 * n + m - 1))
                    gpsimd.dma_start(out=gwb[m % 2][:].rearrange("p (k f) -> p k f", k=16),
                                     in_=GWB[m].rearrange("k p f -> p k f")
                                     ).then_inc(lgw[m % 2].h, 16)
                    gpsimd.dma_start(out=uwb[m % 2][:].rearrange("p (k f) -> p k f", k=16),
                                     in_=UWB[m].rearrange("k p f -> p k f")
                                     ).then_inc(lgw[m % 2].h, 16)
                gpsimd.wait_ge(stX1.h, 16 * (n + 1))
                gpsimd.dma_start(out=x1fn[:].rearrange("p (k t) -> p k t", k=16),
                                 in_=X1F.rearrange("(k p) t -> p k t", p=128)[:, :, n * 512:(n + 1) * 512]
                                 ).then_inc(lx1.h, 16)
                for m in range(16):
                    if n * 16 + m >= 1:
                        gpsimd.wait_ge(mmC.h, 16 * n + m)
                    gpsimd.dma_start(out=dwb[0][:].rearrange("p (k f) -> p k f", k=44),
                                     in_=DWB[m].rearrange("k p f -> p k f")
                                     ).then_inc(ldw[0].h, 16)

        @block.tensor
        def _(tensor):
            tensor.wait_ge(lres.h, 64)
            for n in range(NSL):
                tensor.wait_ge(lon.h, 16 * (n + 1))
                for m in range(16):
                    tensor.wait_ge(lwm[0].h, 16 * (16 * n + m + 1))
                    if m >= 2:
                        tensor.wait_ge(evA.h, 16 * n + m - 1)
                    inst = None
                    for k in range(32):
                        ps = psA if m % 2 == 0 else psB
                        inst = tensor.matmul(ps[:], wob[m % 2][:, k * 128:(k + 1) * 128],
                                             onb[:, k * 512:(k + 1) * 512],
                                             start=(k == 0), stop=(k == 31))
                    inst.then_inc(mmA.h, 1)
                ln = 2 * n
                tensor.wait_ge(cst.h, ln + 1)
                tensor.wait_ge(cstd.h, ln + 1)
                tensor.wait_ge(sqq.h, ln + 1)
                tensor.wait_ge(wse.h, 3 + 2 * ln + 1)
                inst = None
                for k in range(16):
                    inst = tensor.matmul(psM[:], one[:], sumb[:, k * 512:(k + 1) * 512],
                                         start=(k == 0), stop=(k == 15))
                inst.then_inc(mmS.h, 1)
                inst = None
                for k in range(16):
                    inst = tensor.matmul(psQ[:], one[:], sqb[:, k * 512:(k + 1) * 512],
                                         start=(k == 0), stop=(k == 15))
                inst.then_inc(mmS.h, 1)
                tensor.wait_ge(apq.h, 16 * (ln + 1))
                tensor.wait_ge(apqd.h, ln + 1)
                for m in range(44):
                    tensor.wait_ge(lgw[m % 2].h, 32 * ((44 * n + m) // 2 + 1))
                    if m >= 1:
                        tensor.wait_ge(seB.h, 44 * n + m)
                    if m >= 1:
                        tensor.wait_ge(evB.h, 2 * (44 * n + m) - 1)
                    inst = None
                    for k in range(16):
                        inst = tensor.matmul(psA[:], gwb[m % 2][:, k * 128:(k + 1) * 128],
                                             x1bn[:, k * 512:(k + 1) * 512],
                                             start=(k == 0), stop=(k == 15))
                    inst.then_inc(mmB.h, 1)
                    inst = None
                    for k in range(16):
                        inst = tensor.matmul(psB[:], uwb[m % 2][:, k * 128:(k + 1) * 128],
                                             x1bn[:, k * 512:(k + 1) * 512],
                                             start=(k == 0), stop=(k == 15))
                    inst.then_inc(mmB.h, 1)
                tensor.wait_ge(evB.h, 88 * n + 88)
                tensor.wait_ge(evBd.h, n + 1)
                for m in range(16):
                    tensor.wait_ge(ldw[0].h, 16 * (16 * n + m + 1))
                    if m >= 2:
                        tensor.wait_ge(evC.h, 16 * n + m - 1)
                    inst = None
                    for k in range(44):
                        ps = psA if m % 2 == 0 else psB
                        inst = tensor.matmul(ps[:], dwb[m % 2][:, k * 128:(k + 1) * 128],
                                             hdn[:, k * 512:(k + 1) * 512],
                                             start=(k == 0), stop=(k == 43))
                    inst.then_inc(mmC.h, 1)
                ln = 2 * n + 1
                tensor.wait_ge(cst.h, ln + 1)
                tensor.wait_ge(cstd.h, ln + 1)
                tensor.wait_ge(sqq.h, ln + 1)
                tensor.wait_ge(wse.h, 3 + 2 * ln + 1)
                inst = None
                for k in range(16):
                    inst = tensor.matmul(psM[:], one[:], sumb[:, k * 512:(k + 1) * 512],
                                         start=(k == 0), stop=(k == 15))
                inst.then_inc(mmS.h, 1)
                inst = None
                for k in range(16):
                    inst = tensor.matmul(psQ[:], one[:], sqb[:, k * 512:(k + 1) * 512],
                                         start=(k == 0), stop=(k == 15))
                inst.then_inc(mmS.h, 1)

        @block.scalar
        def _(scalar):
            wn = [0]

            def wdummy():
                scalar.wait_ge(wse.h, wn[0])
                wn[0] += 1
                return scalar.activation(warm[:, 0:16], warm[:, 16:32], AF.Exp
                                         ).then_inc(wse.h, 1)

            scalar.memzero(warm[:]).then_inc(wse.h, 1)
            wn[0] = 1
            wdummy()
            wdummy()
            scalar.wait_ge(lres.h, 64)
            for n in range(NSL):
                for ph in range(2):
                    ln = 2 * n + ph
                    if ph == 1:
                        for m in range(44):
                            scalar.wait_ge(mmB.h, 2 * (44 * n + m) + 1)
                            scalar.wait_ge(evB.h, 2 * (44 * n + m))
                            scalar.activation(gm[:], psA[:], AF.Silu, scale=gs[:, m:m + 1]
                                              ).then_inc(seB.h, 1)
                    src_ = y1raw if ph == 0 else y2raw
                    if ph == 0:
                        scalar.wait_ge(evA.h, 16 * (n + 1))
                        scalar.wait_ge(evAd.h, n + 1)
                    else:
                        scalar.wait_ge(evC.h, 16 * (n + 1))
                        scalar.wait_ge(evCd.h, n + 1)
                    scalar.activation(sqb[:, 0:8192], src_[:], AF.Square).then_inc(sqq.h, 1)
                    wdummy()
                    scalar.wait_ge(stq.h, 5 * ln + 2)
                    scalar.activation(tmp6, tmp5, AF.Sqrt).then_inc(stq.h, 1)
                    wdummy()

        @block.vector
        def _(vector):
            vn = [0]

            def vdummy():
                if DBG_NOPDUM:
                    return vector.nop()
                return vector.tensor_copy(dum2[0:1, 0:1], one[0:1, 0:1])

            vector.wait_ge(lres.h, 64)
            for n in range(NSL):
                for ph in range(2):
                    ln = 2 * n + ph
                    if ph == 0:
                        vector.wait_ge(lut.h, 16 * (n + 1))
                        for m in range(16):
                            vector.wait_ge(mmA.h, 16 * n + m + 1)
                            ps = psA if m % 2 == 0 else psB
                            vector.tensor_add(y1raw[:, m * 512:(m + 1) * 512], ps[:],
                                              utn[:, m * 512:(m + 1) * 512]).then_inc(evA.h, 1)
                        vdummy().then_inc(evAd.h, 1)
                        src_ = y1raw
                    else:
                        for m in range(44):
                            vector.wait_ge(mmB.h, 2 * (44 * n + m) + 2)
                            vector.wait_ge(evB.h, 2 * (44 * n + m))
                            vector.tensor_scalar(um[:], psB[:], us[:, m:m + 1], None, ALU.mult
                                                 ).then_inc(evB.h, 1)
                            vector.wait_ge(seB.h, 44 * n + m + 1)
                            vector.wait_ge(evB.h, 2 * (44 * n + m) + 1)
                            vector.tensor_tensor(hdn[:, m * 512:(m + 1) * 512], gm[:], um[:],
                                                 ALU.mult).then_inc(evB.h, 1)
                        vdummy().then_inc(evBd.h, 1)
                        vector.wait_ge(lx1.h, 16 * (n + 1))
                        for m in range(16):
                            vector.wait_ge(mmC.h, 16 * n + m + 1)
                            ps = psA if m % 2 == 0 else psB
                            vector.scalar_tensor_tensor(y2raw[:, m * 512:(m + 1) * 512], ps[:],
                                                        ds[:, m:m + 1], x1fn[:, m * 512:(m + 1) * 512],
                                                        ALU.mult, ALU.add).then_inc(evC.h, 1)
                        vdummy().then_inc(evCd.h, 1)
                        src_ = y2raw
                    if ph == 0:
                        vector.wait_ge(evA.h, 16 * (n + 1))
                    else:
                        vector.wait_ge(evC.h, 16 * (n + 1))
                    vector.tensor_copy(sumb[:], src_[:]).then_inc(cst.h, 1)
                    vdummy().then_inc(cstd.h, 1)
                    vector.wait_ge(mmS.h, 2 * ln + 2)
                    vector.tensor_scalar(srow[:, 0:512], psM[:], 1.0 / 2048.0, None, ALU.mult
                                         ).then_inc(vch.h, 1); vn[0] += 1
                    s1 = vn[0]
                    vector.tensor_scalar(tmp5, psQ[:], 1.0 / 2048.0, None, ALU.mult
                                         ).then_inc(vch.h, 1); vn[0] += 1
                    vector.wait_ge(vch.h, s1)
                    vector.tensor_tensor(tmp6, srow[:, 0:512], srow[:, 0:512], ALU.mult
                                         ).then_inc(vch.h, 1); vn[0] += 1
                    vector.wait_ge(vch.h, vn[0])
                    vector.tensor_sub(tmp5, tmp5, tmp6).then_inc(vch.h, 1); vn[0] += 1
                    vector.wait_ge(vch.h, vn[0])
                    vector.tensor_scalar(tmp5, tmp5, 1e-5, None, ALU.add).then_inc(stq.h, 1)
                    vdummy().then_inc(stq.h, 1)
                    vector.wait_ge(stq.h, 5 * ln + 3)
                    vector.wait_ge(wse.h, 3 + 2 * ln + 2)
                    vector.reciprocal(srow[:, 512:1024], tmp6).then_inc(stq.h, 1)
                    vdummy().then_inc(stq.h, 1)
                    vector.wait_ge(rbq.h, 16 * (ln + 1))
                    for m in range(16):
                        vector.wait_ge(vch.h, vn[0])
                        if ph == 1 and m >= 1:
                            vector.wait_ge(apq.h, 16 * ln + m)
                        vector.tensor_sub(tmps, src_[:, m * 512:(m + 1) * 512], mb[:, 0:512]
                                          ).then_inc(vch.h, 1); vn[0] += 1
                        vector.wait_ge(vch.h, vn[0])
                        if ph == 0:
                            vector.tensor_mul(y1raw[:, m * 512:(m + 1) * 512], tmps, mb[:, 512:1024]
                                              ).then_inc(vch.h, 1); vn[0] += 1
                            vector.wait_ge(vch.h, vn[0])
                            vector.tensor_copy(x1bn[:, m * 512:(m + 1) * 512],
                                               y1raw[:, m * 512:(m + 1) * 512]).then_inc(apq.h, 1)
                        else:
                            vector.tensor_mul(y2raw[:, m * 512:(m + 1) * 512], tmps,
                                              mb[:, 512:1024]).then_inc(apq.h, 1)
                    vdummy().then_inc(apqd.h, 1)

        @block.sync
        def _(sync):
            for n in range(NSL):
                for ph in range(2):
                    ln = 2 * n + ph
                    sync.wait_ge(stq.h, 5 * (ln + 1))
                    sync.dma_start(out=scr[ln % 8], in_=srow[:]).then_inc(rwq.h, 16)
                    sync.wait_ge(rwq.h, 16 * (ln + 1))
                    sync.dma_start(out=mb[:], in_=scr[ln % 8].partition_broadcast(128)
                                   ).then_inc(rbq.h, 16)
                    sync.wait_ge(apq.h, 16 * (ln + 1))
                    sync.wait_ge(apqd.h, ln + 1)
                    if ph == 0:
                        sync.dma_start(
                            out=X1F.rearrange("(k p) t -> p k t", p=128)[:, :, n * 512:(n + 1) * 512],
                            in_=y1raw[:].rearrange("p (k t) -> p k t", k=16)).then_inc(stX1.h, 16)
                    else:
                        sync.dma_start(
                            out=X2.rearrange("(k p) t -> p k t", p=128)[:, :, n * 512:(n + 1) * 512],
                            in_=y2raw[:].rearrange("p (k t) -> p k t", k=16)).then_inc(stX2.h, 16)
            sync.wait_ge(stX2.h, 16 * NSL)

    return nc


def host_masks(role):
    r = np.arange(128)[:, None]
    c = np.arange(512)[None, :]
    M = [(c >= 128 * v + r).astype(np.float32) for v in range(4)]
    ones = np.ones((128, 512), dtype=np.float32)
    zeros = np.zeros((128, 512), dtype=np.float32)
    larger = [ones] * 4 + M
    smaller = M + [zeros] * 4
    out = []
    for par in (0, 1):
        q = QUADS[role][par]
        out += larger if q == 2 * par + 1 else smaller
    return np.concatenate(out, axis=1).astype(ml_dtypes.bfloat16)


def run_l0_proj(inputs):
    u = inputs["L0_attn_u"]
    W_in = inputs["L0_attn_W_in"]
    WqT = np.ascontiguousarray(W_in[6144:10240].T)
    WkT = np.ascontiguousarray(W_in[5120:6144].T)
    WvT = np.ascontiguousarray(W_in[4096:5120].T)
    cos, sin = rope_host_tables()
    s = np.float32(1.0 / np.sqrt(128.0))
    PIT = np.ascontiguousarray(pi_matrix().T)
    in_maps = []
    for c in range(8):
        b, r = divmod(c, 2)
        st = slot_tokens(r)
        uT = np.ascontiguousarray(u[b][st].T)
        ROPE = np.concatenate([
            np.ascontiguousarray((cos[st] * s).T),
            np.ascontiguousarray((sin[st] * s).T),
            np.ascontiguousarray(cos[st].T),
            np.ascontiguousarray(sin[st].T)], axis=1).astype(np.float32)
        in_maps.append({"uT": uT, "WqT": WqT, "WkT": WkT, "WvT": WvT,
                        "ROPE": np.ascontiguousarray(ROPE), "PIT": PIT})
    nc = build_l0_proj()
    res = run_bass_kernel_spmd(nc, in_maps, core_ids=list(range(8)))
    return res.results


def run_l0_attn(inputs, proj):
    in_maps = []
    ONE = np.ones((128, 1), dtype=ml_dtypes.bfloat16)
    for c in range(8):
        b, r = divmod(c, 2)
        p0, p1 = 2 * b, 2 * b + 1
        Knat = np.zeros((1024, 4096), dtype=ml_dtypes.bfloat16)
        Vnat = np.zeros((1024, 4096), dtype=ml_dtypes.bfloat16)
        Knat[:, slot_tokens(0)] = proj[p0]["Ko"]
        Knat[:, slot_tokens(1)] = proj[p1]["Ko"]
        Vnat[:, slot_tokens(0)] = proj[p0]["Vo"]
        Vnat[:, slot_tokens(1)] = proj[p1]["Vo"]
        Vt = np.ascontiguousarray(
            Vnat.reshape(8, 128, 32, 128).transpose(3, 0, 2, 1).reshape(128, 8 * 32 * 128))
        in_maps.append({"Q": proj[c]["Qo"], "K": np.ascontiguousarray(Knat),
                        "Vt": Vt, "MSK": host_masks(r), "ONE": ONE})
    nc = build_l0_attn()
    res = run_bass_kernel_spmd(nc, in_maps, core_ids=list(range(8)))
    return res.results




def run_l0_tail(inputs, att):
    W_out = inputs["L0_attn_W_out"]
    WoutT = np.ascontiguousarray(W_out.T)
    WOB = np.ascontiguousarray(WoutT.reshape(32, 128, 16, 128).transpose(2, 0, 1, 3))
    gwT = inputs["L0_ffn_gate_w"].T
    uwT = inputs["L0_ffn_up_w"].T
    dwT = inputs["L0_ffn_down_w"].T
    GWB = np.ascontiguousarray(gwT.reshape(16, 128, 44, 128).transpose(2, 0, 1, 3))
    UWB = np.ascontiguousarray(uwT.reshape(16, 128, 44, 128).transpose(2, 0, 1, 3))
    DWB = np.ascontiguousarray(dwT.reshape(44, 128, 16, 128).transpose(2, 0, 1, 3))
    GS = np.ascontiguousarray(inputs["L0_ffn_gate_s"].reshape(44, 128).T)
    US = np.ascontiguousarray(inputs["L0_ffn_up_s"].reshape(44, 128).T)
    DS = np.ascontiguousarray(inputs["L0_ffn_down_s"].reshape(16, 128).T)
    ONE = np.ones((128, 1), dtype=ml_dtypes.bfloat16)
    u = inputs["L0_attn_u"]
    in_maps = []
    for c in range(8):
        b, r = divmod(c, 2)
        st = slot_tokens(r)
        uT = np.ascontiguousarray(u[b][st].T)
        in_maps.append({"O": att[c]["Oo"], "WOB": WOB, "UT": uT, "GWB": GWB,
                        "UWB": UWB, "DWB": DWB, "GS": GS, "US": US, "DS": DS,
                        "ONE": ONE})
    nc = build_l0_tail()
    res = run_bass_kernel_spmd(nc, in_maps, core_ids=list(range(8)))
    return res.results





def build_quant():
    nc = bass.Bass("TRN2", target_bir_lowering=False, debug=False)
    WGU = nc.dram_tensor("WGU", [2048, 2048], F32, kind="ExternalInput").ap()
    WP = nc.dram_tensor("WP", [256, 8192], F32, kind="ExternalInput").ap()
    QGU = nc.dram_tensor("QGU", [2048, 2048], BF16, kind="ExternalOutput").ap()
    QP = nc.dram_tensor("QP", [256, 8192], BF16, kind="ExternalOutput").ap()

    FMAX = 8192
    wb = nc.alloc_sbuf_tensor("wb", [128, FMAX], BF16)
    aw = nc.alloc_sbuf_tensor("aw", [128, FMAX], F32)
    qv = nc.alloc_sbuf_tensor("qv", [128, FMAX], F32)
    tb = nc.alloc_sbuf_tensor("tb", [128, FMAX], F32)
    rv = nc.alloc_sbuf_tensor("rv", [128, FMAX], BF16)
    whb = nc.alloc_sbuf_tensor("whb", [128, FMAX], BF16)
    outb = nc.alloc_sbuf_tensor("outb", [128, FMAX], BF16)
    gsum = nc.alloc_sbuf_tensor("gsum", [128, FMAX // 64], F32)
    scb = nc.alloc_sbuf_tensor("scb", [128, FMAX // 64], BF16)
    scf = nc.alloc_sbuf_tensor("scf", [128, FMAX // 64], F32)

    lq = Sem(nc, "lq"); qch = Sem(nc, "qch"); stq = Sem(nc, "stq")

    tiles = [("WGU", i, 2048) for i in range(16)] + [("WP", i, 8192) for i in range(2)]

    with nc.Block() as block:
        @block.gpsimd
        def _(gpsimd):
            for ti, (src, i, F) in enumerate(tiles):
                if ti >= 1:
                    gpsimd.wait_ge(stq.h, 16 * ti)
                d = WGU if src == "WGU" else WP
                gpsimd.dma_start(out=wb[:, 0:F], in_=d[i * 128:(i + 1) * 128, :]
                                 ).then_inc(lq.h, 16)

        @block.vector
        def _(vector):
            qn = [0]

            def op(inst):
                inst.then_inc(qch.h, 1)
                qn[0] += 1

            for ti, (src, i, F) in enumerate(tiles):
                G = F // 64
                vector.wait_ge(lq.h, 16 * (ti + 1))
                op(vector.tensor_scalar(tb[:, 0:F], wb[:, 0:F], -1.0, None, ALU.mult))
                vector.wait_ge(qch.h, qn[0])
                op(vector.tensor_tensor(aw[:, 0:F], wb[:, 0:F], tb[:, 0:F], ALU.max))
                vector.wait_ge(qch.h, qn[0])
                op(vector.reduce_sum(gsum[:, 0:G].unsqueeze(2),
                                     aw[:, 0:F].rearrange("p (g e) -> p g e", e=64),
                                     axis=AX.X))
                vector.wait_ge(qch.h, qn[0])
                op(vector.tensor_scalar(scf[:, 0:G], gsum[:, 0:G], 1.0 / 64.0, None, ALU.mult))
                vector.wait_ge(qch.h, qn[0])
                op(vector.tensor_scalar(scf[:, 0:G], scf[:, 0:G], 1e-8, None, ALU.max))
                vector.wait_ge(qch.h, qn[0])
                op(vector.tensor_copy(scb[:, 0:G], scf[:, 0:G]))
                vector.wait_ge(qch.h, qn[0])
                op(vector.tensor_copy(scf[:, 0:G], scb[:, 0:G]))
                vector.wait_ge(qch.h, qn[0])
                op(vector.reciprocal(gsum[:, 0:G], scf[:, 0:G]))
                vector.wait_ge(qch.h, qn[0])
                op(vector.tensor_tensor(rv[:, 0:F].rearrange("p (g e) -> p g e", e=64),
                                        wb[:, 0:F].rearrange("p (g e) -> p g e", e=64),
                                        gsum[:, 0:G].unsqueeze(2).to_broadcast([128, G, 64]),
                                        ALU.mult))
                vector.wait_ge(qch.h, qn[0])
                op(vector.tensor_scalar(qv[:, 0:F], rv[:, 0:F], 0.5, None, ALU.is_gt))
                vector.wait_ge(qch.h, qn[0])
                op(vector.tensor_scalar(tb[:, 0:F], rv[:, 0:F], -0.5, None, ALU.is_lt))
                vector.wait_ge(qch.h, qn[0])
                op(vector.tensor_sub(qv[:, 0:F], qv[:, 0:F], tb[:, 0:F]))
                vector.wait_ge(qch.h, qn[0])
                op(vector.tensor_tensor(whb[:, 0:F].rearrange("p (g e) -> p g e", e=64),
                                        qv[:, 0:F].rearrange("p (g e) -> p g e", e=64),
                                        scf[:, 0:G].unsqueeze(2).to_broadcast([128, G, 64]),
                                        ALU.mult))
                vector.wait_ge(qch.h, qn[0])
                op(vector.tensor_sub(whb[:, 0:F], whb[:, 0:F], wb[:, 0:F]))
                vector.wait_ge(qch.h, qn[0])
                op(vector.tensor_add(outb[:, 0:F], wb[:, 0:F], whb[:, 0:F]))
                vector.wait_ge(qch.h, qn[0])
                op(vector.tensor_copy(wb[:, 0:F], wb[:, 0:F]))

        @block.sync
        def _(sync):
            nops = 16
            for ti, (src, i, F) in enumerate(tiles):
                sync.wait_ge(qch.h, nops * (ti + 1))
                d = QGU if src == "WGU" else QP
                sync.dma_start(out=d[i * 128:(i + 1) * 128, :], in_=outb[:, 0:F]
                               ).then_inc(stq.h, 16)
            sync.wait_ge(stq.h, 16 * len(tiles))

    return nc


def run_quant(inputs):
    wgu = inputs["L1_ffn_w_gate_up"]
    wp = inputs["L1_ffn_w_proj"]
    in_maps = []
    for c in range(8):
        in_maps.append({"WGU": np.ascontiguousarray(wgu[c * 2048:(c + 1) * 2048]),
                        "WP": np.ascontiguousarray(wp[c * 256:(c + 1) * 256])})
    nc = build_quant()
    res = run_bass_kernel_spmd(nc, in_maps, core_ids=list(range(8)))
    qgu = np.concatenate([res.results[c]["QGU"] for c in range(8)], axis=0)
    qp = np.concatenate([res.results[c]["QP"] for c in range(8)], axis=0)
    return qgu, qp




def build_l1_qkv():
    rec = {}
    for pas in range(2):
        nc = bass.Bass("TRN2", target_bir_lowering=False, debug=False)
        X2B = nc.dram_tensor("X2B", [D, 4096], F32, kind="ExternalInput").ap()
        WQ3 = nc.dram_tensor("WQ3", [D, 3 * 1024], F32, kind="ExternalInput").ap()
        WB8 = nc.dram_tensor("WB8", [D, 8], F32, kind="ExternalInput").ap()
        CONV = nc.dram_tensor("CONV", [128, 12], F32, kind="ExternalInput").ap()
        ONE = nc.dram_tensor("ONE", [128, 1], BF16, kind="ExternalInput").ap()
        QN = nc.dram_tensor("QN", [1024, 4096], BF16, kind="ExternalOutput").ap()
        KN = nc.dram_tensor("KN", [1024, 4096], BF16, kind="ExternalOutput").ap()
        KB = nc.dram_tensor("KB", [1024, 4096], BF16, kind="ExternalOutput").ap()
        VB = nc.dram_tensor("VB", [1024, 4096], BF16, kind="ExternalOutput").ap()
        rscr = nc.dram_tensor("rscr", [2, 4096], F32, kind="Internal").ap()
        bscr = nc.dram_tensor("bscr", [8, 4096], F32, kind="Internal").ap()

        xn = [nc.alloc_sbuf_tensor(f"xn{i}", [128, 16 * 512], BF16) for i in range(2)]
        w3 = nc.alloc_sbuf_tensor("w3", [128, 16 * 128], BF16)
        wb8 = nc.alloc_sbuf_tensor("wb8", [128, 16 * 8], BF16)
        cvw = nc.alloc_sbuf_tensor("cvw", [128, 12], F32)
        one = nc.alloc_sbuf_tensor("one", [128, 1], BF16)
        cbuf = nc.alloc_sbuf_tensor("cbuf", [128, 515], F32)
        c1 = nc.alloc_sbuf_tensor("c1", [128, 512], F32)
        c2 = nc.alloc_sbuf_tensor("c2", [128, 512], F32)
        tail = nc.alloc_sbuf_tensor("tail", [128, 3], F32)
        sraw = nc.alloc_sbuf_tensor("sraw", [128, 4096], F32)
        sqb = nc.alloc_sbuf_tensor("sqb", [128, 512], BF16)
        beta = nc.alloc_sbuf_tensor("beta", [8, 4096], F32)
        rsr = nc.alloc_sbuf_tensor("rsr", [1, 4096], F32)
        rs2 = nc.alloc_sbuf_tensor("rs2", [1, 4096], F32)
        brow = nc.alloc_sbuf_tensor("brow", [1, 4096], F32)
        bc1 = nc.alloc_sbuf_tensor("bc1", [128, 4096], F32)
        bc2 = nc.alloc_sbuf_tensor("bc2", [128, 4096], F32)
        obuf = nc.alloc_sbuf_tensor("obuf", [128, 2 * 4096], BF16)
        warm = nc.alloc_sbuf_tensor("warm", [1, 32], F32)
        dum2 = nc.alloc_sbuf_tensor("dum2", [1, 8], F32)
        psA = nc.alloc_psum_tensor("psA", [128, 512], F32)
        psB = nc.alloc_psum_tensor("psB", [128, 512], F32)
        psM = nc.alloc_psum_tensor("psM", [1, 512], F32)

        lx = [Sem(nc, "lx0"), Sem(nc, "lx1")]
        lw = Sem(nc, "lw"); lcst = Sem(nc, "lcst")
        mmP = Sem(nc, "mmP"); mmS = Sem(nc, "mmS")
        vch = Sem(nc, "vch"); cnvd = Sem(nc, "cnvd")
        sil = Sem(nc, "sil"); sqe = Sem(nc, "sqe"); sqr = Sem(nc, "sqr")
        wse = Sem(nc, "wse"); sse = Sem(nc, "sse")
        rwq = Sem(nc, "rwq"); rbq = Sem(nc, "rbq"); stq = Sem(nc, "stq")
        bst = Sem(nc, "bst"); bld = Sem(nc, "bld")

        NM = 24
        mkind = lambda m: ("q" if m < 8 else ("k" if m < 16 else "v"))
        R = (lambda k, d=0: rec.get(k, d))

        with nc.Block() as block:
            @block.vector
            def _(vector):
                vn = [0]

                def op(inst):
                    inst.then_inc(vch.h, 1)
                    vn[0] += 1
                    return vn[0]

                cnt_cnvd = [0]

                def vdummy():
                    vector.tensor_copy(dum2[0:1, 0:1], one[0:1, 0:1]).then_inc(cnvd.h, 1)
                    cnt_cnvd[0] += 1
                    return cnt_cnvd[0]

                vector.wait_ge(lcst.h, 48)
                nsq = 0
                for m in range(NM):
                    knd = mkind(m)
                    for n in range(8):
                        if n == 0:
                            vector.wait_ge(vch.h, vn[0])
                            op(vector.memset(tail[:], 0.0))
                        vector.wait_ge(vch.h, vn[0])
                        op(vector.tensor_copy(cbuf[:, 0:3], tail[:]))
                        vector.wait_ge(mmP.h, 16 * (8 + m * 8 + n + 1))
                        vector.wait_ge(vch.h, vn[0])
                        ps = psA if n % 2 == 0 else psB
                        rec[("evv", m, n)] = op(vector.tensor_copy(cbuf[:, 3:515], ps[:]))
                        vector.wait_ge(vch.h, vn[0])
                        op(vector.tensor_copy(tail[:], cbuf[:, 512:515]))
                        cw = 4 * (0 if knd == "q" else (1 if knd == "k" else 2))
                        vector.wait_ge(vch.h, vn[0])
                        op(vector.tensor_scalar(c1[:], cbuf[:, 3:515], cvw[:, cw + 3:cw + 4], None, ALU.mult))
                        vector.wait_ge(vch.h, vn[0])
                        op(vector.scalar_tensor_tensor(c2[:], cbuf[:, 2:514], cvw[:, cw + 2:cw + 3], c1[:], ALU.mult, ALU.add))
                        vector.wait_ge(vch.h, vn[0])
                        op(vector.scalar_tensor_tensor(c1[:], cbuf[:, 1:513], cvw[:, cw + 1:cw + 2], c2[:], ALU.mult, ALU.add))
                        vector.wait_ge(vch.h, vn[0])
                        rec[("cnv", m, n)] = op(vector.scalar_tensor_tensor(c2[:], cbuf[:, 0:512], cvw[:, cw + 0:cw + 1], c1[:], ALU.mult, ALU.add))
                        rec[("cnvd", m, n)] = vdummy()
                        if knd != "v":
                            nsq += 1
                            vector.wait_ge(mmS.h, nsq)
                            rec[("psmv", m, n)] = op(vector.tensor_scalar(
                                rsr[:, n * 512:(n + 1) * 512], psM[:], 1e-6, None, ALU.add))
                    if knd != "v":
                        rec[("rmv", m)] = vn[0]
                        rec[("rmvd", m)] = vdummy()
                        vector.wait_ge(sqr.h, R(("sqr", m), 0))
                        vector.wait_ge(wse.h, R(("wsesqr", m), 0))
                        op(vector.reciprocal(rsr[:], rs2[:]))
                        vector.wait_ge(vch.h, vn[0])
                        if knd == "q":
                            op(vector.tensor_scalar(rs2[:], rsr[:], float(128 ** -0.5), None, ALU.mult))
                        else:
                            vector.wait_ge(bld.h, 16 * (m - 8 + 1))
                            op(vector.tensor_tensor(rs2[:], rsr[:], brow[:], ALU.mult))
                    rec[("rowv", m)] = vn[0]
                    rec[("rowd", m)] = vdummy()
                    vector.wait_ge(rbq.h, 16 * R(("rbc", m), 0))
                    vector.wait_ge(sil.h, 8 * (m + 1))
                    vector.wait_ge(wse.h, R(("wsem", m), 0))
                    if m >= 2:
                        vector.wait_ge(stq.h, 16 * R(("stc", m - 2), 0))
                    vector.wait_ge(vch.h, vn[0])
                    op(vector.tensor_tensor(obuf[:, 0:4096], sraw[:], bc1[:], ALU.mult))
                    if knd == "k":
                        vector.wait_ge(vch.h, vn[0])
                        op(vector.tensor_tensor(obuf[:, 4096:8192], sraw[:], bc2[:], ALU.mult))
                    rec[("stv", m)] = vn[0]
                    rec[("std", m)] = vdummy()

            @block.scalar
            def _(scalar):
                wn = [0]

                def wdummy():
                    scalar.wait_ge(wse.h, wn[0])
                    wn[0] += 1
                    return scalar.activation(warm[:, 0:16], warm[:, 16:32], AF.Exp
                                             ).then_inc(wse.h, 1)

                scalar.memzero(warm[:]).then_inc(wse.h, 1)
                wn[0] = 1
                scalar.wait_ge(lcst.h, 48)
                nsq = 0
                nsqr = 0
                for n in range(8):
                    scalar.wait_ge(mmP.h, 16 * (n + 1))
                    scalar.activation(beta[:, n * 512:(n + 1) * 512], psA[0:8, :], AF.Sigmoid
                                      ).then_inc(sse.h, 1)
                wdummy()
                rec["wsebeta"] = wn[0]
                for m in range(NM):
                    knd = mkind(m)
                    for n in range(8):
                        scalar.wait_ge(vch.h, R(("cnv", m, n), 0))
                        scalar.wait_ge(cnvd.h, R(("cnvd", m, n), 0))
                        scalar.activation(sraw[:, n * 512:(n + 1) * 512], c2[:], AF.Silu
                                          ).then_inc(sil.h, 1)
                        if knd != "v":
                            nsq += 1
                            scalar.activation(sqb[:], sraw[:, n * 512:(n + 1) * 512], AF.Square
                                              ).then_inc(sqe.h, 1)
                        wdummy()
                    rec[("wsem", m)] = wn[0]
                    if knd != "v":
                        scalar.wait_ge(vch.h, R(("rmv", m), 0))
                        scalar.wait_ge(cnvd.h, R(("rmvd", m), 0))
                        nsqr += 1
                        scalar.activation(rs2[:], rsr[:], AF.Sqrt).then_inc(sqr.h, 1)
                        rec[("sqr", m)] = nsqr
                        wdummy()
                        rec[("wsesqr", m)] = wn[0]

            @block.tensor
            def _(tensor):
                tensor.wait_ge(lcst.h, 48)
                for n in range(8):
                    tensor.wait_ge(lx[n % 2].h, 16 * (n // 2 + 1))
                    if n >= 1:
                        tensor.wait_ge(sse.h, n)
                    inst = None
                    for k in range(16):
                        inst = tensor.matmul(psA[0:8, :], wb8[:, k * 8:(k + 1) * 8],
                                             xn[n % 2][:, k * 512:(k + 1) * 512],
                                             start=(k == 0), stop=(k == 15))
                    inst.then_inc(mmP.h, 16)
                nsq = 0
                gseq = []
                for m in range(NM):
                    for n in range(8):
                        gseq.append((m, n))
                for m in range(NM):
                    knd = mkind(m)
                    tensor.wait_ge(lw.h, 16 * (m + 1))
                    for n in range(8):
                        gi = 8 + m * 8 + n
                        tensor.wait_ge(lx[gi % 2].h, 16 * (gi // 2 + 1))
                        idx = m * 8 + n
                        if idx >= 2:
                            pm, pn = gseq[idx - 2]
                            tensor.wait_ge(vch.h, R(("evv", pm, pn), 0))
                        if idx < 2:
                            tensor.wait_ge(sse.h, 8)
                        ps = psA if n % 2 == 0 else psB
                        inst = None
                        for k in range(16):
                            inst = tensor.matmul(ps[:], w3[:, k * 128:(k + 1) * 128],
                                                 xn[gi % 2][:, k * 512:(k + 1) * 512],
                                                 start=(k == 0), stop=(k == 15))
                        inst.then_inc(mmP.h, 16)
                        if knd != "v":
                            nsq += 1
                            tensor.wait_ge(sqe.h, nsq)
                            tensor.wait_ge(wse.h, R("wsebeta", 0) + m * 8 + n + 1)
                            if nsq >= 2:
                                tensor.wait_ge(vch.h, R(("psmvprev", m, n), 0))
                            tensor.matmul(psM[:], one[:], sqb[:], start=True, stop=True
                                          ).then_inc(mmS.h, 1)

            @block.sync
            def _(sync):
                nrw = 0
                nrb = 0
                nst = 0
                for m in range(NM):
                    knd = mkind(m)
                    sync.wait_ge(vch.h, R(("rowv", m), 0))
                    sync.wait_ge(cnvd.h, R(("rowd", m), 0))
                    src1 = rs2 if knd == "q" else (rsr if knd == "k" else beta[m - 16:m - 15, :])
                    if knd == "v":
                        sync.dma_start(out=rscr[0], in_=src1).then_inc(rwq.h, 16)
                    else:
                        sync.dma_start(out=rscr[0], in_=(rs2[:] if knd == "q" else rsr[:])).then_inc(rwq.h, 16)
                    nrw += 1
                    if knd == "k":
                        sync.dma_start(out=rscr[1], in_=rs2[:]).then_inc(rwq.h, 16)
                        nrw += 1
                    sync.wait_ge(rwq.h, 16 * nrw)
                    if m >= 1:
                        sync.wait_ge(vch.h, R(("stv", m - 1), 0))
                        sync.wait_ge(cnvd.h, R(("std", m - 1), 0))
                    sync.dma_start(out=bc1[:], in_=rscr[0].partition_broadcast(128)).then_inc(rbq.h, 16)
                    nrb += 1
                    if knd == "k":
                        sync.dma_start(out=bc2[:], in_=rscr[1].partition_broadcast(128)).then_inc(rbq.h, 16)
                        nrb += 1
                    rec[("rbc", m)] = nrb
                    sync.wait_ge(vch.h, R(("stv", m), 0))
                    sync.wait_ge(cnvd.h, R(("std", m), 0))
                    dst1 = {"q": QN, "k": KN, "v": VB}[knd]
                    mm = m % 8
                    sync.dma_start(out=dst1.rearrange("(h p) t -> p h t", p=128)[:, mm],
                                   in_=obuf[:, 0:4096]).then_inc(stq.h, 16)
                    nst += 1
                    if knd == "k":
                        sync.dma_start(out=KB.rearrange("(h p) t -> p h t", p=128)[:, mm],
                                       in_=obuf[:, 4096:8192]).then_inc(stq.h, 16)
                        nst += 1
                    rec[("stc", m)] = nst
                sync.wait_ge(stq.h, 16 * nst)

            @block.gpsimd
            def _(gpsimd):
                gpsimd.dma_start(out=wb8[:].rearrange("p (k f) -> p k f", k=16),
                                 in_=WB8.rearrange("(k p) f -> p k f", p=128)).then_inc(lcst.h, 16)
                gpsimd.dma_start(out=cvw[:], in_=CONV).then_inc(lcst.h, 16)
                gpsimd.dma_start(out=one[:], in_=ONE).then_inc(lcst.h, 16)
                gi = 0
                for n in range(8):
                    if gi >= 2:
                        gpsimd.wait_ge(mmP.h, 16 * (gi - 1))
                    gpsimd.dma_start(out=xn[gi % 2][:].rearrange("p (k t) -> p k t", k=16),
                                     in_=X2B.rearrange("(k p) t -> p k t", p=128)[:, :, n * 512:(n + 1) * 512]
                                     ).then_inc(lx[gi % 2].h, 16)
                    gi += 1
                gpsimd.wait_ge(sse.h, 8)
                gpsimd.wait_ge(wse.h, 2)
                gpsimd.dma_start(out=bscr, in_=beta[:]).then_inc(bst.h, 16)
                for m in range(NM):
                    if m == 8:
                        gpsimd.wait_ge(bst.h, 16)
                    if 8 <= m < 16:
                        mk = m - 8
                        if mk >= 1:
                            gpsimd.wait_ge(vch.h, R(("rowv", 8 + mk - 1), 0))
                        gpsimd.dma_start(out=brow[:], in_=bscr[mk]).then_inc(bld.h, 16)
                    if m >= 1:
                        gpsimd.wait_ge(mmP.h, 16 * (8 + m * 8))
                    gpsimd.dma_start(out=w3[:].rearrange("p (k f) -> p k f", k=16),
                                     in_=WQ3.rearrange("(k p) f -> p k f", p=128)[:, :, m * 128:(m + 1) * 128]
                                     ).then_inc(lw.h, 16)
                    for n in range(8):
                        if gi >= 2:
                            gpsimd.wait_ge(mmP.h, 16 * (gi - 1))
                        gpsimd.dma_start(out=xn[gi % 2][:].rearrange("p (k t) -> p k t", k=16),
                                         in_=X2B.rearrange("(k p) t -> p k t", p=128)[:, :, n * 512:(n + 1) * 512]
                                         ).then_inc(lx[gi % 2].h, 16)
                        gi += 1

        if pas == 0:
            prev = 0
            for m in range(NM):
                if mkind(m) == "v":
                    continue
                for n in range(8):
                    rec[("psmvprev", m, n)] = prev
                    prev = rec.get(("psmv", m, n), 0)
    return nc


def run_l1_qkv(inputs, x2_results):
    Wq = inputs["L1_attn_Wq"]; Wk = inputs["L1_attn_Wk"]; Wv = inputs["L1_attn_Wv"]
    Wb = inputs["L1_attn_Wb"]
    ONE = np.ones((128, 1), dtype=ml_dtypes.bfloat16)
    in_maps = []
    for c in range(8):
        b, r = divmod(c, 2)
        X2nat = np.zeros((D, 4096), dtype=np.float32)
        X2nat[:, slot_tokens(0)] = x2_results[2 * b]["X2"]
        X2nat[:, slot_tokens(1)] = x2_results[2 * b + 1]["X2"]
        cols = slice(r * 1024, (r + 1) * 1024)
        WQ3 = np.ascontiguousarray(np.concatenate([Wq[:, cols], Wk[:, cols], Wv[:, cols]], axis=1))
        WB8 = np.ascontiguousarray(Wb[:, r * 8:(r + 1) * 8])
        CONV = np.ascontiguousarray(np.tile(np.concatenate(
            [inputs["L1_attn_conv_q"], inputs["L1_attn_conv_k"], inputs["L1_attn_conv_v"]]),
            (128, 1)).astype(np.float32))
        in_maps.append({"X2B": np.ascontiguousarray(X2nat), "WQ3": WQ3, "WB8": WB8,
                        "CONV": CONV, "ONE": ONE})
    nc = build_l1_qkv()
    res = run_bass_kernel_spmd(nc, in_maps, core_ids=list(range(8)))
    return res.results




def build_l1_scan(NH=8):
    nc = bass.Bass("TRN2", target_bir_lowering=False, debug=False)
    QN = nc.dram_tensor("QN", [1024, 4096], BF16, kind="ExternalInput").ap()
    KN = nc.dram_tensor("KN", [1024, 4096], BF16, kind="ExternalInput").ap()
    KB = nc.dram_tensor("KB", [1024, 4096], BF16, kind="ExternalInput").ap()
    KNT = nc.dram_tensor("KNT", [128, 8 * 32 * 128], BF16, kind="ExternalInput").ap()
    KBT = nc.dram_tensor("KBT", [128, 8 * 32 * 128], BF16, kind="ExternalInput").ap()
    VBT = nc.dram_tensor("VBT", [128, 8 * 32 * 128], BF16, kind="ExternalInput").ap()
    MSL = nc.dram_tensor("MSL", [128, 384], F32, kind="ExternalInput").ap()
    IDN = nc.dram_tensor("IDN", [128, 128], F32, kind="ExternalInput").ap()
    ONW = nc.dram_tensor("ONW", [128, 1], F32, kind="ExternalInput").ap()
    ONE = nc.dram_tensor("ONE", [128, 1], BF16, kind="ExternalInput").ap()
    O1 = nc.dram_tensor("O1", [1024, 4096], BF16, kind="ExternalOutput").ap()
    rscr = nc.dram_tensor("rscr", [1, 4096], F32, kind="Internal").ap()

    qn = nc.alloc_sbuf_tensor("qn", [128, 4096], BF16)
    kn = nc.alloc_sbuf_tensor("kn", [128, 4096], BF16)
    kb = nc.alloc_sbuf_tensor("kb", [128, 4096], BF16)
    knt = nc.alloc_sbuf_tensor("knt", [128, 4096], BF16)
    kbt = nc.alloc_sbuf_tensor("kbt", [128, 4096], BF16)
    vbt = nc.alloc_sbuf_tensor("vbt", [128, 4096], BF16)
    msl = nc.alloc_sbuf_tensor("msl", [128, 384], F32)
    idn = nc.alloc_sbuf_tensor("idn", [128, 128], F32)
    onw = nc.alloc_sbuf_tensor("onw", [128, 1], F32)
    one = nc.alloc_sbuf_tensor("one", [128, 1], BF16)
    Sf = nc.alloc_sbuf_tensor("Sf", [128, 128], F32)
    Sb = nc.alloc_sbuf_tensor("Sb", [128, 128], BF16)
    Lt1 = nc.alloc_sbuf_tensor("Lt1", [128, 128], F32)
    Lt2 = nc.alloc_sbuf_tensor("Lt2", [128, 128], F32)
    Px = nc.alloc_sbuf_tensor("Px", [128, 128], F32)
    PxT = nc.alloc_sbuf_tensor("PxT", [128, 128], F32)
    Gx = nc.alloc_sbuf_tensor("Gx", [128, 128], F32)
    GxT = nc.alloc_sbuf_tensor("GxT", [128, 128], F32)
    GxTb = nc.alloc_sbuf_tensor("GxTb", [128, 128], BF16)
    ut = nc.alloc_sbuf_tensor("ut", [128, 128], BF16)
    upt = nc.alloc_sbuf_tensor("upt", [128, 128], BF16)
    wf = nc.alloc_sbuf_tensor("wf", [128, 128], BF16)
    att = nc.alloc_sbuf_tensor("att", [128, 128], BF16)
    ost = nc.alloc_sbuf_tensor("ost", [128, 4096], BF16)
    osq = nc.alloc_sbuf_tensor("osq", [128, 512], BF16)
    rsr = nc.alloc_sbuf_tensor("rsr", [1, 4096], F32)
    rs2 = nc.alloc_sbuf_tensor("rs2", [1, 4096], F32)
    bc1 = nc.alloc_sbuf_tensor("bc1", [128, 4096], F32)
    tmo = nc.alloc_sbuf_tensor("tmo", [128, 512], F32)
    warm = nc.alloc_sbuf_tensor("warm", [1, 32], F32)
    dum2 = nc.alloc_sbuf_tensor("dum2", [1, 8], F32)
    psU = nc.alloc_psum_tensor("psU", [128, 512], F32)
    psV = nc.alloc_psum_tensor("psV", [128, 512], F32)
    psM = nc.alloc_psum_tensor("psM", [1, 512], F32)

    lds = Sem(nc, "lds"); tch = Sem(nc, "tch"); vchd = Sem(nc, "vchd")
    sqe = Sem(nc, "sqe"); wse = Sem(nc, "wse")
    rwq = Sem(nc, "rwq"); rbq = Sem(nc, "rbq"); stq = Sem(nc, "stq")

    NLEV = 6
    prog = []

    def T(fn):
        prog.append(("t", fn))

    def V(fn):
        prog.append(("v", fn))

    def S(fn):
        prog.append(("s", fn))

    def DMA(kind, fn):
        prog.append((kind, fn))

    for h in range(NH):
        hh = h
        V(lambda v, hh=hh: (v.wait_ge(stq.h, 16 * hh), v.memset(Sf[:], 0.0)))
        V(lambda v: v.memset(Sb[:], 0.0))
        for c in range(32):
            sl_ = slice(c * 128, (c + 1) * 128)
            T(lambda t, sl_=sl_: t.matmul(psU[:, 0:128], kb[:, sl_], kn[:, sl_], start=True, stop=True))
            T(lambda t, sl_=sl_: t.matmul(psU[:, 128:256], kn[:, sl_], kb[:, sl_], start=True, stop=True))
            V(lambda v: v.tensor_tensor(Lt1[:], psU[:, 0:128], msl[:, 0:128], ALU.mult))
            V(lambda v: v.tensor_tensor(Lt2[:], psU[:, 128:256], msl[:, 128:256], ALU.mult))
            V(lambda v: v.tensor_sub(Gx[:], idn[:], Lt1[:]))
            V(lambda v: v.tensor_sub(GxT[:], idn[:], Lt2[:]))
            for lev in range(NLEV):
                Xs, XsT = (Lt1, Lt2) if lev == 0 else (Px, PxT)
                T(lambda t, XsT=XsT, Xs=Xs: t.matmul(psU[:, 0:128], XsT[:], Xs[:], start=True, stop=True))
                T(lambda t, XsT=XsT, Xs=Xs: t.matmul(psU[:, 128:256], Xs[:], XsT[:], start=True, stop=True))
                V(lambda v: v.tensor_copy(Px[:], psU[:, 0:128]))
                V(lambda v: v.tensor_copy(PxT[:], psU[:, 128:256]))
                T(lambda t: t.matmul(psV[:, 0:128], idn[:], Gx[:], start=True, stop=False))
                T(lambda t: t.matmul(psV[:, 0:128], GxT[:], Px[:], start=False, stop=True))
                T(lambda t: t.matmul(psV[:, 128:256], idn[:], GxT[:], start=True, stop=False))
                T(lambda t: t.matmul(psV[:, 128:256], Px[:], GxT[:], start=False, stop=True))
                V(lambda v: v.tensor_copy(Gx[:], psV[:, 0:128]))
                V(lambda v: v.tensor_copy(GxT[:], psV[:, 128:256]))
            V(lambda v: v.tensor_copy(GxTb[:], GxT[:]))
            T(lambda t, sl_=sl_: t.matmul(psU[:, 0:128], GxTb[:], vbt[:, sl_], start=True, stop=True))
            T(lambda t, sl_=sl_: t.matmul(psU[:, 128:256], kbt[:, sl_], GxTb[:], start=True, stop=True))
            V(lambda v: v.tensor_copy(ut[:], psU[:, 0:128]))
            V(lambda v: v.tensor_copy(wf[:], psU[:, 128:256]))
            T(lambda t: t.matmul(psV[:, 0:128], wf[:], Sb[:], start=True, stop=True))
            V(lambda v: v.tensor_sub(upt[:], ut[:], psV[:, 0:128]))
            T(lambda t, sl_=sl_: t.matmul(psV[:, 128:256], kn[:, sl_], qn[:, sl_], start=True, stop=True))
            V(lambda v: v.tensor_tensor(att[:], psV[:, 128:256], msl[:, 256:384], ALU.mult))
            T(lambda t, sl_=sl_: t.matmul(psU[:, 256:384], Sb[:], qn[:, sl_], start=True, stop=False))
            T(lambda t: t.matmul(psU[:, 256:384], upt[:], att[:], start=False, stop=True))
            V(lambda v, sl_=sl_: v.tensor_copy(ost[:, sl_], psU[:, 256:384]))
            T(lambda t, sl_=sl_: t.matmul(psV[:, 256:384], knt[:, sl_], upt[:], start=True, stop=True))
            V(lambda v: v.tensor_add(Sf[:], Sf[:], psV[:, 256:384]))
            V(lambda v: v.tensor_copy(Sb[:], Sf[:]))
        for s8 in range(8):
            so = slice(s8 * 512, (s8 + 1) * 512)
            S(lambda s, so=so: s.activation(osq[:], ost[:, so], AF.Square))
            T(lambda t: t.matmul(psM[:], one[:], osq[:], start=True, stop=True))
            V(lambda v, so=so: v.tensor_scalar(rsr[0:1, so], psM[:], 1.0 / 128.0, 1e-5,
                                               ALU.mult, ALU.add))
        S(lambda s: s.activation(rs2[:], rsr[:], AF.Sqrt))
        V(lambda v: v.reciprocal(rsr[:], rs2[:]))
        DMA("dmaw", lambda sy, hh=hh: sy.dma_start(out=rscr, in_=rsr[:]).then_inc(rwq.h, 16))
        DMA("dmab", lambda sy, hh=hh: (sy.wait_ge(rwq.h, 16 * (hh + 1)),
                                       sy.dma_start(out=bc1[:], in_=rscr.partition_broadcast(128)
                                                    ).then_inc(rbq.h, 16)))
        for s8 in range(8):
            so = slice(s8 * 512, (s8 + 1) * 512)
            V(lambda v, so=so, hh=hh: (v.wait_ge(rbq.h, 16 * (hh + 1)),
                                       v.tensor_scalar(tmo[:], ost[:, so], onw[:, 0:1], None, ALU.mult)))
            V(lambda v, so=so: v.tensor_tensor(ost[:, so], tmo[:], bc1[:, so], ALU.mult))
        DMA("dmas", lambda sy, hh=hh: sy.dma_start(
            out=O1.rearrange("(g p) t -> p g t", p=128)[:, hh], in_=ost[:]).then_inc(stq.h, 16))

    nT = [0]; nV = [0]; nS = [0]
    pref = []
    for eng, fn in prog:
        pref.append((nT[0], nV[0], nS[0]))
        if eng == "t":
            nT[0] += 1
        elif eng == "v":
            nV[0] += 1
        elif eng == "s":
            nS[0] += 1
    OPS_PER_HEAD = len(prog) // NH

    with nc.Block() as block:
        @block.tensor
        def _(tensor):
            lastv = -1
            lasts = -1
            lasth = -1
            for i, (eng, fn) in enumerate(prog):
                if eng != "t":
                    continue
                h = i // OPS_PER_HEAD
                tb, vb, sb = pref[i]
                if h != lasth:
                    tensor.wait_ge(lds.h, 16 * (4 + 6 * (h + 1)))
                    lasth = h
                if vb != lastv:
                    tensor.wait_ge(vchd.h, vb)
                    lastv = vb
                if sb != lasts:
                    tensor.wait_ge(sqe.h, sb)
                    tensor.wait_ge(wse.h, 1 + sb)
                    lasts = sb
                fn(tensor).then_inc(tch.h, 1)

        @block.vector
        def _(vector):
            lastt = -1
            lasts = -1
            for i, (eng, fn) in enumerate(prog):
                if eng != "v":
                    continue
                tb, vb, sb = pref[i]
                if tb != lastt:
                    vector.wait_ge(tch.h, tb)
                    lastt = tb
                if sb != lasts:
                    vector.wait_ge(sqe.h, sb)
                    vector.wait_ge(wse.h, 1 + sb)
                    lasts = sb
                fn(vector)
                vector.tensor_copy(dum2[0:1, 0:1], one[0:1, 0:1]).then_inc(vchd.h, 1)

        @block.scalar
        def _(scalar):
            wn = [0]
            scalar.memzero(warm[:]).then_inc(wse.h, 1)
            wn[0] = 1
            for i, (eng, fn) in enumerate(prog):
                if eng != "s":
                    continue
                tb, vb, sb = pref[i]
                scalar.wait_ge(vchd.h, vb)
                fn(scalar).then_inc(sqe.h, 1)
                scalar.wait_ge(wse.h, wn[0])
                wn[0] += 1
                scalar.activation(warm[:, 0:16], warm[:, 16:32], AF.Exp).then_inc(wse.h, 1)

        @block.sync
        def _(sync):
            for i, (eng, fn) in enumerate(prog):
                if eng not in ("dmaw", "dmab", "dmas"):
                    continue
                tb, vb, sb = pref[i]
                sync.wait_ge(vchd.h, vb)
                fn(sync)
            sync.wait_ge(stq.h, 16 * NH)

        @block.gpsimd
        def _(gpsimd):
            gpsimd.dma_start(out=msl[:], in_=MSL).then_inc(lds.h, 16)
            gpsimd.dma_start(out=idn[:], in_=IDN).then_inc(lds.h, 16)
            gpsimd.dma_start(out=onw[:], in_=ONW).then_inc(lds.h, 16)
            gpsimd.dma_start(out=one[:], in_=ONE).then_inc(lds.h, 16)
            for h in range(NH):
                if h >= 1:
                    gpsimd.wait_ge(stq.h, 16 * h)
                for src_d, dst in ((QN, qn), (KN, kn), (KB, kb)):
                    gpsimd.dma_start(out=dst[:], in_=src_d.rearrange("(g p) t -> p g t", p=128)[:, h]
                                     ).then_inc(lds.h, 16)
                for src_d, dst in ((KNT, knt), (KBT, kbt), (VBT, vbt)):
                    gpsimd.dma_start(out=dst[:], in_=src_d[:, h * 4096:(h + 1) * 4096]
                                     ).then_inc(lds.h, 16)

    return nc


def run_l1_scan(inputs, qkv):
    cosn = None
    ONE = np.ones((128, 1), dtype=ml_dtypes.bfloat16)
    IDN = np.eye(128, dtype=np.float32)
    i_ = np.arange(128)[:, None]; j_ = np.arange(128)[None, :]
    stril_cm = (j_ < i_).astype(np.float32)
    striu_mc = (j_ > i_).astype(np.float32)
    inclT = (j_ >= i_).astype(np.float32)
    MSL = np.concatenate([stril_cm, striu_mc, inclT], axis=1)
    ONW = np.ascontiguousarray(inputs["L1_attn_onorm_w"].astype(np.float32)[:, None])
    in_maps = []
    for c in range(8):
        def tmize(a):
            return np.ascontiguousarray(
                np.asarray(a).reshape(8, 128, 32, 128).transpose(3, 0, 2, 1).reshape(128, 8 * 32 * 128))
        in_maps.append({"QN": qkv[c]["QN"], "KN": qkv[c]["KN"], "KB": qkv[c]["KB"],
                        "KNT": tmize(qkv[c]["KN"]), "KBT": tmize(qkv[c]["KB"]),
                        "VBT": tmize(qkv[c]["VB"]), "MSL": MSL, "IDN": IDN,
                        "ONW": ONW, "ONE": ONE})
    nc = build_l1_scan()
    res = run_bass_kernel_spmd(nc, in_maps, core_ids=list(range(8)))
    return res.results




def build_l1_tail():
    nc = bass.Bass("TRN2", target_bir_lowering=False, debug=False)
    O = nc.dram_tensor("O", [D, 2048], BF16, kind="ExternalInput").ap()
    X2IN = nc.dram_tensor("X2IN", [D, 2048], F32, kind="ExternalInput").ap()
    WOB = nc.dram_tensor("WOB", [16, 16, 128, 128], BF16, kind="ExternalInput").ap()
    GWB = nc.dram_tensor("GWB", [64, 16, 128, 128], BF16, kind="ExternalInput").ap()
    UWB = nc.dram_tensor("UWB", [64, 16, 128, 128], BF16, kind="ExternalInput").ap()
    WPB = nc.dram_tensor("WPB", [16, 64, 128, 128], BF16, kind="ExternalInput").ap()
    ONE = nc.dram_tensor("ONE", [128, 1], BF16, kind="ExternalInput").ap()
    OUT = nc.dram_tensor("OUT", [D, 2048], F32, kind="ExternalOutput").ap()
    X3F = nc.dram_tensor("X3F", [D, 2048], F32, kind="Internal").ap()
    scr = nc.dram_tensor("scr", [8, 1024], F32, kind="Internal").ap()
    scr2 = nc.dram_tensor("scr2", [1, 512], F32, kind="Internal").ap()

    NSL = 4
    RMS_EPS2 = 1.1920929e-07

    hdn = nc.alloc_sbuf_tensor("hdn", [128, 64 * 512], BF16)
    wob = nc.alloc_sbuf_tensor("wob0", [128, 16 * 128], BF16)
    y1raw = nc.alloc_sbuf_tensor("y1raw", [128, 16 * 512], F32)
    x1bn = nc.alloc_sbuf_tensor("x1bn", [128, 16 * 512], BF16)
    gwb = [nc.alloc_sbuf_tensor(f"gwb{i}", [128, 16 * 128], BF16) for i in range(2)]
    uwb = [nc.alloc_sbuf_tensor(f"uwb{i}", [128, 16 * 128], BF16) for i in range(2)]
    wpb = nc.alloc_sbuf_tensor("wpb0", [128, 64 * 128], BF16)
    gm = nc.alloc_sbuf_tensor("gm", [128, 512], BF16)
    um = nc.alloc_sbuf_tensor("um", [128, 512], BF16)
    x1fn = nc.alloc_sbuf_tensor("x1fn", [128, 16 * 512], F32)
    utn = x1fn
    y2raw = y1raw
    sumb = x1bn
    one = nc.alloc_sbuf_tensor("one", [128, 1], BF16)
    mb = nc.alloc_sbuf_tensor("mb", [128, 1024], F32)
    mb2 = nc.alloc_sbuf_tensor("mb2", [128, 512], F32)
    rrow = nc.alloc_sbuf_tensor("rrow", [1, 1536], F32)
    y1q = y1raw[:].bitcast(BF16)
    utq = x1fn[:].bitcast(BF16)
    srow = x1fn[0:1, 0:1024]
    tmp5 = mb[0:1, 0:512]
    tmp6 = mb[0:1, 512:1024]
    warm = nc.alloc_sbuf_tensor("warm", [1, 32], F32)
    dum2 = nc.alloc_sbuf_tensor("dum2", [1, 8], F32)
    tmpa = nc.alloc_sbuf_tensor("tmpa", [128, 512], F32)
    psA = nc.alloc_psum_tensor("psA", [128, 512], F32)
    psB = nc.alloc_psum_tensor("psB", [128, 512], F32)
    psM = nc.alloc_psum_tensor("psM", [1, 512], F32)
    psQ = nc.alloc_psum_tensor("psQ", [1, 512], F32)
    psQ2 = nc.alloc_psum_tensor("psQ2", [1, 512], F32)

    lres = Sem(nc, "lres")
    lon = Sem(nc, "lon"); lut = Sem(nc, "lut"); lx1 = Sem(nc, "lx1")
    lwm = Sem(nc, "lwm0")
    lgw = [Sem(nc, "lgw0"), Sem(nc, "lgw1")]
    ldw = Sem(nc, "ldw0")
    mmA = Sem(nc, "mmA"); mmB = Sem(nc, "mmB"); mmC = Sem(nc, "mmC"); mmS = Sem(nc, "mmS")
    mmS2 = Sem(nc, "mmS2")
    evA = Sem(nc, "evA"); evAd = Sem(nc, "evAd")
    seB = Sem(nc, "seB"); evB = Sem(nc, "evB"); evBd = Sem(nc, "evBd")
    evC = Sem(nc, "evC"); evCd = Sem(nc, "evCd")
    cst = Sem(nc, "cst"); cstd = Sem(nc, "cstd"); vch = Sem(nc, "vch")
    sqq = Sem(nc, "sqq"); wse = Sem(nc, "wse")
    sqq2 = Sem(nc, "sqq2"); wse2 = Sem(nc, "wse2")
    rq = Sem(nc, "rq"); rqd = Sem(nc, "rqd"); rq2 = Sem(nc, "rq2")
    rq3 = Sem(nc, "rq3"); rq3d = Sem(nc, "rq3d")
    rw2 = Sem(nc, "rw2"); rb2 = Sem(nc, "rb2")
    ap2 = Sem(nc, "ap2"); ap2d = Sem(nc, "ap2d")
    stq = Sem(nc, "stq"); rwq = Sem(nc, "rwq"); rbq = Sem(nc, "rbq")
    apq = Sem(nc, "apq"); apqd = Sem(nc, "apqd")
    stX3 = Sem(nc, "stX3"); stX2 = Sem(nc, "stX2")

    with nc.Block() as block:
        @block.gpsimd
        def _(gpsimd):
            gpsimd.dma_start(out=one[:], in_=ONE).then_inc(lres.h, 16)
            for n in range(NSL):
                if n >= 1:
                    gpsimd.wait_ge(stX2.h, 16 * n)
                gpsimd.dma_start(out=hdn[:].rearrange("p (k t) -> p k t", k=64)[:, 0:16],
                                 in_=O.rearrange("(k p) t -> p k t", p=128)[:, :, n * 512:(n + 1) * 512]
                                 ).then_inc(lon.h, 16)
                gpsimd.dma_start(out=utn[:].rearrange("p (k t) -> p k t", k=16),
                                 in_=X2IN.rearrange("(k p) t -> p k t", p=128)[:, :, n * 512:(n + 1) * 512]
                                 ).then_inc(lut.h, 16)
                for m in range(16):
                    if n * 16 + m >= 1:
                        gpsimd.wait_ge(mmA.h, 16 * n + m)
                    gpsimd.dma_start(out=wob[:].rearrange("p (k f) -> p k f", k=16),
                                     in_=WOB[m].rearrange("k p f -> p k f")
                                     ).then_inc(lwm.h, 16)
                for m in range(64):
                    if n * 64 + m >= 2:
                        gpsimd.wait_ge(mmB.h, 2 * (64 * n + m - 1))
                    gpsimd.dma_start(out=gwb[m % 2][:].rearrange("p (k f) -> p k f", k=16),
                                     in_=GWB[m].rearrange("k p f -> p k f")
                                     ).then_inc(lgw[m % 2].h, 16)
                    gpsimd.dma_start(out=uwb[m % 2][:].rearrange("p (k f) -> p k f", k=16),
                                     in_=UWB[m].rearrange("k p f -> p k f")
                                     ).then_inc(lgw[m % 2].h, 16)
                gpsimd.wait_ge(stX3.h, 16 * (n + 1))
                gpsimd.wait_ge(mmS2.h, n + 1)
                gpsimd.dma_start(out=x1fn[:].rearrange("p (k t) -> p k t", k=16),
                                 in_=X3F.rearrange("(k p) t -> p k t", p=128)[:, :, n * 512:(n + 1) * 512]
                                 ).then_inc(lx1.h, 16)
                for m in range(16):
                    if n * 16 + m >= 1:
                        gpsimd.wait_ge(mmC.h, 16 * n + m)
                    gpsimd.dma_start(out=wpb[:].rearrange("p (k f) -> p k f", k=64),
                                     in_=WPB[m].rearrange("k p f -> p k f")
                                     ).then_inc(ldw.h, 16)

        @block.tensor
        def _(tensor):
            tensor.wait_ge(lres.h, 16)
            for n in range(NSL):
                tensor.wait_ge(lon.h, 16 * (n + 1))
                for m in range(16):
                    tensor.wait_ge(lwm.h, 16 * (16 * n + m + 1))
                    if m >= 2:
                        tensor.wait_ge(evA.h, 16 * n + m - 1)
                    inst = None
                    for k in range(16):
                        ps = psA if m % 2 == 0 else psB
                        inst = tensor.matmul(ps[:], wob[:, k * 128:(k + 1) * 128],
                                             hdn[:, k * 512:(k + 1) * 512],
                                             start=(k == 0), stop=(k == 15))
                    inst.then_inc(mmA.h, 1)
                ln = 2 * n
                tensor.wait_ge(cst.h, ln + 1)
                tensor.wait_ge(cstd.h, ln + 1)
                tensor.wait_ge(sqq.h, ln + 1)
                tensor.wait_ge(wse.h, 3 + 2 * ln + 1)
                inst = None
                for k in range(16):
                    inst = tensor.matmul(psM[:], one[:], sumb[:, k * 512:(k + 1) * 512],
                                         start=(k == 0), stop=(k == 15))
                inst.then_inc(mmS.h, 1)
                inst = None
                for k in range(16):
                    inst = tensor.matmul(psQ[:], one[:], hdn[:, k * 512:(k + 1) * 512],
                                         start=(k == 0), stop=(k == 15))
                inst.then_inc(mmS.h, 1)
                tensor.wait_ge(apq.h, 16 * (ln + 1))
                tensor.wait_ge(apqd.h, ln + 1)
                for m in range(64):
                    tensor.wait_ge(lgw[m % 2].h, 32 * ((64 * n + m) // 2 + 1))
                    if m >= 1:
                        tensor.wait_ge(seB.h, 64 * n + m)
                        tensor.wait_ge(evB.h, 2 * (64 * n + m) - 1)
                    inst = None
                    for k in range(16):
                        inst = tensor.matmul(psA[:], gwb[m % 2][:, k * 128:(k + 1) * 128],
                                             x1bn[:, k * 512:(k + 1) * 512],
                                             start=(k == 0), stop=(k == 15))
                    inst.then_inc(mmB.h, 1)
                    inst = None
                    for k in range(16):
                        inst = tensor.matmul(psB[:], uwb[m % 2][:, k * 128:(k + 1) * 128],
                                             x1bn[:, k * 512:(k + 1) * 512],
                                             start=(k == 0), stop=(k == 15))
                    inst.then_inc(mmB.h, 1)
                tensor.wait_ge(sqq2.h, 2 * (n + 1))
                tensor.wait_ge(wse2.h, 3 * n + 2)
                inst = None
                for k in range(64):
                    src = y1q[:, k * 512:(k + 1) * 512] if k < 32 \
                        else utq[:, (k - 32) * 512:(k - 31) * 512]
                    inst = tensor.matmul(psQ2[:], one[:], src,
                                         start=(k == 0), stop=(k == 63))
                inst.then_inc(mmS2.h, 1)
                tensor.wait_ge(ap2.h, 64 * (n + 1))
                tensor.wait_ge(ap2d.h, n + 1)
                for m in range(16):
                    tensor.wait_ge(ldw.h, 16 * (16 * n + m + 1))
                    if m >= 2:
                        tensor.wait_ge(evC.h, 16 * n + m - 1)
                    inst = None
                    for k in range(64):
                        ps = psA if m % 2 == 0 else psB
                        inst = tensor.matmul(ps[:], wpb[:, k * 128:(k + 1) * 128],
                                             hdn[:, k * 512:(k + 1) * 512],
                                             start=(k == 0), stop=(k == 63))
                    inst.then_inc(mmC.h, 1)
                ln = 2 * n + 1
                tensor.wait_ge(cst.h, ln + 1)
                tensor.wait_ge(cstd.h, ln + 1)
                tensor.wait_ge(sqq.h, ln + 1)
                tensor.wait_ge(wse.h, 3 + 2 * ln + 1)
                inst = None
                for k in range(16):
                    inst = tensor.matmul(psM[:], one[:], sumb[:, k * 512:(k + 1) * 512],
                                         start=(k == 0), stop=(k == 15))
                inst.then_inc(mmS.h, 1)
                inst = None
                for k in range(16):
                    inst = tensor.matmul(psQ[:], one[:], hdn[:, k * 512:(k + 1) * 512],
                                         start=(k == 0), stop=(k == 15))
                inst.then_inc(mmS.h, 1)

        @block.scalar
        def _(scalar):
            wn = [0]
            w2n = [0]

            def wdummy():
                scalar.wait_ge(wse.h, wn[0])
                wn[0] += 1
                return scalar.activation(warm[:, 0:16], warm[:, 16:32], AF.Exp
                                         ).then_inc(wse.h, 1)

            def w2dummy():
                scalar.wait_ge(wse2.h, w2n[0])
                w2n[0] += 1
                return scalar.activation(warm[:, 0:16], warm[:, 16:32], AF.Exp
                                         ).then_inc(wse2.h, 1)

            scalar.memzero(warm[:]).then_inc(wse.h, 1)
            wn[0] = 1
            wdummy()
            wdummy()
            scalar.wait_ge(lres.h, 16)
            for n in range(NSL):
                ln = 2 * n
                scalar.wait_ge(evA.h, 16 * (n + 1))
                scalar.wait_ge(evAd.h, n + 1)
                scalar.activation(hdn[:, 0:8192], y1raw[:], AF.Square).then_inc(sqq.h, 1)
                wdummy()
                scalar.wait_ge(stq.h, 5 * ln + 2)
                scalar.activation(tmp6, tmp5, AF.Sqrt).then_inc(stq.h, 1)
                wdummy()
                for m in range(64):
                    scalar.wait_ge(mmB.h, 2 * (64 * n + m) + 1)
                    scalar.wait_ge(evB.h, 2 * (64 * n + m))
                    scalar.activation(gm[:], psA[:], AF.Silu).then_inc(seB.h, 1)
                scalar.wait_ge(evB.h, 128 * (n + 1))
                scalar.wait_ge(evBd.h, n + 1)
                scalar.wait_ge(stX3.h, 16 * (n + 1))
                scalar.activation(y1q[:], hdn[:, 0:16384], AF.Square).then_inc(sqq2.h, 1)
                w2dummy()
                scalar.activation(utq[:], hdn[:, 16384:32768], AF.Square).then_inc(sqq2.h, 1)
                w2dummy()
                scalar.wait_ge(rq.h, n + 1)
                scalar.wait_ge(rqd.h, n + 1)
                scalar.activation(rrow[0:1, 512:1024], rrow[0:1, 0:512], AF.Sqrt
                                  ).then_inc(rq2.h, 1)
                w2dummy()
                ln = 2 * n + 1
                scalar.wait_ge(evC.h, 16 * (n + 1))
                scalar.wait_ge(evCd.h, n + 1)
                scalar.activation(hdn[:, 0:8192], y2raw[:], AF.Square).then_inc(sqq.h, 1)
                wdummy()
                scalar.wait_ge(stq.h, 5 * ln + 2)
                scalar.activation(tmp6, tmp5, AF.Sqrt).then_inc(stq.h, 1)
                wdummy()

        @block.vector
        def _(vector):
            vn = [0]

            def vdummy():
                return vector.tensor_copy(dum2[0:1, 0:1], one[0:1, 0:1])

            def ln_stats_apply(n, ph, src_):
                ln = 2 * n + ph
                vector.wait_ge(mmS.h, 2 * ln + 2)
                vector.tensor_scalar(srow[:, 0:512], psM[:], 1.0 / 2048.0, None, ALU.mult
                                     ).then_inc(vch.h, 1); vn[0] += 1
                s1 = vn[0]
                vector.tensor_scalar(tmp5, psQ[:], 1.0 / 2048.0, None, ALU.mult
                                     ).then_inc(vch.h, 1); vn[0] += 1
                vector.wait_ge(vch.h, s1)
                vector.tensor_tensor(tmp6, srow[:, 0:512], srow[:, 0:512], ALU.mult
                                     ).then_inc(vch.h, 1); vn[0] += 1
                vector.wait_ge(vch.h, vn[0])
                vector.tensor_sub(tmp5, tmp5, tmp6).then_inc(vch.h, 1); vn[0] += 1
                vector.wait_ge(vch.h, vn[0])
                vector.tensor_scalar(tmp5, tmp5, 1e-5, None, ALU.add).then_inc(stq.h, 1)
                vdummy().then_inc(stq.h, 1)
                vector.wait_ge(stq.h, 5 * ln + 3)
                vector.wait_ge(wse.h, 3 + 2 * ln + 2)
                vector.reciprocal(srow[:, 512:1024], tmp6).then_inc(stq.h, 1)
                vdummy().then_inc(stq.h, 1)
                vector.wait_ge(rbq.h, 16 * (ln + 1))
                for m in range(16):
                    vector.wait_ge(vch.h, vn[0])
                    if ph == 1 and m >= 1:
                        vector.wait_ge(apq.h, 16 * ln + m)
                    vector.tensor_sub(tmpa[:], src_[:, m * 512:(m + 1) * 512], mb[:, 0:512]
                                      ).then_inc(vch.h, 1); vn[0] += 1
                    vector.wait_ge(vch.h, vn[0])
                    if ph == 0:
                        vector.tensor_mul(y1raw[:, m * 512:(m + 1) * 512], tmpa[:],
                                          mb[:, 512:1024]).then_inc(vch.h, 1); vn[0] += 1
                        vector.wait_ge(vch.h, vn[0])
                        vector.tensor_copy(x1bn[:, m * 512:(m + 1) * 512],
                                           y1raw[:, m * 512:(m + 1) * 512]).then_inc(apq.h, 1)
                    else:
                        vector.tensor_mul(y2raw[:, m * 512:(m + 1) * 512], tmpa[:],
                                          mb[:, 512:1024]).then_inc(apq.h, 1)
                vdummy().then_inc(apqd.h, 1)

            vector.wait_ge(lres.h, 16)
            for n in range(NSL):
                vector.wait_ge(lut.h, 16 * (n + 1))
                for m in range(16):
                    vector.wait_ge(mmA.h, 16 * n + m + 1)
                    ps = psA if m % 2 == 0 else psB
                    vector.tensor_add(y1raw[:, m * 512:(m + 1) * 512], ps[:],
                                      utn[:, m * 512:(m + 1) * 512]).then_inc(evA.h, 1)
                vdummy().then_inc(evAd.h, 1)
                vector.wait_ge(evA.h, 16 * (n + 1))
                vector.tensor_copy(sumb[:], y1raw[:]).then_inc(cst.h, 1)
                vdummy().then_inc(cstd.h, 1)
                ln_stats_apply(n, 0, y1raw)
                for m in range(64):
                    vector.wait_ge(mmB.h, 2 * (64 * n + m) + 2)
                    vector.wait_ge(evB.h, 2 * (64 * n + m))
                    vector.tensor_copy(um[:], psB[:]).then_inc(evB.h, 1)
                    vector.wait_ge(seB.h, 64 * n + m + 1)
                    vector.wait_ge(evB.h, 2 * (64 * n + m) + 1)
                    vector.tensor_tensor(hdn[:, m * 512:(m + 1) * 512], gm[:], um[:],
                                         ALU.mult).then_inc(evB.h, 1)
                vdummy().then_inc(evBd.h, 1)
                vector.wait_ge(mmS2.h, n + 1)
                vector.tensor_scalar(rrow[0:1, 0:512], psQ2[:], 1.0 / 8192.0, RMS_EPS2,
                                     ALU.mult, ALU.add).then_inc(rq.h, 1)
                vdummy().then_inc(rqd.h, 1)
                vector.wait_ge(rq2.h, n + 1)
                vector.wait_ge(wse2.h, 3 * (n + 1))
                vector.reciprocal(rrow[0:1, 1024:1536], rrow[0:1, 512:1024]).then_inc(rq3.h, 1)
                vdummy().then_inc(rq3d.h, 1)
                vector.wait_ge(rb2.h, 16 * (n + 1))
                for m in range(64):
                    vector.tensor_mul(hdn[:, m * 512:(m + 1) * 512],
                                      hdn[:, m * 512:(m + 1) * 512], mb2[:]).then_inc(ap2.h, 1)
                vdummy().then_inc(ap2d.h, 1)
                vector.wait_ge(lx1.h, 16 * (n + 1))
                for m in range(16):
                    vector.wait_ge(mmC.h, 16 * n + m + 1)
                    ps = psA if m % 2 == 0 else psB
                    vector.tensor_add(y2raw[:, m * 512:(m + 1) * 512], ps[:],
                                      x1fn[:, m * 512:(m + 1) * 512]).then_inc(evC.h, 1)
                vdummy().then_inc(evCd.h, 1)
                vector.wait_ge(evC.h, 16 * (n + 1))
                vector.tensor_copy(sumb[:], y2raw[:]).then_inc(cst.h, 1)
                vdummy().then_inc(cstd.h, 1)
                ln_stats_apply(n, 1, y2raw)

        @block.sync
        def _(sync):
            for n in range(NSL):
                ln = 2 * n
                sync.wait_ge(stq.h, 5 * (ln + 1))
                sync.dma_start(out=scr[ln % 8], in_=srow[:]).then_inc(rwq.h, 16)
                sync.wait_ge(rwq.h, 16 * (ln + 1))
                sync.dma_start(out=mb[:], in_=scr[ln % 8].partition_broadcast(128)
                               ).then_inc(rbq.h, 16)
                sync.wait_ge(apq.h, 16 * (ln + 1))
                sync.wait_ge(apqd.h, ln + 1)
                sync.dma_start(
                    out=X3F.rearrange("(k p) t -> p k t", p=128)[:, :, n * 512:(n + 1) * 512],
                    in_=y1raw[:].rearrange("p (k t) -> p k t", k=16)).then_inc(stX3.h, 16)
                sync.wait_ge(rq3.h, n + 1)
                sync.wait_ge(rq3d.h, n + 1)
                sync.dma_start(out=scr2[0], in_=rrow[0:1, 1024:1536]).then_inc(rw2.h, 16)
                sync.wait_ge(rw2.h, 16 * (n + 1))
                sync.dma_start(out=mb2[:], in_=scr2[0].partition_broadcast(128)
                               ).then_inc(rb2.h, 16)
                ln = 2 * n + 1
                sync.wait_ge(stq.h, 5 * (ln + 1))
                sync.dma_start(out=scr[ln % 8], in_=srow[:]).then_inc(rwq.h, 16)
                sync.wait_ge(rwq.h, 16 * (ln + 1))
                sync.dma_start(out=mb[:], in_=scr[ln % 8].partition_broadcast(128)
                               ).then_inc(rbq.h, 16)
                sync.wait_ge(apq.h, 16 * (ln + 1))
                sync.wait_ge(apqd.h, ln + 1)
                sync.dma_start(
                    out=OUT.rearrange("(k p) t -> p k t", p=128)[:, :, n * 512:(n + 1) * 512],
                    in_=y2raw[:].rearrange("p (k t) -> p k t", k=16)).then_inc(stX2.h, 16)
            sync.wait_ge(stX2.h, 16 * NSL)

    return nc


def run_l1_tail(inputs, x2_results, scan_results, qgu, qp):
    Wo = np.asarray(inputs["L1_attn_Wo"], np.float32)
    WOB1 = np.ascontiguousarray(
        Wo.reshape(16, 128, 16, 128).transpose(2, 0, 1, 3).astype(ml_dtypes.bfloat16))
    WGUT = np.ascontiguousarray(np.asarray(qgu).T)
    GW = WGUT.reshape(16, 128, 128, 128).transpose(2, 0, 1, 3)
    GWB1 = np.ascontiguousarray(GW[0:64])
    UWB1 = np.ascontiguousarray(GW[64:128])
    WPT = np.ascontiguousarray(np.asarray(qp).T)
    WPB1 = np.ascontiguousarray(WPT.reshape(64, 128, 16, 128).transpose(2, 0, 1, 3))
    ONE = np.ones((128, 1), dtype=ml_dtypes.bfloat16)
    ofull = {}
    for b in range(4):
        ofull[b] = np.concatenate(
            [np.asarray(scan_results[2 * b]["O1"]).T,
             np.asarray(scan_results[2 * b + 1]["O1"]).T], axis=1)
    in_maps = []
    for c in range(8):
        b, role = divmod(c, 2)
        of = np.ascontiguousarray(ofull[b][slot_tokens(role)].T)
        in_maps.append({"O": of, "X2IN": np.ascontiguousarray(x2_results[c]["X2"]),
                        "WOB": WOB1, "GWB": GWB1, "UWB": UWB1, "WPB": WPB1, "ONE": ONE})
    nc = build_l1_tail()
    res = run_bass_kernel_spmd(nc, in_maps, core_ids=list(range(8)))
    return res.results


def kernel(**inputs):
    qgu, qp = run_quant(inputs)
    proj = run_l0_proj(inputs)
    att = run_l0_attn(inputs, proj)
    tail = run_l0_tail(inputs, att)
    qkv = run_l1_qkv(inputs, tail)
    scan = run_l1_scan(inputs, qkv)
    fin = run_l1_tail(inputs, tail, scan, qgu, qp)
    out = np.zeros((B, S, D), dtype=np.float32)
    for c in range(8):
        b, role = divmod(c, 2)
        out[b, slot_tokens(role), :] = np.asarray(fin[c]["OUT"], np.float32).T
    return out


# version 02
# speedup vs baseline: 1.0819x; 1.0819x over previous
import numpy as np
import ml_dtypes
import concourse.bass as bass
import concourse.mybir as mybir
from concourse.bass_utils import run_bass_kernel_spmd

F32 = mybir.dt.float32
BF16 = mybir.dt.bfloat16
AF = mybir.ActivationFunctionType
ALU = mybir.AluOpType
AX = mybir.AxisListType

B, S, D = 4, 4096, 2048
QUADS = [[0, 3, 4, 7], [1, 2, 5, 6]]
NKT = [8, 16, 24, 32]


def slot_tokens(role):
    return np.concatenate([np.arange(512 * q, 512 * (q + 1)) for q in QUADS[role]])


class Sem:
    def __init__(self, nc, name):
        self.h = nc.alloc_semaphore(name)


def rope_host_tables():
    inv = 1.0 / (10000.0 ** (np.arange(0, 128, 2, dtype=np.float64) / 128))
    pos = np.arange(S, dtype=np.float64)
    fr = pos[:, None] * inv[None, :]
    emb = np.concatenate([fr, fr], axis=-1)
    return np.cos(emb).astype(np.float32), np.sin(emb).astype(np.float32)


def pi_matrix():
    P = np.zeros((128, 128), dtype=np.float32)
    for i in range(64):
        P[i, 64 + i] = -1.0
        P[64 + i, i] = 1.0
    return P



def build_l0_proj():
    nc = bass.Bass("TRN2", target_bir_lowering=False, debug=False)
    uT = nc.dram_tensor("uT", [D, 2048], F32, kind="ExternalInput").ap()
    WqT = nc.dram_tensor("WqT", [D, 4096], F32, kind="ExternalInput").ap()
    WkT = nc.dram_tensor("WkT", [D, 1024], F32, kind="ExternalInput").ap()
    WvT = nc.dram_tensor("WvT", [D, 1024], F32, kind="ExternalInput").ap()
    ROPE = nc.dram_tensor("ROPE", [128, 4 * 2048], F32, kind="ExternalInput").ap()
    PIT = nc.dram_tensor("PIT", [128, 128], F32, kind="ExternalInput").ap()
    Qo = nc.dram_tensor("Qo", [4096, 2048], BF16, kind="ExternalOutput").ap()
    Ko = nc.dram_tensor("Ko", [1024, 2048], BF16, kind="ExternalOutput").ap()
    Vo = nc.dram_tensor("Vo", [1024, 2048], BF16, kind="ExternalOutput").ap()

    NK = 16
    ub = nc.alloc_sbuf_tensor("ub", [128, NK * 2048], BF16)
    wt = [nc.alloc_sbuf_tensor(f"wt{i}", [128, NK * 128], BF16) for i in range(2)]
    rope = nc.alloc_sbuf_tensor("rope", [128, 4 * 2048], F32)
    pit = nc.alloc_sbuf_tensor("pit", [128, 128], BF16)
    qraw = [nc.alloc_sbuf_tensor(f"qraw{i}", [128, 512], BF16) for i in range(2)]
    t1 = nc.alloc_sbuf_tensor("t1", [128, 512], F32)
    t2 = nc.alloc_sbuf_tensor("t2", [128, 512], F32)
    dum = nc.alloc_sbuf_tensor("dum", [1, 8], F32)
    dum2 = nc.alloc_sbuf_tensor("dum2", [1, 8], F32)
    stage = [nc.alloc_sbuf_tensor(f"stage{i}", [128, 2048], BF16) for i in range(2)]
    psA = nc.alloc_psum_tensor("psA", [128, 512], F32)
    psB = nc.alloc_psum_tensor("psB", [128, 512], F32)
    psC = nc.alloc_psum_tensor("psC", [128, 512], F32)

    wld = [Sem(nc, "wld0"), Sem(nc, "wld1")]; uld = Sem(nc, "uld"); mmA = Sem(nc, "mmA")
    veq = Sem(nc, "veq"); rot = Sem(nc, "rot"); gfin = Sem(nc, "gfin")
    sfl = Sem(nc, "sfl"); sto = [Sem(nc, "sto0"), Sem(nc, "sto1")]

    MTOT = 48
    NQK = 160
    kind = lambda im: ("q" if im < 32 else ("k" if im < 40 else "v"))

    def wsrc(im):
        if im < 32:
            return WqT, im
        if im < 40:
            return WkT, im - 32
        return WvT, im - 40

    with nc.Block() as block:
        @block.gpsimd
        def _(gpsimd):
            gpsimd.dma_start(out=ub[:].rearrange("p (k t) -> p k t", k=NK),
                             in_=uT.rearrange("(k p) t -> p k t", p=128)).then_inc(uld.h, 16)
            gpsimd.dma_start(out=rope[:], in_=ROPE).then_inc(uld.h, 16)
            gpsimd.dma_start(out=pit[:], in_=PIT).then_inc(uld.h, 16)
            for im in range(MTOT):
                Wd, m = wsrc(im)
                if im >= 2:
                    gpsimd.wait_ge(mmA.h, 4 * (im - 1))
                gpsimd.dma_start(
                    out=wt[im % 2][:].rearrange("p (k f) -> p k f", k=NK),
                    in_=Wd.rearrange("(k p) f -> p k f", p=128)[:, :, m * 128:(m + 1) * 128]
                ).then_inc(wld[im % 2].h, 16)

        @block.tensor
        def _(tensor):
            tensor.wait_ge(uld.h, 48)
            for im in range(MTOT):
                tensor.wait_ge(wld[im % 2].h, 16 * (im // 2 + 1))
                for n in range(4):
                    g = im * 4 + n
                    ps = psA if n % 2 == 0 else psB
                    if g >= 2:
                        if g - 2 >= NQK:
                            tensor.wait_ge(gfin.h, g - 1)
                        else:
                            tensor.wait_ge(veq.h, g - 1)
                    inst = None
                    for k in range(NK):
                        inst = tensor.matmul(
                            ps[:],
                            wt[im % 2][:, k * 128:(k + 1) * 128],
                            ub[:, k * 2048 + n * 512: k * 2048 + (n + 1) * 512],
                            start=(k == 0), stop=(k == NK - 1))
                    inst.then_inc(mmA.h, 1)
                    gp = g - 1
                    if 0 <= gp < NQK and g <= NQK:
                        tensor.wait_ge(veq.h, g + 1)
                        if gp >= 1:
                            tensor.wait_ge(gfin.h, gp)
                        tensor.matmul(psC[:], pit[:], qraw[gp % 2][:],
                                      start=True, stop=True).then_inc(rot.h, 1)

        @block.vector
        def _(vector):
            nrope = 0

            def rope_emit(gp):
                nonlocal nrope
                imp, np_ = divmod(gp, 4)
                knd = kind(imp)
                if np_ == 0 and imp >= 2:
                    vector.wait_ge(sto[imp % 2].h, 16 * (imp // 2))
                off = 0 if knd == "q" else 2 * 2048
                cos = rope[:, off + np_ * 512: off + (np_ + 1) * 512]
                sin = rope[:, off + 2048 + np_ * 512: off + 2048 + (np_ + 1) * 512]
                vector.tensor_tensor(t1[:], qraw[gp % 2][:], cos, ALU.mult)
                nrope += 1
                vector.wait_ge(rot.h, nrope)
                vector.tensor_tensor(t2[:], psC[:], sin, ALU.mult)
                vector.tensor_add(stage[imp % 2][:, np_ * 512:(np_ + 1) * 512],
                                  t1[:], t2[:]).then_inc(gfin.h, 1)
                if np_ == 3:
                    vector.tensor_copy(dum2[0:1, 0:8], pit[0:1, 0:8]).then_inc(sfl.h, 1)

            for g in range(NQK):
                vector.wait_ge(mmA.h, g + 1)
                ps = psA if g % 2 == 0 else psB
                vector.tensor_copy(qraw[g % 2][:], ps[:]).then_inc(veq.h, 1)
                if g == NQK - 1:
                    vector.tensor_copy(dum2[0:1, 0:8], pit[0:1, 0:8]).then_inc(veq.h, 1)
                if g >= 1:
                    rope_emit(g - 1)
            rope_emit(NQK - 1)
            for g in range(NQK, 192):
                im, n = divmod(g, 4)
                if n == 0 and im >= 2:
                    vector.wait_ge(sto[im % 2].h, 16 * (im // 2))
                vector.wait_ge(mmA.h, g + 1)
                ps = psA if g % 2 == 0 else psB
                vector.tensor_copy(stage[im % 2][:, n * 512:(n + 1) * 512], ps[:]
                                   ).then_inc(gfin.h, 1)
                if n == 3:
                    vector.tensor_copy(dum2[0:1, 0:8], pit[0:1, 0:8]).then_inc(sfl.h, 1)

        @block.sync
        def _(sync):
            for im in range(MTOT):
                knd = kind(im)
                sync.wait_ge(gfin.h, 4 * (im + 1))
                sync.wait_ge(sfl.h, im + 1)
                if knd == "q":
                    dst = Qo.rearrange("(m p) t -> p m t", p=128)[:, im]
                elif knd == "k":
                    dst = Ko.rearrange("(m p) t -> p m t", p=128)[:, im - 32]
                else:
                    dst = Vo.rearrange("(m p) t -> p m t", p=128)[:, im - 40]
                sync.dma_start(out=dst, in_=stage[im % 2][:]).then_inc(sto[im % 2].h, 16)
            sync.wait_ge(sto[0].h, 16 * (MTOT // 2))
            sync.wait_ge(sto[1].h, 16 * (MTOT // 2))

    return nc



def attn_tiles():
    tiles = []
    for g in range(4):
        for kt in range(NKT[g]):
            masked = kt >= NKT[g] - 8
            j = kt - (NKT[g] - 8)
            tiles.append((g, kt, masked, j))
    assert len(tiles) == 80
    return tiles




DBG_NH = 32
DBG_NOPDUM = False


def build_l0_attn():
    nc = bass.Bass("TRN2", target_bir_lowering=False, debug=False)
    Q = nc.dram_tensor("Q", [4096, 2048], BF16, kind="ExternalInput").ap()
    K = nc.dram_tensor("K", [1024, 4096], BF16, kind="ExternalInput").ap()
    Vt = nc.dram_tensor("Vt", [128, 8 * 32 * 128], BF16, kind="ExternalInput").ap()
    MSK = nc.dram_tensor("MSK", [128, 16 * 512], BF16, kind="ExternalInput").ap()
    ONE = nc.dram_tensor("ONE", [128, 1], BF16, kind="ExternalInput").ap()
    Oo = nc.dram_tensor("Oo", [4096, 2048], BF16, kind="ExternalOutput").ap()
    scr = nc.dram_tensor("scr", [4, 2048], F32, kind="Internal").ap()

    kb = nc.alloc_sbuf_tensor("kb", [128, 8 * 4096], BF16)
    vb = nc.alloc_sbuf_tensor("vb", [128, 8 * 32 * 128], BF16)
    msk = nc.alloc_sbuf_tensor("msk", [128, 16 * 512], BF16)
    one = nc.alloc_sbuf_tensor("one", [128, 1], BF16)
    qhb = [nc.alloc_sbuf_tensor(f"qhb{i}", [128, 2048], BF16) for i in range(2)]
    pt = [nc.alloc_sbuf_tensor(f"pt{i}", [128, 512], BF16) for i in range(4)]
    ptm = [nc.alloc_sbuf_tensor(f"ptm{i}", [128, 512], BF16) for i in range(4)]
    ost = [nc.alloc_sbuf_tensor(f"ost{i}", [128, 2048], BF16) for i in range(2)]
    rs = nc.alloc_sbuf_tensor("rs", [1, 2048], F32)
    rcpr = nc.alloc_sbuf_tensor("rcpr", [1, 2048], F32)
    warm = nc.alloc_sbuf_tensor("warm", [1, 32], F32)
    dum = nc.alloc_sbuf_tensor("dum", [1, 8], F32)
    dum2 = nc.alloc_sbuf_tensor("dum2", [1, 8], F32)
    recipb = [nc.alloc_sbuf_tensor(f"recipb{i}", [128, 2048], F32) for i in range(2)]
    psS = [nc.alloc_psum_tensor(f"psS{i}", [128, 512], F32) for i in range(3)]
    psO = [nc.alloc_psum_tensor(f"psO{i}", [128, 512], F32) for i in range(4)]
    psC = [nc.alloc_psum_tensor(f"psC{i}", [1, 512], F32) for i in range(1)]

    ld = Sem(nc, "ld"); qld = [Sem(nc, "qld0"), Sem(nc, "qld1")]; smm = Sem(nc, "smm"); wse = Sem(nc, "wse")
    exq = Sem(nc, "exq"); mskd = Sem(nc, "mskd"); mskp = Sem(nc, "mskp"); avm = Sem(nc, "avm")
    rcq = Sem(nc, "rcq")
    avq = Sem(nc, "avq"); cse = Sem(nc, "cse"); rcp = Sem(nc, "rcp")
    rw = Sem(nc, "rw"); rb = Sem(nc, "rb"); oev = Sem(nc, "oev")
    sto = [Sem(nc, "sto0"), Sem(nc, "sto1")]

    NH = DBG_NH
    tiles = attn_tiles()
    mreal_of = {}
    mreal = 0
    for h in range(NH):
        for i, (g, kt, masked, j) in enumerate(tiles):
            if masked:
                mreal_of[(h, i)] = mreal
                mreal += 1

    with nc.Block() as block:
        @block.gpsimd
        def _(gpsimd):
            gpsimd.dma_start(out=kb[:].rearrange("p (h t) -> p h t", h=8),
                             in_=K.rearrange("(h p) t -> p h t", p=128)).then_inc(ld.h, 16)
            gpsimd.dma_start(out=vb[:], in_=Vt).then_inc(ld.h, 16)
            gpsimd.dma_start(out=msk[:], in_=MSK).then_inc(ld.h, 16)
            gpsimd.dma_start(out=one[:], in_=ONE).then_inc(ld.h, 16)
            for h in range(NH):
                if h >= 2:
                    gpsimd.wait_ge(avq.h, (h - 1) * 4)
                gpsimd.dma_start(out=qhb[h % 2][:],
                                 in_=Q.rearrange("(m p) t -> p m t", p=128)[:, h]
                                 ).then_inc(qld[h % 2].h, 16)

        @block.tensor
        def _(tensor):
            tensor.wait_ge(ld.h, 64)
            LA = 3
            for h in range(NH):
                kvh = h // 4
                tensor.wait_ge(qld[h % 2].h, 16 * (h // 2 + 1))
                for i in range(80 + LA):
                    if i < 80:
                        g, kt, masked, j = tiles[i]
                        tot = h * 80 + i
                        if tot >= LA:
                            tp = tot - LA
                            tensor.wait_ge(exq.h, 81 * (tp // 80) + (tp % 80) + 1)
                        tensor.matmul(psS[tot % 3][:],
                                      kb[:, kvh * 4096 + kt * 128: kvh * 4096 + (kt + 1) * 128],
                                      qhb[h % 2][:, g * 512:(g + 1) * 512],
                                      start=True, stop=True).then_inc(smm.h, 1)
                    if i >= LA:
                        g, kt, masked, j = tiles[i - LA]
                        tot = h * 80 + i - LA
                        gi = h * 4 + g
                        if masked:
                            tensor.wait_ge(mskp.h, mreal_of[(h, i - LA)] + 1)
                            tensor.wait_ge(mskd.h, mreal_of[(h, i - LA)] + 1)
                            src = ptm[mreal_of[(h, i - LA)] % 4]
                        else:
                            tensor.wait_ge(exq.h, 81 * h + min((i - LA) + 2, 81))
                            src = pt[tot % 4]
                        if kt == 0 and h >= 1:
                            tensor.wait_ge(oev.h, 5 * h)
                        av = tensor.matmul(psO[g][:],
                                           vb[:, (kvh * 32 + kt) * 128:(kvh * 32 + kt + 1) * 128],
                                           src[:], start=(kt == 0), stop=(kt == NKT[g] - 1))
                        if masked:
                            av.then_inc(avm.h, 1)
                        if kt == 0 and gi >= 1:
                            tensor.wait_ge(cse.h, gi)
                        ci = tensor.matmul(psC[0][:], one[:], src[:],
                                           start=(kt == 0), stop=(kt == NKT[g] - 1))
                        if kt == NKT[g] - 1:
                            ci.then_inc(avq.h, 1)

        @block.scalar
        def _(scalar):
            wn = [0]

            def wdummy():
                scalar.wait_ge(wse.h, wn[0])
                wn[0] += 1
                if DBG_NOPDUM:
                    return scalar.nop().then_inc(wse.h, 1)
                return scalar.activation(warm[:, 0:16], warm[:, 16:32], AF.Exp
                                         ).then_inc(wse.h, 1)

            scalar.memzero(warm[:]).then_inc(wse.h, 1)
            wn[0] = 1
            wdummy()
            wdummy()
            for h in range(NH):
                for i in range(80):
                    tot = h * 80 + i
                    scalar.wait_ge(smm.h, tot + 1)
                    if tot >= 4:
                        h2, i2 = divmod(tot - 4, 80)
                        if tiles[i2][2]:
                            scalar.wait_ge(mskp.h, mreal_of[(h2, i2)] + 1)
                    scalar.activation(pt[tot % 4][:], psS[tot % 3][:], AF.Exp
                                      ).then_inc(exq.h, 1)
                scalar.wait_ge(wse.h, 3)
                scalar.wait_ge(exq.h, 81 * h + 80)
                if DBG_NOPDUM:
                    scalar.nop().then_inc(exq.h, 1)
                else:
                    scalar.activation(warm[:, 0:16], warm[:, 16:32], AF.Exp
                                      ).then_inc(exq.h, 1)

        @block.vector
        def _(vector):
            for h in range(NH):
                for i, (g, kt, masked, j) in enumerate(tiles):
                    tot = h * 80 + i
                    if masked:
                        mr = mreal_of[(h, i)]
                        vector.wait_ge(exq.h, 81 * h + min(i + 2, 81))
                        if mr >= 4:
                            vector.wait_ge(avm.h, mr - 3)
                        vector.tensor_tensor(ptm[mr % 4][:], pt[tot % 4][:],
                                             msk[:, ((g % 2) * 8 + j) * 512:((g % 2) * 8 + j + 1) * 512],
                                             ALU.mult).then_inc(mskp.h, 1)
                        (vector.nop() if DBG_NOPDUM else vector.tensor_copy(dum2[0:1, 0:1], one[0:1, 0:1])).then_inc(mskd.h, 1)
                    if kt == NKT[g] - 1:
                        gi = h * 4 + g
                        vector.wait_ge(avq.h, gi + 1)
                        vector.tensor_copy(rs[:, g * 512:(g + 1) * 512],
                                           psC[0][:]).then_inc(cse.h, 1)
                vector.wait_ge(cse.h, 4 * (h + 1))
                if h >= 1:
                    vector.wait_ge(rw.h, 16 * h)
                vector.reciprocal(rcpr[:], rs[:]).then_inc(rcq.h, 1)
                (vector.nop() if DBG_NOPDUM else vector.tensor_copy(dum2[0:1, 0:1], one[0:1, 0:1])).then_inc(rcp.h, 1)
                if h >= 2:
                    vector.wait_ge(sto[h % 2].h, 16 * (h // 2))
                vector.wait_ge(rb.h, 16 * (h + 1))
                for g in range(4):
                    vector.tensor_tensor(ost[h % 2][:, g * 512:(g + 1) * 512],
                                         psO[g][:], recipb[h % 2][:, g * 512:(g + 1) * 512],
                                         ALU.mult).then_inc(oev.h, 1)
                (vector.nop() if DBG_NOPDUM else vector.tensor_copy(dum2[0:1, 0:1], one[0:1, 0:1])).then_inc(oev.h, 1)

        @block.sync
        def _(sync):
            for h in range(NH):
                sync.wait_ge(rcq.h, h + 1)
                sync.wait_ge(rcp.h, h + 1)
                sync.dma_start(out=scr[h % 4], in_=rcpr[:]).then_inc(rw.h, 16)
                sync.wait_ge(rw.h, 16 * (h + 1))
                if h >= 2:
                    sync.wait_ge(oev.h, 5 * (h - 1))
                sync.dma_start(out=recipb[h % 2][:], in_=scr[h % 4].partition_broadcast(128)
                               ).then_inc(rb.h, 16)
                sync.wait_ge(oev.h, 5 * (h + 1))
                sync.dma_start(out=Oo.rearrange("(m p) t -> p m t", p=128)[:, h],
                               in_=ost[h % 2][:]).then_inc(sto[h % 2].h, 16)
            sync.wait_ge(sto[0].h, 16 * ((NH + 1) // 2))
            sync.wait_ge(sto[1].h, 16 * (NH // 2))

    return nc




DBG_NSL = 4


def build_l0_tail():
    nc = bass.Bass("TRN2", target_bir_lowering=False, debug=False)
    O = nc.dram_tensor("O", [4096, 2048], BF16, kind="ExternalInput").ap()
    WOB = nc.dram_tensor("WOB", [16, 32, 128, 128], F32, kind="ExternalInput").ap()
    UT = nc.dram_tensor("UT", [D, 2048], F32, kind="ExternalInput").ap()
    GWB = nc.dram_tensor("GWB", [44, 16, 128, 128], F32, kind="ExternalInput").ap()
    UWB = nc.dram_tensor("UWB", [44, 16, 128, 128], F32, kind="ExternalInput").ap()
    DWB = nc.dram_tensor("DWB", [16, 44, 128, 128], F32, kind="ExternalInput").ap()
    GS = nc.dram_tensor("GS", [128, 44], F32, kind="ExternalInput").ap()
    US = nc.dram_tensor("US", [128, 44], F32, kind="ExternalInput").ap()
    DS = nc.dram_tensor("DS", [128, 16], F32, kind="ExternalInput").ap()
    ONE = nc.dram_tensor("ONE", [128, 1], BF16, kind="ExternalInput").ap()
    X2 = nc.dram_tensor("X2", [D, 2048], F32, kind="ExternalOutput").ap()
    X1F = nc.dram_tensor("X1F", [D, 2048], F32, kind="Internal").ap()
    scr = nc.dram_tensor("scr", [8, 1024], F32, kind="Internal").ap()

    NSL = DBG_NSL

    onb = nc.alloc_sbuf_tensor("onb", [128, 32 * 512], BF16)
    wob = [nc.alloc_sbuf_tensor("wob0", [128, 32 * 128], BF16)] * 2
    y1raw = nc.alloc_sbuf_tensor("y1raw", [128, 16 * 512], F32)
    x1bn = nc.alloc_sbuf_tensor("x1bn", [128, 16 * 512], BF16)
    gwb = [nc.alloc_sbuf_tensor(f"gwb{i}", [128, 16 * 128], BF16) for i in range(2)]
    uwb = [nc.alloc_sbuf_tensor(f"uwb{i}", [128, 16 * 128], BF16) for i in range(2)]
    dwb = [nc.alloc_sbuf_tensor("dwb0", [128, 44 * 128], BF16)] * 2
    hdn = nc.alloc_sbuf_tensor("hdn", [128, 44 * 512], BF16)
    sumb = x1bn
    sqb = hdn
    gm = nc.alloc_sbuf_tensor("gm", [128, 512], BF16)
    um = nc.alloc_sbuf_tensor("um", [128, 512], BF16)
    x1fn = nc.alloc_sbuf_tensor("x1fn", [128, 16 * 512], F32)
    utn = x1fn
    y2raw = y1raw
    gs = nc.alloc_sbuf_tensor("gs", [128, 44], F32)
    us = nc.alloc_sbuf_tensor("us", [128, 44], F32)
    ds = nc.alloc_sbuf_tensor("ds", [128, 16], F32)
    one = nc.alloc_sbuf_tensor("one", [128, 1], BF16)
    mb = nc.alloc_sbuf_tensor("mb", [128, 1024], F32)
    srow = x1fn[0:1, 0:1024]
    tmps = onb[:, 0:1024].bitcast(F32)
    tmp5 = mb[0:1, 0:512]
    tmp6 = mb[0:1, 512:1024]
    warm = nc.alloc_sbuf_tensor("warm", [1, 32], F32)
    dum2 = nc.alloc_sbuf_tensor("dum2", [1, 8], F32)
    psA = nc.alloc_psum_tensor("psA", [128, 512], F32)
    psB = nc.alloc_psum_tensor("psB", [128, 512], F32)
    psM = nc.alloc_psum_tensor("psM", [1, 512], F32)
    psQ = nc.alloc_psum_tensor("psQ", [1, 512], F32)

    lres = Sem(nc, "lres")
    lon = Sem(nc, "lon"); lut = Sem(nc, "lut"); lx1 = Sem(nc, "lx1")
    lwm = [Sem(nc, "lwm0"), Sem(nc, "lwm1")]
    lgw = [Sem(nc, "lgw0"), Sem(nc, "lgw1")]
    ldw = [Sem(nc, "ldw0"), Sem(nc, "ldw1")]
    mmA = Sem(nc, "mmA"); mmB = Sem(nc, "mmB"); mmC = Sem(nc, "mmC"); mmS = Sem(nc, "mmS")
    evA = Sem(nc, "evA"); evAd = Sem(nc, "evAd")
    seB = Sem(nc, "seB"); evB = Sem(nc, "evB"); evBd = Sem(nc, "evBd")
    evC = Sem(nc, "evC"); evCd = Sem(nc, "evCd")
    cst = Sem(nc, "cst"); cstd = Sem(nc, "cstd"); vch = Sem(nc, "vch")
    sqq = Sem(nc, "sqq"); wse = Sem(nc, "wse")
    stq = Sem(nc, "stq"); rwq = Sem(nc, "rwq"); rbq = Sem(nc, "rbq")
    apq = Sem(nc, "apq"); apqd = Sem(nc, "apqd")
    stX1 = Sem(nc, "stX1"); stX2 = Sem(nc, "stX2")

    with nc.Block() as block:
        @block.gpsimd
        def _(gpsimd):
            gpsimd.dma_start(out=gs[:], in_=GS).then_inc(lres.h, 16)
            gpsimd.dma_start(out=us[:], in_=US).then_inc(lres.h, 16)
            gpsimd.dma_start(out=ds[:], in_=DS).then_inc(lres.h, 16)
            gpsimd.dma_start(out=one[:], in_=ONE).then_inc(lres.h, 16)
            for n in range(NSL):
                if n >= 1:
                    gpsimd.wait_ge(stX2.h, 16 * n)
                gpsimd.dma_start(out=onb[:].rearrange("p (k t) -> p k t", k=32),
                                 in_=O.rearrange("(k p) t -> p k t", p=128)[:, :, n * 512:(n + 1) * 512]
                                 ).then_inc(lon.h, 16)
                gpsimd.dma_start(out=utn[:].rearrange("p (k t) -> p k t", k=16),
                                 in_=UT.rearrange("(k p) t -> p k t", p=128)[:, :, n * 512:(n + 1) * 512]
                                 ).then_inc(lut.h, 16)
                for m in range(16):
                    if n * 16 + m >= 1:
                        gpsimd.wait_ge(mmA.h, 16 * n + m)
                    gpsimd.dma_start(out=wob[0][:].rearrange("p (k f) -> p k f", k=32),
                                     in_=WOB[m].rearrange("k p f -> p k f")
                                     ).then_inc(lwm[0].h, 16)
                for m in range(44):
                    if n * 44 + m >= 2:
                        gpsimd.wait_ge(mmB.h, 2 * (44 * n + m - 1))
                    gpsimd.dma_start(out=gwb[m % 2][:].rearrange("p (k f) -> p k f", k=16),
                                     in_=GWB[m].rearrange("k p f -> p k f")
                                     ).then_inc(lgw[m % 2].h, 16)
                    gpsimd.dma_start(out=uwb[m % 2][:].rearrange("p (k f) -> p k f", k=16),
                                     in_=UWB[m].rearrange("k p f -> p k f")
                                     ).then_inc(lgw[m % 2].h, 16)
                gpsimd.wait_ge(stX1.h, 16 * (n + 1))
                gpsimd.dma_start(out=x1fn[:].rearrange("p (k t) -> p k t", k=16),
                                 in_=X1F.rearrange("(k p) t -> p k t", p=128)[:, :, n * 512:(n + 1) * 512]
                                 ).then_inc(lx1.h, 16)
                for m in range(16):
                    if n * 16 + m >= 1:
                        gpsimd.wait_ge(mmC.h, 16 * n + m)
                    gpsimd.dma_start(out=dwb[0][:].rearrange("p (k f) -> p k f", k=44),
                                     in_=DWB[m].rearrange("k p f -> p k f")
                                     ).then_inc(ldw[0].h, 16)

        @block.tensor
        def _(tensor):
            tensor.wait_ge(lres.h, 64)
            for n in range(NSL):
                tensor.wait_ge(lon.h, 16 * (n + 1))
                for m in range(16):
                    tensor.wait_ge(lwm[0].h, 16 * (16 * n + m + 1))
                    if m >= 2:
                        tensor.wait_ge(evA.h, 16 * n + m - 1)
                    inst = None
                    for k in range(32):
                        ps = psA if m % 2 == 0 else psB
                        inst = tensor.matmul(ps[:], wob[m % 2][:, k * 128:(k + 1) * 128],
                                             onb[:, k * 512:(k + 1) * 512],
                                             start=(k == 0), stop=(k == 31))
                    inst.then_inc(mmA.h, 1)
                ln = 2 * n
                tensor.wait_ge(cst.h, ln + 1)
                tensor.wait_ge(cstd.h, ln + 1)
                tensor.wait_ge(sqq.h, ln + 1)
                tensor.wait_ge(wse.h, 3 + 2 * ln + 1)
                inst = None
                for k in range(16):
                    inst = tensor.matmul(psM[:], one[:], sumb[:, k * 512:(k + 1) * 512],
                                         start=(k == 0), stop=(k == 15))
                inst.then_inc(mmS.h, 1)
                inst = None
                for k in range(16):
                    inst = tensor.matmul(psQ[:], one[:], sqb[:, k * 512:(k + 1) * 512],
                                         start=(k == 0), stop=(k == 15))
                inst.then_inc(mmS.h, 1)
                tensor.wait_ge(apq.h, 16 * (ln + 1))
                tensor.wait_ge(apqd.h, ln + 1)
                for m in range(44):
                    tensor.wait_ge(lgw[m % 2].h, 32 * ((44 * n + m) // 2 + 1))
                    if m >= 1:
                        tensor.wait_ge(seB.h, 44 * n + m)
                    if m >= 1:
                        tensor.wait_ge(evB.h, 2 * (44 * n + m) - 1)
                    inst = None
                    for k in range(16):
                        inst = tensor.matmul(psA[:], gwb[m % 2][:, k * 128:(k + 1) * 128],
                                             x1bn[:, k * 512:(k + 1) * 512],
                                             start=(k == 0), stop=(k == 15))
                    inst.then_inc(mmB.h, 1)
                    inst = None
                    for k in range(16):
                        inst = tensor.matmul(psB[:], uwb[m % 2][:, k * 128:(k + 1) * 128],
                                             x1bn[:, k * 512:(k + 1) * 512],
                                             start=(k == 0), stop=(k == 15))
                    inst.then_inc(mmB.h, 1)
                tensor.wait_ge(evB.h, 88 * n + 88)
                tensor.wait_ge(evBd.h, n + 1)
                for m in range(16):
                    tensor.wait_ge(ldw[0].h, 16 * (16 * n + m + 1))
                    if m >= 2:
                        tensor.wait_ge(evC.h, 16 * n + m - 1)
                    inst = None
                    for k in range(44):
                        ps = psA if m % 2 == 0 else psB
                        inst = tensor.matmul(ps[:], dwb[m % 2][:, k * 128:(k + 1) * 128],
                                             hdn[:, k * 512:(k + 1) * 512],
                                             start=(k == 0), stop=(k == 43))
                    inst.then_inc(mmC.h, 1)
                ln = 2 * n + 1
                tensor.wait_ge(cst.h, ln + 1)
                tensor.wait_ge(cstd.h, ln + 1)
                tensor.wait_ge(sqq.h, ln + 1)
                tensor.wait_ge(wse.h, 3 + 2 * ln + 1)
                inst = None
                for k in range(16):
                    inst = tensor.matmul(psM[:], one[:], sumb[:, k * 512:(k + 1) * 512],
                                         start=(k == 0), stop=(k == 15))
                inst.then_inc(mmS.h, 1)
                inst = None
                for k in range(16):
                    inst = tensor.matmul(psQ[:], one[:], sqb[:, k * 512:(k + 1) * 512],
                                         start=(k == 0), stop=(k == 15))
                inst.then_inc(mmS.h, 1)

        @block.scalar
        def _(scalar):
            wn = [0]

            def wdummy():
                scalar.wait_ge(wse.h, wn[0])
                wn[0] += 1
                return scalar.activation(warm[:, 0:16], warm[:, 16:32], AF.Exp
                                         ).then_inc(wse.h, 1)

            scalar.memzero(warm[:]).then_inc(wse.h, 1)
            wn[0] = 1
            wdummy()
            wdummy()
            scalar.wait_ge(lres.h, 64)
            for n in range(NSL):
                for ph in range(2):
                    ln = 2 * n + ph
                    if ph == 1:
                        for m in range(44):
                            scalar.wait_ge(mmB.h, 2 * (44 * n + m) + 1)
                            scalar.wait_ge(evB.h, 2 * (44 * n + m))
                            scalar.activation(gm[:], psA[:], AF.Silu, scale=gs[:, m:m + 1]
                                              ).then_inc(seB.h, 1)
                    src_ = y1raw if ph == 0 else y2raw
                    if ph == 0:
                        scalar.wait_ge(evA.h, 16 * (n + 1))
                        scalar.wait_ge(evAd.h, n + 1)
                    else:
                        scalar.wait_ge(evC.h, 16 * (n + 1))
                        scalar.wait_ge(evCd.h, n + 1)
                    scalar.activation(sqb[:, 0:8192], src_[:], AF.Square).then_inc(sqq.h, 1)
                    wdummy()
                    scalar.wait_ge(stq.h, 5 * ln + 2)
                    scalar.activation(tmp6, tmp5, AF.Sqrt).then_inc(stq.h, 1)
                    wdummy()

        @block.vector
        def _(vector):
            vn = [0]

            def vdummy():
                if DBG_NOPDUM:
                    return vector.nop()
                return vector.tensor_copy(dum2[0:1, 0:1], one[0:1, 0:1])

            vector.wait_ge(lres.h, 64)
            for n in range(NSL):
                for ph in range(2):
                    ln = 2 * n + ph
                    if ph == 0:
                        vector.wait_ge(lut.h, 16 * (n + 1))
                        for m in range(16):
                            vector.wait_ge(mmA.h, 16 * n + m + 1)
                            ps = psA if m % 2 == 0 else psB
                            vector.tensor_add(y1raw[:, m * 512:(m + 1) * 512], ps[:],
                                              utn[:, m * 512:(m + 1) * 512]).then_inc(evA.h, 1)
                        vdummy().then_inc(evAd.h, 1)
                        src_ = y1raw
                    else:
                        for m in range(44):
                            vector.wait_ge(mmB.h, 2 * (44 * n + m) + 2)
                            vector.wait_ge(evB.h, 2 * (44 * n + m))
                            vector.tensor_scalar(um[:], psB[:], us[:, m:m + 1], None, ALU.mult
                                                 ).then_inc(evB.h, 1)
                            vector.wait_ge(seB.h, 44 * n + m + 1)
                            vector.wait_ge(evB.h, 2 * (44 * n + m) + 1)
                            vector.tensor_tensor(hdn[:, m * 512:(m + 1) * 512], gm[:], um[:],
                                                 ALU.mult).then_inc(evB.h, 1)
                        vdummy().then_inc(evBd.h, 1)
                        vector.wait_ge(lx1.h, 16 * (n + 1))
                        for m in range(16):
                            vector.wait_ge(mmC.h, 16 * n + m + 1)
                            ps = psA if m % 2 == 0 else psB
                            vector.scalar_tensor_tensor(y2raw[:, m * 512:(m + 1) * 512], ps[:],
                                                        ds[:, m:m + 1], x1fn[:, m * 512:(m + 1) * 512],
                                                        ALU.mult, ALU.add).then_inc(evC.h, 1)
                        vdummy().then_inc(evCd.h, 1)
                        src_ = y2raw
                    if ph == 0:
                        vector.wait_ge(evA.h, 16 * (n + 1))
                    else:
                        vector.wait_ge(evC.h, 16 * (n + 1))
                    vector.tensor_copy(sumb[:], src_[:]).then_inc(cst.h, 1)
                    vdummy().then_inc(cstd.h, 1)
                    vector.wait_ge(mmS.h, 2 * ln + 2)
                    vector.tensor_scalar(srow[:, 0:512], psM[:], 1.0 / 2048.0, None, ALU.mult
                                         ).then_inc(vch.h, 1); vn[0] += 1
                    s1 = vn[0]
                    vector.tensor_scalar(tmp5, psQ[:], 1.0 / 2048.0, None, ALU.mult
                                         ).then_inc(vch.h, 1); vn[0] += 1
                    vector.wait_ge(vch.h, s1)
                    vector.tensor_tensor(tmp6, srow[:, 0:512], srow[:, 0:512], ALU.mult
                                         ).then_inc(vch.h, 1); vn[0] += 1
                    vector.wait_ge(vch.h, vn[0])
                    vector.tensor_sub(tmp5, tmp5, tmp6).then_inc(vch.h, 1); vn[0] += 1
                    vector.wait_ge(vch.h, vn[0])
                    vector.tensor_scalar(tmp5, tmp5, 1e-5, None, ALU.add).then_inc(stq.h, 1)
                    vdummy().then_inc(stq.h, 1)
                    vector.wait_ge(stq.h, 5 * ln + 3)
                    vector.wait_ge(wse.h, 3 + 2 * ln + 2)
                    vector.reciprocal(srow[:, 512:1024], tmp6).then_inc(stq.h, 1)
                    vdummy().then_inc(stq.h, 1)
                    vector.wait_ge(rbq.h, 16 * (ln + 1))
                    for m in range(16):
                        vector.wait_ge(vch.h, vn[0])
                        if ph == 1 and m >= 1:
                            vector.wait_ge(apq.h, 16 * ln + m)
                        vector.tensor_sub(tmps, src_[:, m * 512:(m + 1) * 512], mb[:, 0:512]
                                          ).then_inc(vch.h, 1); vn[0] += 1
                        vector.wait_ge(vch.h, vn[0])
                        if ph == 0:
                            vector.tensor_mul(y1raw[:, m * 512:(m + 1) * 512], tmps, mb[:, 512:1024]
                                              ).then_inc(vch.h, 1); vn[0] += 1
                            vector.wait_ge(vch.h, vn[0])
                            vector.tensor_copy(x1bn[:, m * 512:(m + 1) * 512],
                                               y1raw[:, m * 512:(m + 1) * 512]).then_inc(apq.h, 1)
                        else:
                            vector.tensor_mul(y2raw[:, m * 512:(m + 1) * 512], tmps,
                                              mb[:, 512:1024]).then_inc(apq.h, 1)
                    vdummy().then_inc(apqd.h, 1)

        @block.sync
        def _(sync):
            for n in range(NSL):
                for ph in range(2):
                    ln = 2 * n + ph
                    sync.wait_ge(stq.h, 5 * (ln + 1))
                    sync.dma_start(out=scr[ln % 8], in_=srow[:]).then_inc(rwq.h, 16)
                    sync.wait_ge(rwq.h, 16 * (ln + 1))
                    sync.dma_start(out=mb[:], in_=scr[ln % 8].partition_broadcast(128)
                                   ).then_inc(rbq.h, 16)
                    sync.wait_ge(apq.h, 16 * (ln + 1))
                    sync.wait_ge(apqd.h, ln + 1)
                    if ph == 0:
                        sync.dma_start(
                            out=X1F.rearrange("(k p) t -> p k t", p=128)[:, :, n * 512:(n + 1) * 512],
                            in_=y1raw[:].rearrange("p (k t) -> p k t", k=16)).then_inc(stX1.h, 16)
                    else:
                        sync.dma_start(
                            out=X2.rearrange("(k p) t -> p k t", p=128)[:, :, n * 512:(n + 1) * 512],
                            in_=y2raw[:].rearrange("p (k t) -> p k t", k=16)).then_inc(stX2.h, 16)
            sync.wait_ge(stX2.h, 16 * NSL)

    return nc


def host_masks(role):
    r = np.arange(128)[:, None]
    c = np.arange(512)[None, :]
    M = [(c >= 128 * v + r).astype(np.float32) for v in range(4)]
    ones = np.ones((128, 512), dtype=np.float32)
    zeros = np.zeros((128, 512), dtype=np.float32)
    larger = [ones] * 4 + M
    smaller = M + [zeros] * 4
    out = []
    for par in (0, 1):
        q = QUADS[role][par]
        out += larger if q == 2 * par + 1 else smaller
    return np.concatenate(out, axis=1).astype(ml_dtypes.bfloat16)


def run_l0_proj(inputs):
    u = inputs["L0_attn_u"]
    W_in = inputs["L0_attn_W_in"]
    WqT = np.ascontiguousarray(W_in[6144:10240].T)
    WkT = np.ascontiguousarray(W_in[5120:6144].T)
    WvT = np.ascontiguousarray(W_in[4096:5120].T)
    cos, sin = rope_host_tables()
    s = np.float32(1.0 / np.sqrt(128.0))
    PIT = np.ascontiguousarray(pi_matrix().T)
    in_maps = []
    for c in range(8):
        b, r = divmod(c, 2)
        st = slot_tokens(r)
        uT = np.ascontiguousarray(u[b][st].T)
        ROPE = np.concatenate([
            np.ascontiguousarray((cos[st] * s).T),
            np.ascontiguousarray((sin[st] * s).T),
            np.ascontiguousarray(cos[st].T),
            np.ascontiguousarray(sin[st].T)], axis=1).astype(np.float32)
        in_maps.append({"uT": uT, "WqT": WqT, "WkT": WkT, "WvT": WvT,
                        "ROPE": np.ascontiguousarray(ROPE), "PIT": PIT})
    nc = build_l0_proj()
    res = run_bass_kernel_spmd(nc, in_maps, core_ids=list(range(8)))
    return res.results


def run_l0_attn(inputs, proj):
    in_maps = []
    ONE = np.ones((128, 1), dtype=ml_dtypes.bfloat16)
    for c in range(8):
        b, r = divmod(c, 2)
        p0, p1 = 2 * b, 2 * b + 1
        Knat = np.zeros((1024, 4096), dtype=ml_dtypes.bfloat16)
        Vnat = np.zeros((1024, 4096), dtype=ml_dtypes.bfloat16)
        Knat[:, slot_tokens(0)] = proj[p0]["Ko"]
        Knat[:, slot_tokens(1)] = proj[p1]["Ko"]
        Vnat[:, slot_tokens(0)] = proj[p0]["Vo"]
        Vnat[:, slot_tokens(1)] = proj[p1]["Vo"]
        Vt = np.ascontiguousarray(
            Vnat.reshape(8, 128, 32, 128).transpose(3, 0, 2, 1).reshape(128, 8 * 32 * 128))
        in_maps.append({"Q": proj[c]["Qo"], "K": np.ascontiguousarray(Knat),
                        "Vt": Vt, "MSK": host_masks(r), "ONE": ONE})
    nc = build_l0_attn()
    res = run_bass_kernel_spmd(nc, in_maps, core_ids=list(range(8)))
    return res.results




def run_l0_tail(inputs, att):
    W_out = inputs["L0_attn_W_out"]
    WoutT = np.ascontiguousarray(W_out.T)
    WOB = np.ascontiguousarray(WoutT.reshape(32, 128, 16, 128).transpose(2, 0, 1, 3))
    gwT = inputs["L0_ffn_gate_w"].T
    uwT = inputs["L0_ffn_up_w"].T
    dwT = inputs["L0_ffn_down_w"].T
    GWB = np.ascontiguousarray(gwT.reshape(16, 128, 44, 128).transpose(2, 0, 1, 3))
    UWB = np.ascontiguousarray(uwT.reshape(16, 128, 44, 128).transpose(2, 0, 1, 3))
    DWB = np.ascontiguousarray(dwT.reshape(44, 128, 16, 128).transpose(2, 0, 1, 3))
    GS = np.ascontiguousarray(inputs["L0_ffn_gate_s"].reshape(44, 128).T)
    US = np.ascontiguousarray(inputs["L0_ffn_up_s"].reshape(44, 128).T)
    DS = np.ascontiguousarray(inputs["L0_ffn_down_s"].reshape(16, 128).T)
    ONE = np.ones((128, 1), dtype=ml_dtypes.bfloat16)
    u = inputs["L0_attn_u"]
    in_maps = []
    for c in range(8):
        b, r = divmod(c, 2)
        st = slot_tokens(r)
        uT = np.ascontiguousarray(u[b][st].T)
        in_maps.append({"O": att[c]["Oo"], "WOB": WOB, "UT": uT, "GWB": GWB,
                        "UWB": UWB, "DWB": DWB, "GS": GS, "US": US, "DS": DS,
                        "ONE": ONE})
    nc = build_l0_tail()
    res = run_bass_kernel_spmd(nc, in_maps, core_ids=list(range(8)))
    return res.results





def build_quant():
    nc = bass.Bass("TRN2", target_bir_lowering=False, debug=False)
    WGU = nc.dram_tensor("WGU", [2048, 2048], F32, kind="ExternalInput").ap()
    WP = nc.dram_tensor("WP", [256, 8192], F32, kind="ExternalInput").ap()
    QGU = nc.dram_tensor("QGU", [2048, 2048], BF16, kind="ExternalOutput").ap()
    QP = nc.dram_tensor("QP", [256, 8192], BF16, kind="ExternalOutput").ap()

    FMAX = 8192
    wb = nc.alloc_sbuf_tensor("wb", [128, FMAX], BF16)
    aw = nc.alloc_sbuf_tensor("aw", [128, FMAX], F32)
    qv = nc.alloc_sbuf_tensor("qv", [128, FMAX], F32)
    tb = nc.alloc_sbuf_tensor("tb", [128, FMAX], F32)
    rv = nc.alloc_sbuf_tensor("rv", [128, FMAX], BF16)
    whb = nc.alloc_sbuf_tensor("whb", [128, FMAX], BF16)
    outb = nc.alloc_sbuf_tensor("outb", [128, FMAX], BF16)
    gsum = nc.alloc_sbuf_tensor("gsum", [128, FMAX // 64], F32)
    scb = nc.alloc_sbuf_tensor("scb", [128, FMAX // 64], BF16)
    scf = nc.alloc_sbuf_tensor("scf", [128, FMAX // 64], F32)

    lq = Sem(nc, "lq"); qch = Sem(nc, "qch"); stq = Sem(nc, "stq")

    tiles = [("WGU", i, 2048) for i in range(16)] + [("WP", i, 8192) for i in range(2)]

    with nc.Block() as block:
        @block.gpsimd
        def _(gpsimd):
            for ti, (src, i, F) in enumerate(tiles):
                if ti >= 1:
                    gpsimd.wait_ge(stq.h, 16 * ti)
                d = WGU if src == "WGU" else WP
                gpsimd.dma_start(out=wb[:, 0:F], in_=d[i * 128:(i + 1) * 128, :]
                                 ).then_inc(lq.h, 16)

        @block.vector
        def _(vector):
            qn = [0]

            def op(inst):
                inst.then_inc(qch.h, 1)
                qn[0] += 1

            for ti, (src, i, F) in enumerate(tiles):
                G = F // 64
                vector.wait_ge(lq.h, 16 * (ti + 1))
                op(vector.tensor_scalar(tb[:, 0:F], wb[:, 0:F], -1.0, None, ALU.mult))
                vector.wait_ge(qch.h, qn[0])
                op(vector.tensor_tensor(aw[:, 0:F], wb[:, 0:F], tb[:, 0:F], ALU.max))
                vector.wait_ge(qch.h, qn[0])
                op(vector.reduce_sum(gsum[:, 0:G].unsqueeze(2),
                                     aw[:, 0:F].rearrange("p (g e) -> p g e", e=64),
                                     axis=AX.X))
                vector.wait_ge(qch.h, qn[0])
                op(vector.tensor_scalar(scf[:, 0:G], gsum[:, 0:G], 1.0 / 64.0, None, ALU.mult))
                vector.wait_ge(qch.h, qn[0])
                op(vector.tensor_scalar(scf[:, 0:G], scf[:, 0:G], 1e-8, None, ALU.max))
                vector.wait_ge(qch.h, qn[0])
                op(vector.tensor_copy(scb[:, 0:G], scf[:, 0:G]))
                vector.wait_ge(qch.h, qn[0])
                op(vector.tensor_copy(scf[:, 0:G], scb[:, 0:G]))
                vector.wait_ge(qch.h, qn[0])
                op(vector.reciprocal(gsum[:, 0:G], scf[:, 0:G]))
                vector.wait_ge(qch.h, qn[0])
                op(vector.tensor_tensor(rv[:, 0:F].rearrange("p (g e) -> p g e", e=64),
                                        wb[:, 0:F].rearrange("p (g e) -> p g e", e=64),
                                        gsum[:, 0:G].unsqueeze(2).to_broadcast([128, G, 64]),
                                        ALU.mult))
                vector.wait_ge(qch.h, qn[0])
                op(vector.tensor_scalar(qv[:, 0:F], rv[:, 0:F], 0.5, None, ALU.is_gt))
                vector.wait_ge(qch.h, qn[0])
                op(vector.tensor_scalar(tb[:, 0:F], rv[:, 0:F], -0.5, None, ALU.is_lt))
                vector.wait_ge(qch.h, qn[0])
                op(vector.tensor_sub(qv[:, 0:F], qv[:, 0:F], tb[:, 0:F]))
                vector.wait_ge(qch.h, qn[0])
                op(vector.tensor_tensor(whb[:, 0:F].rearrange("p (g e) -> p g e", e=64),
                                        qv[:, 0:F].rearrange("p (g e) -> p g e", e=64),
                                        scf[:, 0:G].unsqueeze(2).to_broadcast([128, G, 64]),
                                        ALU.mult))
                vector.wait_ge(qch.h, qn[0])
                op(vector.tensor_sub(whb[:, 0:F], whb[:, 0:F], wb[:, 0:F]))
                vector.wait_ge(qch.h, qn[0])
                op(vector.tensor_add(outb[:, 0:F], wb[:, 0:F], whb[:, 0:F]))
                vector.wait_ge(qch.h, qn[0])
                op(vector.tensor_copy(wb[:, 0:F], wb[:, 0:F]))

        @block.sync
        def _(sync):
            nops = 16
            for ti, (src, i, F) in enumerate(tiles):
                sync.wait_ge(qch.h, nops * (ti + 1))
                d = QGU if src == "WGU" else QP
                sync.dma_start(out=d[i * 128:(i + 1) * 128, :], in_=outb[:, 0:F]
                               ).then_inc(stq.h, 16)
            sync.wait_ge(stq.h, 16 * len(tiles))

    return nc


def run_quant(inputs):
    wgu = inputs["L1_ffn_w_gate_up"]
    wp = inputs["L1_ffn_w_proj"]
    in_maps = []
    for c in range(8):
        in_maps.append({"WGU": np.ascontiguousarray(wgu[c * 2048:(c + 1) * 2048]),
                        "WP": np.ascontiguousarray(wp[c * 256:(c + 1) * 256])})
    nc = build_quant()
    res = run_bass_kernel_spmd(nc, in_maps, core_ids=list(range(8)))
    qgu = np.concatenate([res.results[c]["QGU"] for c in range(8)], axis=0)
    qp = np.concatenate([res.results[c]["QP"] for c in range(8)], axis=0)
    return qgu, qp




def build_l1_qkv():
    rec = {}
    for pas in range(2):
        nc = bass.Bass("TRN2", target_bir_lowering=False, debug=False)
        X2B = nc.dram_tensor("X2B", [D, 4096], F32, kind="ExternalInput").ap()
        WQ3 = nc.dram_tensor("WQ3", [D, 3 * 1024], F32, kind="ExternalInput").ap()
        WB8 = nc.dram_tensor("WB8", [D, 8], F32, kind="ExternalInput").ap()
        CONV = nc.dram_tensor("CONV", [128, 12], F32, kind="ExternalInput").ap()
        ONE = nc.dram_tensor("ONE", [128, 1], BF16, kind="ExternalInput").ap()
        QN = nc.dram_tensor("QN", [1024, 4096], BF16, kind="ExternalOutput").ap()
        KN = nc.dram_tensor("KN", [1024, 4096], BF16, kind="ExternalOutput").ap()
        KB = nc.dram_tensor("KB", [1024, 4096], BF16, kind="ExternalOutput").ap()
        VB = nc.dram_tensor("VB", [1024, 4096], BF16, kind="ExternalOutput").ap()
        rscr = nc.dram_tensor("rscr", [2, 4096], F32, kind="Internal").ap()
        bscr = nc.dram_tensor("bscr", [8, 4096], F32, kind="Internal").ap()

        xn = [nc.alloc_sbuf_tensor(f"xn{i}", [128, 16 * 512], BF16) for i in range(2)]
        w3 = nc.alloc_sbuf_tensor("w3", [128, 16 * 128], BF16)
        wb8 = nc.alloc_sbuf_tensor("wb8", [128, 16 * 8], BF16)
        cvw = nc.alloc_sbuf_tensor("cvw", [128, 12], F32)
        one = nc.alloc_sbuf_tensor("one", [128, 1], BF16)
        cbuf = nc.alloc_sbuf_tensor("cbuf", [128, 515], F32)
        c1 = nc.alloc_sbuf_tensor("c1", [128, 512], F32)
        c2 = nc.alloc_sbuf_tensor("c2", [128, 512], F32)
        tail = nc.alloc_sbuf_tensor("tail", [128, 3], F32)
        sraw = nc.alloc_sbuf_tensor("sraw", [128, 4096], F32)
        sqb = nc.alloc_sbuf_tensor("sqb", [128, 512], BF16)
        beta = nc.alloc_sbuf_tensor("beta", [8, 4096], F32)
        rsr = nc.alloc_sbuf_tensor("rsr", [1, 4096], F32)
        rs2 = nc.alloc_sbuf_tensor("rs2", [1, 4096], F32)
        brow = nc.alloc_sbuf_tensor("brow", [1, 4096], F32)
        bc1 = nc.alloc_sbuf_tensor("bc1", [128, 4096], F32)
        bc2 = nc.alloc_sbuf_tensor("bc2", [128, 4096], F32)
        obuf = nc.alloc_sbuf_tensor("obuf", [128, 2 * 4096], BF16)
        warm = nc.alloc_sbuf_tensor("warm", [1, 32], F32)
        dum2 = nc.alloc_sbuf_tensor("dum2", [1, 8], F32)
        psA = nc.alloc_psum_tensor("psA", [128, 512], F32)
        psB = nc.alloc_psum_tensor("psB", [128, 512], F32)
        psM = nc.alloc_psum_tensor("psM", [1, 512], F32)

        lx = [Sem(nc, "lx0"), Sem(nc, "lx1")]
        lw = Sem(nc, "lw"); lcst = Sem(nc, "lcst")
        mmP = Sem(nc, "mmP"); mmS = Sem(nc, "mmS")
        vch = Sem(nc, "vch"); cnvd = Sem(nc, "cnvd")
        sil = Sem(nc, "sil"); sqe = Sem(nc, "sqe"); sqr = Sem(nc, "sqr")
        wse = Sem(nc, "wse"); sse = Sem(nc, "sse")
        rwq = Sem(nc, "rwq"); rbq = Sem(nc, "rbq"); stq = Sem(nc, "stq")
        bst = Sem(nc, "bst"); bld = Sem(nc, "bld")

        NM = 24
        mkind = lambda m: ("q" if m < 8 else ("k" if m < 16 else "v"))
        R = (lambda k, d=0: rec.get(k, d))

        with nc.Block() as block:
            @block.vector
            def _(vector):
                vn = [0]

                def op(inst):
                    inst.then_inc(vch.h, 1)
                    vn[0] += 1
                    return vn[0]

                cnt_cnvd = [0]

                def vdummy():
                    vector.tensor_copy(dum2[0:1, 0:1], one[0:1, 0:1]).then_inc(cnvd.h, 1)
                    cnt_cnvd[0] += 1
                    return cnt_cnvd[0]

                vector.wait_ge(lcst.h, 48)
                nsq = 0
                for m in range(NM):
                    knd = mkind(m)
                    for n in range(8):
                        if n == 0:
                            vector.wait_ge(vch.h, vn[0])
                            op(vector.memset(tail[:], 0.0))
                        vector.wait_ge(vch.h, vn[0])
                        op(vector.tensor_copy(cbuf[:, 0:3], tail[:]))
                        vector.wait_ge(mmP.h, 16 * (8 + m * 8 + n + 1))
                        vector.wait_ge(vch.h, vn[0])
                        ps = psA if n % 2 == 0 else psB
                        rec[("evv", m, n)] = op(vector.tensor_copy(cbuf[:, 3:515], ps[:]))
                        vector.wait_ge(vch.h, vn[0])
                        op(vector.tensor_copy(tail[:], cbuf[:, 512:515]))
                        cw = 4 * (0 if knd == "q" else (1 if knd == "k" else 2))
                        vector.wait_ge(vch.h, vn[0])
                        op(vector.tensor_scalar(c1[:], cbuf[:, 3:515], cvw[:, cw + 3:cw + 4], None, ALU.mult))
                        vector.wait_ge(vch.h, vn[0])
                        op(vector.scalar_tensor_tensor(c2[:], cbuf[:, 2:514], cvw[:, cw + 2:cw + 3], c1[:], ALU.mult, ALU.add))
                        vector.wait_ge(vch.h, vn[0])
                        op(vector.scalar_tensor_tensor(c1[:], cbuf[:, 1:513], cvw[:, cw + 1:cw + 2], c2[:], ALU.mult, ALU.add))
                        vector.wait_ge(vch.h, vn[0])
                        rec[("cnv", m, n)] = op(vector.scalar_tensor_tensor(c2[:], cbuf[:, 0:512], cvw[:, cw + 0:cw + 1], c1[:], ALU.mult, ALU.add))
                        rec[("cnvd", m, n)] = vdummy()
                        if knd != "v":
                            nsq += 1
                            vector.wait_ge(mmS.h, nsq)
                            rec[("psmv", m, n)] = op(vector.tensor_scalar(
                                rsr[:, n * 512:(n + 1) * 512], psM[:], 1e-6, None, ALU.add))
                    if knd != "v":
                        rec[("rmv", m)] = vn[0]
                        rec[("rmvd", m)] = vdummy()
                        vector.wait_ge(sqr.h, R(("sqr", m), 0))
                        vector.wait_ge(wse.h, R(("wsesqr", m), 0))
                        op(vector.reciprocal(rsr[:], rs2[:]))
                        vector.wait_ge(vch.h, vn[0])
                        if knd == "q":
                            op(vector.tensor_scalar(rs2[:], rsr[:], float(128 ** -0.5), None, ALU.mult))
                        else:
                            vector.wait_ge(bld.h, 16 * (m - 8 + 1))
                            op(vector.tensor_tensor(rs2[:], rsr[:], brow[:], ALU.mult))
                    rec[("rowv", m)] = vn[0]
                    rec[("rowd", m)] = vdummy()
                    vector.wait_ge(rbq.h, 16 * R(("rbc", m), 0))
                    vector.wait_ge(sil.h, 8 * (m + 1))
                    vector.wait_ge(wse.h, R(("wsem", m), 0))
                    if m >= 2:
                        vector.wait_ge(stq.h, 16 * R(("stc", m - 2), 0))
                    vector.wait_ge(vch.h, vn[0])
                    op(vector.tensor_tensor(obuf[:, 0:4096], sraw[:], bc1[:], ALU.mult))
                    if knd == "k":
                        vector.wait_ge(vch.h, vn[0])
                        op(vector.tensor_tensor(obuf[:, 4096:8192], sraw[:], bc2[:], ALU.mult))
                    rec[("stv", m)] = vn[0]
                    rec[("std", m)] = vdummy()

            @block.scalar
            def _(scalar):
                wn = [0]

                def wdummy():
                    scalar.wait_ge(wse.h, wn[0])
                    wn[0] += 1
                    return scalar.activation(warm[:, 0:16], warm[:, 16:32], AF.Exp
                                             ).then_inc(wse.h, 1)

                scalar.memzero(warm[:]).then_inc(wse.h, 1)
                wn[0] = 1
                scalar.wait_ge(lcst.h, 48)
                nsq = 0
                nsqr = 0
                for n in range(8):
                    scalar.wait_ge(mmP.h, 16 * (n + 1))
                    scalar.activation(beta[:, n * 512:(n + 1) * 512], psA[0:8, :], AF.Sigmoid
                                      ).then_inc(sse.h, 1)
                wdummy()
                rec["wsebeta"] = wn[0]
                for m in range(NM):
                    knd = mkind(m)
                    for n in range(8):
                        scalar.wait_ge(vch.h, R(("cnv", m, n), 0))
                        scalar.wait_ge(cnvd.h, R(("cnvd", m, n), 0))
                        scalar.activation(sraw[:, n * 512:(n + 1) * 512], c2[:], AF.Silu
                                          ).then_inc(sil.h, 1)
                        if knd != "v":
                            nsq += 1
                            scalar.activation(sqb[:], sraw[:, n * 512:(n + 1) * 512], AF.Square
                                              ).then_inc(sqe.h, 1)
                        wdummy()
                    rec[("wsem", m)] = wn[0]
                    if knd != "v":
                        scalar.wait_ge(vch.h, R(("rmv", m), 0))
                        scalar.wait_ge(cnvd.h, R(("rmvd", m), 0))
                        nsqr += 1
                        scalar.activation(rs2[:], rsr[:], AF.Sqrt).then_inc(sqr.h, 1)
                        rec[("sqr", m)] = nsqr
                        wdummy()
                        rec[("wsesqr", m)] = wn[0]

            @block.tensor
            def _(tensor):
                tensor.wait_ge(lcst.h, 48)
                for n in range(8):
                    tensor.wait_ge(lx[n % 2].h, 16 * (n // 2 + 1))
                    if n >= 1:
                        tensor.wait_ge(sse.h, n)
                    inst = None
                    for k in range(16):
                        inst = tensor.matmul(psA[0:8, :], wb8[:, k * 8:(k + 1) * 8],
                                             xn[n % 2][:, k * 512:(k + 1) * 512],
                                             start=(k == 0), stop=(k == 15))
                    inst.then_inc(mmP.h, 16)
                nsq = 0
                gseq = []
                for m in range(NM):
                    for n in range(8):
                        gseq.append((m, n))
                for m in range(NM):
                    knd = mkind(m)
                    tensor.wait_ge(lw.h, 16 * (m + 1))
                    for n in range(8):
                        gi = 8 + m * 8 + n
                        tensor.wait_ge(lx[gi % 2].h, 16 * (gi // 2 + 1))
                        idx = m * 8 + n
                        if idx >= 2:
                            pm, pn = gseq[idx - 2]
                            tensor.wait_ge(vch.h, R(("evv", pm, pn), 0))
                        if idx < 2:
                            tensor.wait_ge(sse.h, 8)
                        ps = psA if n % 2 == 0 else psB
                        inst = None
                        for k in range(16):
                            inst = tensor.matmul(ps[:], w3[:, k * 128:(k + 1) * 128],
                                                 xn[gi % 2][:, k * 512:(k + 1) * 512],
                                                 start=(k == 0), stop=(k == 15))
                        inst.then_inc(mmP.h, 16)
                        if knd != "v":
                            nsq += 1
                            tensor.wait_ge(sqe.h, nsq)
                            tensor.wait_ge(wse.h, R("wsebeta", 0) + m * 8 + n + 1)
                            if nsq >= 2:
                                tensor.wait_ge(vch.h, R(("psmvprev", m, n), 0))
                            tensor.matmul(psM[:], one[:], sqb[:], start=True, stop=True
                                          ).then_inc(mmS.h, 1)

            @block.sync
            def _(sync):
                nrw = 0
                nrb = 0
                nst = 0
                for m in range(NM):
                    knd = mkind(m)
                    sync.wait_ge(vch.h, R(("rowv", m), 0))
                    sync.wait_ge(cnvd.h, R(("rowd", m), 0))
                    src1 = rs2 if knd == "q" else (rsr if knd == "k" else beta[m - 16:m - 15, :])
                    if knd == "v":
                        sync.dma_start(out=rscr[0], in_=src1).then_inc(rwq.h, 16)
                    else:
                        sync.dma_start(out=rscr[0], in_=(rs2[:] if knd == "q" else rsr[:])).then_inc(rwq.h, 16)
                    nrw += 1
                    if knd == "k":
                        sync.dma_start(out=rscr[1], in_=rs2[:]).then_inc(rwq.h, 16)
                        nrw += 1
                    sync.wait_ge(rwq.h, 16 * nrw)
                    if m >= 1:
                        sync.wait_ge(vch.h, R(("stv", m - 1), 0))
                        sync.wait_ge(cnvd.h, R(("std", m - 1), 0))
                    sync.dma_start(out=bc1[:], in_=rscr[0].partition_broadcast(128)).then_inc(rbq.h, 16)
                    nrb += 1
                    if knd == "k":
                        sync.dma_start(out=bc2[:], in_=rscr[1].partition_broadcast(128)).then_inc(rbq.h, 16)
                        nrb += 1
                    rec[("rbc", m)] = nrb
                    sync.wait_ge(vch.h, R(("stv", m), 0))
                    sync.wait_ge(cnvd.h, R(("std", m), 0))
                    dst1 = {"q": QN, "k": KN, "v": VB}[knd]
                    mm = m % 8
                    sync.dma_start(out=dst1.rearrange("(h p) t -> p h t", p=128)[:, mm],
                                   in_=obuf[:, 0:4096]).then_inc(stq.h, 16)
                    nst += 1
                    if knd == "k":
                        sync.dma_start(out=KB.rearrange("(h p) t -> p h t", p=128)[:, mm],
                                       in_=obuf[:, 4096:8192]).then_inc(stq.h, 16)
                        nst += 1
                    rec[("stc", m)] = nst
                sync.wait_ge(stq.h, 16 * nst)

            @block.gpsimd
            def _(gpsimd):
                gpsimd.dma_start(out=wb8[:].rearrange("p (k f) -> p k f", k=16),
                                 in_=WB8.rearrange("(k p) f -> p k f", p=128)).then_inc(lcst.h, 16)
                gpsimd.dma_start(out=cvw[:], in_=CONV).then_inc(lcst.h, 16)
                gpsimd.dma_start(out=one[:], in_=ONE).then_inc(lcst.h, 16)
                gi = 0
                for n in range(8):
                    if gi >= 2:
                        gpsimd.wait_ge(mmP.h, 16 * (gi - 1))
                    gpsimd.dma_start(out=xn[gi % 2][:].rearrange("p (k t) -> p k t", k=16),
                                     in_=X2B.rearrange("(k p) t -> p k t", p=128)[:, :, n * 512:(n + 1) * 512]
                                     ).then_inc(lx[gi % 2].h, 16)
                    gi += 1
                gpsimd.wait_ge(sse.h, 8)
                gpsimd.wait_ge(wse.h, 2)
                gpsimd.dma_start(out=bscr, in_=beta[:]).then_inc(bst.h, 16)
                for m in range(NM):
                    if m == 8:
                        gpsimd.wait_ge(bst.h, 16)
                    if 8 <= m < 16:
                        mk = m - 8
                        if mk >= 1:
                            gpsimd.wait_ge(vch.h, R(("rowv", 8 + mk - 1), 0))
                        gpsimd.dma_start(out=brow[:], in_=bscr[mk]).then_inc(bld.h, 16)
                    if m >= 1:
                        gpsimd.wait_ge(mmP.h, 16 * (8 + m * 8))
                    gpsimd.dma_start(out=w3[:].rearrange("p (k f) -> p k f", k=16),
                                     in_=WQ3.rearrange("(k p) f -> p k f", p=128)[:, :, m * 128:(m + 1) * 128]
                                     ).then_inc(lw.h, 16)
                    for n in range(8):
                        if gi >= 2:
                            gpsimd.wait_ge(mmP.h, 16 * (gi - 1))
                        gpsimd.dma_start(out=xn[gi % 2][:].rearrange("p (k t) -> p k t", k=16),
                                         in_=X2B.rearrange("(k p) t -> p k t", p=128)[:, :, n * 512:(n + 1) * 512]
                                         ).then_inc(lx[gi % 2].h, 16)
                        gi += 1

        if pas == 0:
            prev = 0
            for m in range(NM):
                if mkind(m) == "v":
                    continue
                for n in range(8):
                    rec[("psmvprev", m, n)] = prev
                    prev = rec.get(("psmv", m, n), 0)
    return nc


def run_l1_qkv(inputs, x2_results):
    Wq = inputs["L1_attn_Wq"]; Wk = inputs["L1_attn_Wk"]; Wv = inputs["L1_attn_Wv"]
    Wb = inputs["L1_attn_Wb"]
    ONE = np.ones((128, 1), dtype=ml_dtypes.bfloat16)
    in_maps = []
    for c in range(8):
        b, r = divmod(c, 2)
        X2nat = np.zeros((D, 4096), dtype=np.float32)
        X2nat[:, slot_tokens(0)] = x2_results[2 * b]["X2"]
        X2nat[:, slot_tokens(1)] = x2_results[2 * b + 1]["X2"]
        cols = slice(r * 1024, (r + 1) * 1024)
        WQ3 = np.ascontiguousarray(np.concatenate([Wq[:, cols], Wk[:, cols], Wv[:, cols]], axis=1))
        WB8 = np.ascontiguousarray(Wb[:, r * 8:(r + 1) * 8])
        CONV = np.ascontiguousarray(np.tile(np.concatenate(
            [inputs["L1_attn_conv_q"], inputs["L1_attn_conv_k"], inputs["L1_attn_conv_v"]]),
            (128, 1)).astype(np.float32))
        in_maps.append({"X2B": np.ascontiguousarray(X2nat), "WQ3": WQ3, "WB8": WB8,
                        "CONV": CONV, "ONE": ONE})
    nc = build_l1_qkv()
    res = run_bass_kernel_spmd(nc, in_maps, core_ids=list(range(8)))
    return res.results




def build_l1_scan(NH=8):
    nc = bass.Bass("TRN2", target_bir_lowering=False, debug=False)
    QN = nc.dram_tensor("QN", [1024, 4096], BF16, kind="ExternalInput").ap()
    KN = nc.dram_tensor("KN", [1024, 4096], BF16, kind="ExternalInput").ap()
    KB = nc.dram_tensor("KB", [1024, 4096], BF16, kind="ExternalInput").ap()
    KNT = nc.dram_tensor("KNT", [128, 8 * 32 * 128], BF16, kind="ExternalInput").ap()
    KBT = nc.dram_tensor("KBT", [128, 8 * 32 * 128], BF16, kind="ExternalInput").ap()
    VBT = nc.dram_tensor("VBT", [128, 8 * 32 * 128], BF16, kind="ExternalInput").ap()
    MSL = nc.dram_tensor("MSL", [128, 384], F32, kind="ExternalInput").ap()
    IDN = nc.dram_tensor("IDN", [128, 128], F32, kind="ExternalInput").ap()
    ONW = nc.dram_tensor("ONW", [128, 1], F32, kind="ExternalInput").ap()
    ONE = nc.dram_tensor("ONE", [128, 1], BF16, kind="ExternalInput").ap()
    O1 = nc.dram_tensor("O1", [1024, 4096], BF16, kind="ExternalOutput").ap()
    rscr = nc.dram_tensor("rscr", [1, 4096], F32, kind="Internal").ap()

    qn = nc.alloc_sbuf_tensor("qn", [128, 4096], BF16)
    kn = nc.alloc_sbuf_tensor("kn", [128, 4096], BF16)
    kb = nc.alloc_sbuf_tensor("kb", [128, 4096], BF16)
    knt = nc.alloc_sbuf_tensor("knt", [128, 4096], BF16)
    kbt = nc.alloc_sbuf_tensor("kbt", [128, 4096], BF16)
    vbt = nc.alloc_sbuf_tensor("vbt", [128, 4096], BF16)
    msl = nc.alloc_sbuf_tensor("msl", [128, 384], F32)
    idn = nc.alloc_sbuf_tensor("idn", [128, 128], F32)
    onw = nc.alloc_sbuf_tensor("onw", [128, 1], F32)
    one = nc.alloc_sbuf_tensor("one", [128, 1], BF16)
    Sf = nc.alloc_sbuf_tensor("Sf", [128, 128], F32)
    Sb = nc.alloc_sbuf_tensor("Sb", [128, 128], BF16)
    Lt1 = nc.alloc_sbuf_tensor("Lt1", [128, 128], F32)
    Lt2 = nc.alloc_sbuf_tensor("Lt2", [128, 128], F32)
    Px = nc.alloc_sbuf_tensor("Px", [128, 128], F32)
    PxT = nc.alloc_sbuf_tensor("PxT", [128, 128], F32)
    Gx = nc.alloc_sbuf_tensor("Gx", [128, 128], F32)
    GxT = nc.alloc_sbuf_tensor("GxT", [128, 128], F32)
    GxTb = nc.alloc_sbuf_tensor("GxTb", [128, 128], BF16)
    ut = nc.alloc_sbuf_tensor("ut", [128, 128], BF16)
    upt = nc.alloc_sbuf_tensor("upt", [128, 128], BF16)
    wf = nc.alloc_sbuf_tensor("wf", [128, 128], BF16)
    att = nc.alloc_sbuf_tensor("att", [128, 128], BF16)
    ost = nc.alloc_sbuf_tensor("ost", [128, 4096], BF16)
    osq = nc.alloc_sbuf_tensor("osq", [128, 512], BF16)
    rsr = nc.alloc_sbuf_tensor("rsr", [1, 4096], F32)
    rs2 = nc.alloc_sbuf_tensor("rs2", [1, 4096], F32)
    bc1 = nc.alloc_sbuf_tensor("bc1", [128, 4096], F32)
    tmo = nc.alloc_sbuf_tensor("tmo", [128, 512], F32)
    warm = nc.alloc_sbuf_tensor("warm", [1, 32], F32)
    dum2 = nc.alloc_sbuf_tensor("dum2", [1, 8], F32)
    psU = nc.alloc_psum_tensor("psU", [128, 512], F32)
    psV = nc.alloc_psum_tensor("psV", [128, 512], F32)
    psM = nc.alloc_psum_tensor("psM", [1, 512], F32)

    lds = Sem(nc, "lds"); tch = Sem(nc, "tch"); vchd = Sem(nc, "vchd")
    sqe = Sem(nc, "sqe"); wse = Sem(nc, "wse")
    rwq = Sem(nc, "rwq"); rbq = Sem(nc, "rbq"); stq = Sem(nc, "stq")

    NLEV = 6
    prog = []

    def T(fn):
        prog.append(("t", fn))

    def V(fn):
        prog.append(("v", fn))

    def S(fn):
        prog.append(("s", fn))

    def DMA(kind, fn):
        prog.append((kind, fn))

    for h in range(NH):
        hh = h
        V(lambda v, hh=hh: (v.wait_ge(stq.h, 16 * hh), v.memset(Sf[:], 0.0)))
        V(lambda v: v.memset(Sb[:], 0.0))
        for c in range(32):
            sl_ = slice(c * 128, (c + 1) * 128)
            T(lambda t, sl_=sl_: t.matmul(psU[:, 0:128], kb[:, sl_], kn[:, sl_], start=True, stop=True))
            T(lambda t, sl_=sl_: t.matmul(psU[:, 128:256], kn[:, sl_], kb[:, sl_], start=True, stop=True))
            V(lambda v: v.tensor_tensor(Lt1[:], psU[:, 0:128], msl[:, 0:128], ALU.mult))
            V(lambda v: v.tensor_tensor(Lt2[:], psU[:, 128:256], msl[:, 128:256], ALU.mult))
            V(lambda v: v.tensor_sub(GxT[:], idn[:], Lt2[:]))
            for lev in range(NLEV):
                Xs, XsT = (Lt1, Lt2) if lev == 0 else (Px, PxT)
                T(lambda t, XsT=XsT, Xs=Xs: t.matmul(psU[:, 0:128], XsT[:], Xs[:], start=True, stop=True))
                if lev < NLEV - 1:
                    T(lambda t, XsT=XsT, Xs=Xs: t.matmul(psU[:, 128:256], Xs[:], XsT[:], start=True, stop=True))
                V(lambda v: v.tensor_copy(Px[:], psU[:, 0:128]))
                if lev < NLEV - 1:
                    V(lambda v: v.tensor_copy(PxT[:], psU[:, 128:256]))
                T(lambda t: t.matmul(psV[:, 128:256], idn[:], GxT[:], start=True, stop=False))
                T(lambda t: t.matmul(psV[:, 128:256], Px[:], GxT[:], start=False, stop=True))
                V(lambda v: v.tensor_copy(GxT[:], psV[:, 128:256]))
            V(lambda v: v.tensor_copy(GxTb[:], GxT[:]))
            T(lambda t, sl_=sl_: t.matmul(psU[:, 0:128], GxTb[:], vbt[:, sl_], start=True, stop=True))
            T(lambda t, sl_=sl_: t.matmul(psU[:, 128:256], kbt[:, sl_], GxTb[:], start=True, stop=True))
            V(lambda v: v.tensor_copy(ut[:], psU[:, 0:128]))
            V(lambda v: v.tensor_copy(wf[:], psU[:, 128:256]))
            T(lambda t: t.matmul(psV[:, 0:128], wf[:], Sb[:], start=True, stop=True))
            V(lambda v: v.tensor_sub(upt[:], ut[:], psV[:, 0:128]))
            T(lambda t, sl_=sl_: t.matmul(psV[:, 128:256], kn[:, sl_], qn[:, sl_], start=True, stop=True))
            V(lambda v: v.tensor_tensor(att[:], psV[:, 128:256], msl[:, 256:384], ALU.mult))
            T(lambda t, sl_=sl_: t.matmul(psU[:, 256:384], Sb[:], qn[:, sl_], start=True, stop=False))
            T(lambda t: t.matmul(psU[:, 256:384], upt[:], att[:], start=False, stop=True))
            V(lambda v, sl_=sl_: v.tensor_copy(ost[:, sl_], psU[:, 256:384]))
            T(lambda t, sl_=sl_: t.matmul(psV[:, 256:384], knt[:, sl_], upt[:], start=True, stop=True))
            V(lambda v: v.tensor_add(Sf[:], Sf[:], psV[:, 256:384]))
            V(lambda v: v.tensor_copy(Sb[:], Sf[:]))
        for s8 in range(8):
            so = slice(s8 * 512, (s8 + 1) * 512)
            S(lambda s, so=so: s.activation(osq[:], ost[:, so], AF.Square))
            T(lambda t: t.matmul(psM[:], one[:], osq[:], start=True, stop=True))
            V(lambda v, so=so: v.tensor_scalar(rsr[0:1, so], psM[:], 1.0 / 128.0, 1e-5,
                                               ALU.mult, ALU.add))
        S(lambda s: s.activation(rs2[:], rsr[:], AF.Sqrt))
        V(lambda v: v.reciprocal(rsr[:], rs2[:]))
        DMA("dmaw", lambda sy, hh=hh: sy.dma_start(out=rscr, in_=rsr[:]).then_inc(rwq.h, 16))
        DMA("dmab", lambda sy, hh=hh: (sy.wait_ge(rwq.h, 16 * (hh + 1)),
                                       sy.dma_start(out=bc1[:], in_=rscr.partition_broadcast(128)
                                                    ).then_inc(rbq.h, 16)))
        for s8 in range(8):
            so = slice(s8 * 512, (s8 + 1) * 512)
            V(lambda v, so=so, hh=hh: (v.wait_ge(rbq.h, 16 * (hh + 1)),
                                       v.tensor_scalar(tmo[:], ost[:, so], onw[:, 0:1], None, ALU.mult)))
            V(lambda v, so=so: v.tensor_tensor(ost[:, so], tmo[:], bc1[:, so], ALU.mult))
        DMA("dmas", lambda sy, hh=hh: sy.dma_start(
            out=O1.rearrange("(g p) t -> p g t", p=128)[:, hh], in_=ost[:]).then_inc(stq.h, 16))

    nT = [0]; nV = [0]; nS = [0]
    pref = []
    for eng, fn in prog:
        pref.append((nT[0], nV[0], nS[0]))
        if eng == "t":
            nT[0] += 1
        elif eng == "v":
            nV[0] += 1
        elif eng == "s":
            nS[0] += 1
    OPS_PER_HEAD = len(prog) // NH

    with nc.Block() as block:
        @block.tensor
        def _(tensor):
            lastv = -1
            lasts = -1
            lasth = -1
            for i, (eng, fn) in enumerate(prog):
                if eng != "t":
                    continue
                h = i // OPS_PER_HEAD
                tb, vb, sb = pref[i]
                if h != lasth:
                    tensor.wait_ge(lds.h, 16 * (4 + 6 * (h + 1)))
                    lasth = h
                if vb != lastv:
                    tensor.wait_ge(vchd.h, vb)
                    lastv = vb
                if sb != lasts:
                    tensor.wait_ge(sqe.h, sb)
                    tensor.wait_ge(wse.h, 1 + sb)
                    lasts = sb
                fn(tensor).then_inc(tch.h, 1)

        @block.vector
        def _(vector):
            lastt = -1
            lasts = -1
            for i, (eng, fn) in enumerate(prog):
                if eng != "v":
                    continue
                tb, vb, sb = pref[i]
                if tb != lastt:
                    vector.wait_ge(tch.h, tb)
                    lastt = tb
                if sb != lasts:
                    vector.wait_ge(sqe.h, sb)
                    vector.wait_ge(wse.h, 1 + sb)
                    lasts = sb
                fn(vector)
                vector.tensor_copy(dum2[0:1, 0:1], one[0:1, 0:1]).then_inc(vchd.h, 1)

        @block.scalar
        def _(scalar):
            wn = [0]
            scalar.memzero(warm[:]).then_inc(wse.h, 1)
            wn[0] = 1
            for i, (eng, fn) in enumerate(prog):
                if eng != "s":
                    continue
                tb, vb, sb = pref[i]
                scalar.wait_ge(vchd.h, vb)
                fn(scalar).then_inc(sqe.h, 1)
                scalar.wait_ge(wse.h, wn[0])
                wn[0] += 1
                scalar.activation(warm[:, 0:16], warm[:, 16:32], AF.Exp).then_inc(wse.h, 1)

        @block.sync
        def _(sync):
            for i, (eng, fn) in enumerate(prog):
                if eng not in ("dmaw", "dmab", "dmas"):
                    continue
                tb, vb, sb = pref[i]
                sync.wait_ge(vchd.h, vb)
                fn(sync)
            sync.wait_ge(stq.h, 16 * NH)

        @block.gpsimd
        def _(gpsimd):
            gpsimd.dma_start(out=msl[:], in_=MSL).then_inc(lds.h, 16)
            gpsimd.dma_start(out=idn[:], in_=IDN).then_inc(lds.h, 16)
            gpsimd.dma_start(out=onw[:], in_=ONW).then_inc(lds.h, 16)
            gpsimd.dma_start(out=one[:], in_=ONE).then_inc(lds.h, 16)
            for h in range(NH):
                if h >= 1:
                    gpsimd.wait_ge(stq.h, 16 * h)
                for src_d, dst in ((QN, qn), (KN, kn), (KB, kb)):
                    gpsimd.dma_start(out=dst[:], in_=src_d.rearrange("(g p) t -> p g t", p=128)[:, h]
                                     ).then_inc(lds.h, 16)
                for src_d, dst in ((KNT, knt), (KBT, kbt), (VBT, vbt)):
                    gpsimd.dma_start(out=dst[:], in_=src_d[:, h * 4096:(h + 1) * 4096]
                                     ).then_inc(lds.h, 16)

    return nc


def run_l1_scan(inputs, qkv):
    cosn = None
    ONE = np.ones((128, 1), dtype=ml_dtypes.bfloat16)
    IDN = np.eye(128, dtype=np.float32)
    i_ = np.arange(128)[:, None]; j_ = np.arange(128)[None, :]
    stril_cm = (j_ < i_).astype(np.float32)
    striu_mc = (j_ > i_).astype(np.float32)
    inclT = (j_ >= i_).astype(np.float32)
    MSL = np.concatenate([stril_cm, striu_mc, inclT], axis=1)
    ONW = np.ascontiguousarray(inputs["L1_attn_onorm_w"].astype(np.float32)[:, None])
    in_maps = []
    for c in range(8):
        def tmize(a):
            return np.ascontiguousarray(
                np.asarray(a).reshape(8, 128, 32, 128).transpose(3, 0, 2, 1).reshape(128, 8 * 32 * 128))
        in_maps.append({"QN": qkv[c]["QN"], "KN": qkv[c]["KN"], "KB": qkv[c]["KB"],
                        "KNT": tmize(qkv[c]["KN"]), "KBT": tmize(qkv[c]["KB"]),
                        "VBT": tmize(qkv[c]["VB"]), "MSL": MSL, "IDN": IDN,
                        "ONW": ONW, "ONE": ONE})
    nc = build_l1_scan()
    res = run_bass_kernel_spmd(nc, in_maps, core_ids=list(range(8)))
    return res.results




def build_l1_tail():
    nc = bass.Bass("TRN2", target_bir_lowering=False, debug=False)
    O = nc.dram_tensor("O", [D, 2048], BF16, kind="ExternalInput").ap()
    X2IN = nc.dram_tensor("X2IN", [D, 2048], F32, kind="ExternalInput").ap()
    WOB = nc.dram_tensor("WOB", [16, 16, 128, 128], BF16, kind="ExternalInput").ap()
    GWB = nc.dram_tensor("GWB", [64, 16, 128, 128], BF16, kind="ExternalInput").ap()
    UWB = nc.dram_tensor("UWB", [64, 16, 128, 128], BF16, kind="ExternalInput").ap()
    WPB = nc.dram_tensor("WPB", [16, 64, 128, 128], BF16, kind="ExternalInput").ap()
    ONE = nc.dram_tensor("ONE", [128, 1], BF16, kind="ExternalInput").ap()
    OUT = nc.dram_tensor("OUT", [D, 2048], F32, kind="ExternalOutput").ap()
    X3F = nc.dram_tensor("X3F", [D, 2048], F32, kind="Internal").ap()
    scr = nc.dram_tensor("scr", [8, 1024], F32, kind="Internal").ap()
    scr2 = nc.dram_tensor("scr2", [1, 512], F32, kind="Internal").ap()

    NSL = 4
    RMS_EPS2 = 1.1920929e-07

    hdn = nc.alloc_sbuf_tensor("hdn", [128, 64 * 512], BF16)
    wob = nc.alloc_sbuf_tensor("wob0", [128, 16 * 128], BF16)
    y1raw = nc.alloc_sbuf_tensor("y1raw", [128, 16 * 512], F32)
    x1bn = nc.alloc_sbuf_tensor("x1bn", [128, 16 * 512], BF16)
    gwb = [nc.alloc_sbuf_tensor(f"gwb{i}", [128, 16 * 128], BF16) for i in range(2)]
    uwb = [nc.alloc_sbuf_tensor(f"uwb{i}", [128, 16 * 128], BF16) for i in range(2)]
    wpb = nc.alloc_sbuf_tensor("wpb0", [128, 64 * 128], BF16)
    gm = nc.alloc_sbuf_tensor("gm", [128, 512], BF16)
    um = nc.alloc_sbuf_tensor("um", [128, 512], BF16)
    x1fn = nc.alloc_sbuf_tensor("x1fn", [128, 16 * 512], F32)
    utn = x1fn
    y2raw = y1raw
    sumb = x1bn
    one = nc.alloc_sbuf_tensor("one", [128, 1], BF16)
    mb = nc.alloc_sbuf_tensor("mb", [128, 1024], F32)
    mb2 = nc.alloc_sbuf_tensor("mb2", [128, 512], F32)
    rrow = nc.alloc_sbuf_tensor("rrow", [1, 1536], F32)
    y1q = y1raw[:].bitcast(BF16)
    utq = x1fn[:].bitcast(BF16)
    srow = x1fn[0:1, 0:1024]
    tmp5 = mb[0:1, 0:512]
    tmp6 = mb[0:1, 512:1024]
    warm = nc.alloc_sbuf_tensor("warm", [1, 32], F32)
    dum2 = nc.alloc_sbuf_tensor("dum2", [1, 8], F32)
    tmpa = nc.alloc_sbuf_tensor("tmpa", [128, 512], F32)
    psA = nc.alloc_psum_tensor("psA", [128, 512], F32)
    psB = nc.alloc_psum_tensor("psB", [128, 512], F32)
    psM = nc.alloc_psum_tensor("psM", [1, 512], F32)
    psQ = nc.alloc_psum_tensor("psQ", [1, 512], F32)
    psQ2 = nc.alloc_psum_tensor("psQ2", [1, 512], F32)

    lres = Sem(nc, "lres")
    lon = Sem(nc, "lon"); lut = Sem(nc, "lut"); lx1 = Sem(nc, "lx1")
    lwm = Sem(nc, "lwm0")
    lgw = [Sem(nc, "lgw0"), Sem(nc, "lgw1")]
    ldw = Sem(nc, "ldw0")
    mmA = Sem(nc, "mmA"); mmB = Sem(nc, "mmB"); mmC = Sem(nc, "mmC"); mmS = Sem(nc, "mmS")
    mmS2 = Sem(nc, "mmS2")
    evA = Sem(nc, "evA"); evAd = Sem(nc, "evAd")
    seB = Sem(nc, "seB"); evB = Sem(nc, "evB"); evBd = Sem(nc, "evBd")
    evC = Sem(nc, "evC"); evCd = Sem(nc, "evCd")
    cst = Sem(nc, "cst"); cstd = Sem(nc, "cstd"); vch = Sem(nc, "vch")
    sqq = Sem(nc, "sqq"); wse = Sem(nc, "wse")
    sqq2 = Sem(nc, "sqq2"); wse2 = Sem(nc, "wse2")
    rq = Sem(nc, "rq"); rqd = Sem(nc, "rqd"); rq2 = Sem(nc, "rq2")
    rq3 = Sem(nc, "rq3"); rq3d = Sem(nc, "rq3d")
    rw2 = Sem(nc, "rw2"); rb2 = Sem(nc, "rb2")
    ap2 = Sem(nc, "ap2"); ap2d = Sem(nc, "ap2d")
    stq = Sem(nc, "stq"); rwq = Sem(nc, "rwq"); rbq = Sem(nc, "rbq")
    apq = Sem(nc, "apq"); apqd = Sem(nc, "apqd")
    stX3 = Sem(nc, "stX3"); stX2 = Sem(nc, "stX2")

    with nc.Block() as block:
        @block.gpsimd
        def _(gpsimd):
            gpsimd.dma_start(out=one[:], in_=ONE).then_inc(lres.h, 16)
            for n in range(NSL):
                if n >= 1:
                    gpsimd.wait_ge(stX2.h, 16 * n)
                gpsimd.dma_start(out=hdn[:].rearrange("p (k t) -> p k t", k=64)[:, 0:16],
                                 in_=O.rearrange("(k p) t -> p k t", p=128)[:, :, n * 512:(n + 1) * 512]
                                 ).then_inc(lon.h, 16)
                gpsimd.dma_start(out=utn[:].rearrange("p (k t) -> p k t", k=16),
                                 in_=X2IN.rearrange("(k p) t -> p k t", p=128)[:, :, n * 512:(n + 1) * 512]
                                 ).then_inc(lut.h, 16)
                for m in range(16):
                    if n * 16 + m >= 1:
                        gpsimd.wait_ge(mmA.h, 16 * n + m)
                    gpsimd.dma_start(out=wob[:].rearrange("p (k f) -> p k f", k=16),
                                     in_=WOB[m].rearrange("k p f -> p k f")
                                     ).then_inc(lwm.h, 16)
                for m in range(64):
                    if n * 64 + m >= 2:
                        gpsimd.wait_ge(mmB.h, 2 * (64 * n + m - 1))
                    gpsimd.dma_start(out=gwb[m % 2][:].rearrange("p (k f) -> p k f", k=16),
                                     in_=GWB[m].rearrange("k p f -> p k f")
                                     ).then_inc(lgw[m % 2].h, 16)
                    gpsimd.dma_start(out=uwb[m % 2][:].rearrange("p (k f) -> p k f", k=16),
                                     in_=UWB[m].rearrange("k p f -> p k f")
                                     ).then_inc(lgw[m % 2].h, 16)
                gpsimd.wait_ge(stX3.h, 16 * (n + 1))
                gpsimd.wait_ge(mmS2.h, n + 1)
                gpsimd.dma_start(out=x1fn[:].rearrange("p (k t) -> p k t", k=16),
                                 in_=X3F.rearrange("(k p) t -> p k t", p=128)[:, :, n * 512:(n + 1) * 512]
                                 ).then_inc(lx1.h, 16)

        @block.tensor
        def _(tensor):
            tensor.wait_ge(lres.h, 16)
            for n in range(NSL):
                tensor.wait_ge(lon.h, 16 * (n + 1))
                for m in range(16):
                    tensor.wait_ge(lwm.h, 16 * (16 * n + m + 1))
                    if m >= 2:
                        tensor.wait_ge(evA.h, 16 * n + m - 1)
                    inst = None
                    for k in range(16):
                        ps = psA if m % 2 == 0 else psB
                        inst = tensor.matmul(ps[:], wob[:, k * 128:(k + 1) * 128],
                                             hdn[:, k * 512:(k + 1) * 512],
                                             start=(k == 0), stop=(k == 15))
                    inst.then_inc(mmA.h, 1)
                ln = 2 * n
                tensor.wait_ge(cst.h, ln + 1)
                tensor.wait_ge(cstd.h, ln + 1)
                tensor.wait_ge(sqq.h, ln + 1)
                tensor.wait_ge(wse.h, 3 + 2 * ln + 1)
                inst = None
                for k in range(16):
                    inst = tensor.matmul(psM[:], one[:], sumb[:, k * 512:(k + 1) * 512],
                                         start=(k == 0), stop=(k == 15))
                inst.then_inc(mmS.h, 1)
                inst = None
                for k in range(16):
                    inst = tensor.matmul(psQ[:], one[:], hdn[:, k * 512:(k + 1) * 512],
                                         start=(k == 0), stop=(k == 15))
                inst.then_inc(mmS.h, 1)
                tensor.wait_ge(apq.h, 16 * (ln + 1))
                tensor.wait_ge(apqd.h, ln + 1)
                for m in range(64):
                    tensor.wait_ge(lgw[m % 2].h, 32 * ((64 * n + m) // 2 + 1))
                    if m >= 1:
                        tensor.wait_ge(seB.h, 64 * n + m)
                        tensor.wait_ge(evB.h, 2 * (64 * n + m) - 1)
                    inst = None
                    for k in range(16):
                        inst = tensor.matmul(psA[:], gwb[m % 2][:, k * 128:(k + 1) * 128],
                                             x1bn[:, k * 512:(k + 1) * 512],
                                             start=(k == 0), stop=(k == 15))
                    inst.then_inc(mmB.h, 1)
                    inst = None
                    for k in range(16):
                        inst = tensor.matmul(psB[:], uwb[m % 2][:, k * 128:(k + 1) * 128],
                                             x1bn[:, k * 512:(k + 1) * 512],
                                             start=(k == 0), stop=(k == 15))
                    inst.then_inc(mmB.h, 1)
                tensor.wait_ge(sqq2.h, 2 * (n + 1))
                tensor.wait_ge(wse2.h, 3 * n + 2)
                inst = None
                for k in range(64):
                    src = y1q[:, k * 512:(k + 1) * 512] if k < 32 \
                        else utq[:, (k - 32) * 512:(k - 31) * 512]
                    inst = tensor.matmul(psQ2[:], one[:], src,
                                         start=(k == 0), stop=(k == 63))
                inst.then_inc(mmS2.h, 1)
                tensor.wait_ge(ap2.h, 64 * (n + 1))
                tensor.wait_ge(ap2d.h, n + 1)
                for m in range(16):
                    tensor.wait_ge(ldw.h, 16 * (16 * n + m + 1))
                    if m >= 2:
                        tensor.wait_ge(evC.h, 16 * n + m - 1)
                    inst = None
                    for k in range(64):
                        ps = psA if m % 2 == 0 else psB
                        inst = tensor.matmul(ps[:], wpb[:, k * 128:(k + 1) * 128],
                                             hdn[:, k * 512:(k + 1) * 512],
                                             start=(k == 0), stop=(k == 63))
                    inst.then_inc(mmC.h, 1)
                ln = 2 * n + 1
                tensor.wait_ge(cst.h, ln + 1)
                tensor.wait_ge(cstd.h, ln + 1)
                tensor.wait_ge(sqq.h, ln + 1)
                tensor.wait_ge(wse.h, 3 + 2 * ln + 1)
                inst = None
                for k in range(16):
                    inst = tensor.matmul(psM[:], one[:], sumb[:, k * 512:(k + 1) * 512],
                                         start=(k == 0), stop=(k == 15))
                inst.then_inc(mmS.h, 1)
                inst = None
                for k in range(16):
                    inst = tensor.matmul(psQ[:], one[:], hdn[:, k * 512:(k + 1) * 512],
                                         start=(k == 0), stop=(k == 15))
                inst.then_inc(mmS.h, 1)

        @block.scalar
        def _(scalar):
            wn = [0]
            w2n = [0]

            def wdummy():
                scalar.wait_ge(wse.h, wn[0])
                wn[0] += 1
                return scalar.activation(warm[:, 0:16], warm[:, 16:32], AF.Exp
                                         ).then_inc(wse.h, 1)

            def w2dummy():
                scalar.wait_ge(wse2.h, w2n[0])
                w2n[0] += 1
                return scalar.activation(warm[:, 0:16], warm[:, 16:32], AF.Exp
                                         ).then_inc(wse2.h, 1)

            scalar.memzero(warm[:]).then_inc(wse.h, 1)
            wn[0] = 1
            wdummy()
            wdummy()
            scalar.wait_ge(lres.h, 16)
            for n in range(NSL):
                ln = 2 * n
                scalar.wait_ge(evA.h, 16 * (n + 1))
                scalar.wait_ge(evAd.h, n + 1)
                scalar.activation(hdn[:, 0:8192], y1raw[:], AF.Square).then_inc(sqq.h, 1)
                wdummy()
                scalar.wait_ge(stq.h, 5 * ln + 2)
                scalar.activation(tmp6, tmp5, AF.Sqrt).then_inc(stq.h, 1)
                wdummy()
                for m in range(64):
                    scalar.wait_ge(mmB.h, 2 * (64 * n + m) + 1)
                    scalar.wait_ge(evB.h, 2 * (64 * n + m))
                    scalar.activation(gm[:], psA[:], AF.Silu).then_inc(seB.h, 1)
                scalar.wait_ge(evB.h, 128 * (n + 1))
                scalar.wait_ge(evBd.h, n + 1)
                scalar.wait_ge(stX3.h, 16 * (n + 1))
                scalar.activation(y1q[:], hdn[:, 0:16384], AF.Square).then_inc(sqq2.h, 1)
                w2dummy()
                scalar.activation(utq[:], hdn[:, 16384:32768], AF.Square).then_inc(sqq2.h, 1)
                w2dummy()
                scalar.wait_ge(rq.h, n + 1)
                scalar.wait_ge(rqd.h, n + 1)
                scalar.activation(rrow[0:1, 512:1024], rrow[0:1, 0:512], AF.Sqrt
                                  ).then_inc(rq2.h, 1)
                w2dummy()
                ln = 2 * n + 1
                scalar.wait_ge(evC.h, 16 * (n + 1))
                scalar.wait_ge(evCd.h, n + 1)
                scalar.activation(hdn[:, 0:8192], y2raw[:], AF.Square).then_inc(sqq.h, 1)
                wdummy()
                scalar.wait_ge(stq.h, 5 * ln + 2)
                scalar.activation(tmp6, tmp5, AF.Sqrt).then_inc(stq.h, 1)
                wdummy()

        @block.vector
        def _(vector):
            vn = [0]

            def vdummy():
                return vector.tensor_copy(dum2[0:1, 0:1], one[0:1, 0:1])

            def ln_stats_apply(n, ph, src_):
                ln = 2 * n + ph
                vector.wait_ge(mmS.h, 2 * ln + 2)
                vector.tensor_scalar(srow[:, 0:512], psM[:], 1.0 / 2048.0, None, ALU.mult
                                     ).then_inc(vch.h, 1); vn[0] += 1
                s1 = vn[0]
                vector.tensor_scalar(tmp5, psQ[:], 1.0 / 2048.0, None, ALU.mult
                                     ).then_inc(vch.h, 1); vn[0] += 1
                vector.wait_ge(vch.h, s1)
                vector.tensor_tensor(tmp6, srow[:, 0:512], srow[:, 0:512], ALU.mult
                                     ).then_inc(vch.h, 1); vn[0] += 1
                vector.wait_ge(vch.h, vn[0])
                vector.tensor_sub(tmp5, tmp5, tmp6).then_inc(vch.h, 1); vn[0] += 1
                vector.wait_ge(vch.h, vn[0])
                vector.tensor_scalar(tmp5, tmp5, 1e-5, None, ALU.add).then_inc(stq.h, 1)
                vdummy().then_inc(stq.h, 1)
                vector.wait_ge(stq.h, 5 * ln + 3)
                vector.wait_ge(wse.h, 3 + 2 * ln + 2)
                vector.reciprocal(srow[:, 512:1024], tmp6).then_inc(stq.h, 1)
                vdummy().then_inc(stq.h, 1)
                vector.wait_ge(rbq.h, 16 * (ln + 1))
                for m in range(16):
                    vector.wait_ge(vch.h, vn[0])
                    if ph == 1 and m >= 1:
                        vector.wait_ge(apq.h, 16 * ln + m)
                    vector.tensor_sub(tmpa[:], src_[:, m * 512:(m + 1) * 512], mb[:, 0:512]
                                      ).then_inc(vch.h, 1); vn[0] += 1
                    vector.wait_ge(vch.h, vn[0])
                    if ph == 0:
                        vector.tensor_mul(y1raw[:, m * 512:(m + 1) * 512], tmpa[:],
                                          mb[:, 512:1024]).then_inc(vch.h, 1); vn[0] += 1
                        vector.wait_ge(vch.h, vn[0])
                        vector.tensor_copy(x1bn[:, m * 512:(m + 1) * 512],
                                           y1raw[:, m * 512:(m + 1) * 512]).then_inc(apq.h, 1)
                    else:
                        vector.tensor_mul(y2raw[:, m * 512:(m + 1) * 512], tmpa[:],
                                          mb[:, 512:1024]).then_inc(apq.h, 1)
                vdummy().then_inc(apqd.h, 1)

            vector.wait_ge(lres.h, 16)
            for n in range(NSL):
                vector.wait_ge(lut.h, 16 * (n + 1))
                for m in range(16):
                    vector.wait_ge(mmA.h, 16 * n + m + 1)
                    ps = psA if m % 2 == 0 else psB
                    vector.tensor_add(y1raw[:, m * 512:(m + 1) * 512], ps[:],
                                      utn[:, m * 512:(m + 1) * 512]).then_inc(evA.h, 1)
                vdummy().then_inc(evAd.h, 1)
                vector.wait_ge(evA.h, 16 * (n + 1))
                vector.tensor_copy(sumb[:], y1raw[:]).then_inc(cst.h, 1)
                vdummy().then_inc(cstd.h, 1)
                ln_stats_apply(n, 0, y1raw)
                for m in range(64):
                    vector.wait_ge(mmB.h, 2 * (64 * n + m) + 2)
                    vector.wait_ge(evB.h, 2 * (64 * n + m))
                    vector.tensor_copy(um[:], psB[:]).then_inc(evB.h, 1)
                    vector.wait_ge(seB.h, 64 * n + m + 1)
                    vector.wait_ge(evB.h, 2 * (64 * n + m) + 1)
                    vector.tensor_tensor(hdn[:, m * 512:(m + 1) * 512], gm[:], um[:],
                                         ALU.mult).then_inc(evB.h, 1)
                vdummy().then_inc(evBd.h, 1)
                vector.wait_ge(mmS2.h, n + 1)
                vector.tensor_scalar(rrow[0:1, 0:512], psQ2[:], 1.0 / 8192.0, RMS_EPS2,
                                     ALU.mult, ALU.add).then_inc(rq.h, 1)
                vdummy().then_inc(rqd.h, 1)
                vector.wait_ge(rq2.h, n + 1)
                vector.wait_ge(wse2.h, 3 * (n + 1))
                vector.reciprocal(rrow[0:1, 1024:1536], rrow[0:1, 512:1024]).then_inc(rq3.h, 1)
                vdummy().then_inc(rq3d.h, 1)
                vector.wait_ge(rb2.h, 16 * (n + 1))
                for m in range(64):
                    vector.tensor_mul(hdn[:, m * 512:(m + 1) * 512],
                                      hdn[:, m * 512:(m + 1) * 512], mb2[:]).then_inc(ap2.h, 1)
                vdummy().then_inc(ap2d.h, 1)
                vector.wait_ge(lx1.h, 16 * (n + 1))
                for m in range(16):
                    vector.wait_ge(mmC.h, 16 * n + m + 1)
                    ps = psA if m % 2 == 0 else psB
                    vector.tensor_add(y2raw[:, m * 512:(m + 1) * 512], ps[:],
                                      x1fn[:, m * 512:(m + 1) * 512]).then_inc(evC.h, 1)
                vdummy().then_inc(evCd.h, 1)
                vector.wait_ge(evC.h, 16 * (n + 1))
                vector.tensor_copy(sumb[:], y2raw[:]).then_inc(cst.h, 1)
                vdummy().then_inc(cstd.h, 1)
                ln_stats_apply(n, 1, y2raw)

        @block.sync
        def _(sync):
            for n in range(NSL):
                ln = 2 * n
                sync.wait_ge(stq.h, 5 * (ln + 1))
                sync.dma_start(out=scr[ln % 8], in_=srow[:]).then_inc(rwq.h, 16)
                sync.wait_ge(rwq.h, 16 * (ln + 1))
                sync.dma_start(out=mb[:], in_=scr[ln % 8].partition_broadcast(128)
                               ).then_inc(rbq.h, 16)
                sync.wait_ge(apq.h, 16 * (ln + 1))
                sync.wait_ge(apqd.h, ln + 1)
                sync.dma_start(
                    out=X3F.rearrange("(k p) t -> p k t", p=128)[:, :, n * 512:(n + 1) * 512],
                    in_=y1raw[:].rearrange("p (k t) -> p k t", k=16)).then_inc(stX3.h, 16)
                sync.wait_ge(rq3.h, n + 1)
                sync.wait_ge(rq3d.h, n + 1)
                sync.dma_start(out=scr2[0], in_=rrow[0:1, 1024:1536]).then_inc(rw2.h, 16)
                sync.wait_ge(rw2.h, 16 * (n + 1))
                sync.dma_start(out=mb2[:], in_=scr2[0].partition_broadcast(128)
                               ).then_inc(rb2.h, 16)
                for m in range(16):
                    if n * 16 + m >= 1:
                        sync.wait_ge(mmC.h, 16 * n + m)
                    sync.dma_start(out=wpb[:].rearrange("p (k f) -> p k f", k=64),
                                   in_=WPB[m].rearrange("k p f -> p k f")
                                   ).then_inc(ldw.h, 16)
                ln = 2 * n + 1
                sync.wait_ge(stq.h, 5 * (ln + 1))
                sync.dma_start(out=scr[ln % 8], in_=srow[:]).then_inc(rwq.h, 16)
                sync.wait_ge(rwq.h, 16 * (ln + 1))
                sync.dma_start(out=mb[:], in_=scr[ln % 8].partition_broadcast(128)
                               ).then_inc(rbq.h, 16)
                sync.wait_ge(apq.h, 16 * (ln + 1))
                sync.wait_ge(apqd.h, ln + 1)
                sync.dma_start(
                    out=OUT.rearrange("(k p) t -> p k t", p=128)[:, :, n * 512:(n + 1) * 512],
                    in_=y2raw[:].rearrange("p (k t) -> p k t", k=16)).then_inc(stX2.h, 16)
            sync.wait_ge(stX2.h, 16 * NSL)

    return nc


def run_l1_tail(inputs, x2_results, scan_results, qgu, qp):
    Wo = np.asarray(inputs["L1_attn_Wo"], np.float32)
    WOB1 = np.ascontiguousarray(
        Wo.reshape(16, 128, 16, 128).transpose(2, 0, 1, 3).astype(ml_dtypes.bfloat16))
    WGUT = np.ascontiguousarray(np.asarray(qgu).T)
    GW = WGUT.reshape(16, 128, 128, 128).transpose(2, 0, 1, 3)
    GWB1 = np.ascontiguousarray(GW[0:64])
    UWB1 = np.ascontiguousarray(GW[64:128])
    WPT = np.ascontiguousarray(np.asarray(qp).T)
    WPB1 = np.ascontiguousarray(WPT.reshape(64, 128, 16, 128).transpose(2, 0, 1, 3))
    ONE = np.ones((128, 1), dtype=ml_dtypes.bfloat16)
    ofull = {}
    for b in range(4):
        ofull[b] = np.concatenate(
            [np.asarray(scan_results[2 * b]["O1"]).T,
             np.asarray(scan_results[2 * b + 1]["O1"]).T], axis=1)
    in_maps = []
    for c in range(8):
        b, role = divmod(c, 2)
        of = np.ascontiguousarray(ofull[b][slot_tokens(role)].T)
        in_maps.append({"O": of, "X2IN": np.ascontiguousarray(x2_results[c]["X2"]),
                        "WOB": WOB1, "GWB": GWB1, "UWB": UWB1, "WPB": WPB1, "ONE": ONE})
    nc = build_l1_tail()
    res = run_bass_kernel_spmd(nc, in_maps, core_ids=list(range(8)))
    return res.results


def kernel(**inputs):
    qgu, qp = run_quant(inputs)
    proj = run_l0_proj(inputs)
    att = run_l0_attn(inputs, proj)
    tail = run_l0_tail(inputs, att)
    qkv = run_l1_qkv(inputs, tail)
    scan = run_l1_scan(inputs, qkv)
    fin = run_l1_tail(inputs, tail, scan, qgu, qp)
    out = np.zeros((B, S, D), dtype=np.float32)
    for c in range(8):
        b, role = divmod(c, 2)
        out[b, slot_tokens(role), :] = np.asarray(fin[c]["OUT"], np.float32).T
    return out


# version 03 (final)
# speedup vs baseline: 1.1023x; 1.0189x over previous
import numpy as np
import ml_dtypes
import concourse.bass as bass
import concourse.mybir as mybir
from concourse.bass_utils import run_bass_kernel_spmd

F32 = mybir.dt.float32
BF16 = mybir.dt.bfloat16
AF = mybir.ActivationFunctionType
ALU = mybir.AluOpType
AX = mybir.AxisListType

B, S, D = 4, 4096, 2048
QUADS = [[0, 3, 4, 7], [1, 2, 5, 6]]
NKT = [8, 16, 24, 32]


def slot_tokens(role):
    return np.concatenate([np.arange(512 * q, 512 * (q + 1)) for q in QUADS[role]])


class Sem:
    def __init__(self, nc, name):
        self.h = nc.alloc_semaphore(name)


def rope_host_tables():
    inv = 1.0 / (10000.0 ** (np.arange(0, 128, 2, dtype=np.float64) / 128))
    pos = np.arange(S, dtype=np.float64)
    fr = pos[:, None] * inv[None, :]
    emb = np.concatenate([fr, fr], axis=-1)
    return np.cos(emb).astype(np.float32), np.sin(emb).astype(np.float32)


def pi_matrix():
    P = np.zeros((128, 128), dtype=np.float32)
    for i in range(64):
        P[i, 64 + i] = -1.0
        P[64 + i, i] = 1.0
    return P



def build_l0_proj():
    nc = bass.Bass("TRN2", target_bir_lowering=False, debug=False)
    uT = nc.dram_tensor("uT", [D, 2048], F32, kind="ExternalInput").ap()
    WqT = nc.dram_tensor("WqT", [D, 4096], F32, kind="ExternalInput").ap()
    WkT = nc.dram_tensor("WkT", [D, 1024], F32, kind="ExternalInput").ap()
    WvT = nc.dram_tensor("WvT", [D, 1024], F32, kind="ExternalInput").ap()
    ROPE = nc.dram_tensor("ROPE", [128, 4 * 2048], F32, kind="ExternalInput").ap()
    PIT = nc.dram_tensor("PIT", [128, 128], F32, kind="ExternalInput").ap()
    Qo = nc.dram_tensor("Qo", [4096, 2048], BF16, kind="ExternalOutput").ap()
    Ko = nc.dram_tensor("Ko", [1024, 2048], BF16, kind="ExternalOutput").ap()
    Vo = nc.dram_tensor("Vo", [1024, 2048], BF16, kind="ExternalOutput").ap()

    NK = 16
    ub = nc.alloc_sbuf_tensor("ub", [128, NK * 2048], BF16)
    wt = [nc.alloc_sbuf_tensor(f"wt{i}", [128, NK * 128], BF16) for i in range(2)]
    rope = nc.alloc_sbuf_tensor("rope", [128, 4 * 2048], F32)
    pit = nc.alloc_sbuf_tensor("pit", [128, 128], BF16)
    qraw = [nc.alloc_sbuf_tensor(f"qraw{i}", [128, 512], BF16) for i in range(2)]
    t1 = nc.alloc_sbuf_tensor("t1", [128, 512], F32)
    t2 = nc.alloc_sbuf_tensor("t2", [128, 512], F32)
    dum = nc.alloc_sbuf_tensor("dum", [1, 8], F32)
    dum2 = nc.alloc_sbuf_tensor("dum2", [1, 8], F32)
    stage = [nc.alloc_sbuf_tensor(f"stage{i}", [128, 2048], BF16) for i in range(2)]
    psA = nc.alloc_psum_tensor("psA", [128, 512], F32)
    psB = nc.alloc_psum_tensor("psB", [128, 512], F32)
    psC = nc.alloc_psum_tensor("psC", [128, 512], F32)

    wld = [Sem(nc, "wld0"), Sem(nc, "wld1")]; uld = Sem(nc, "uld"); mmA = Sem(nc, "mmA")
    veq = Sem(nc, "veq"); rot = Sem(nc, "rot"); gfin = Sem(nc, "gfin")
    sfl = Sem(nc, "sfl"); sto = [Sem(nc, "sto0"), Sem(nc, "sto1")]

    MTOT = 48
    NQK = 160
    kind = lambda im: ("q" if im < 32 else ("k" if im < 40 else "v"))

    def wsrc(im):
        if im < 32:
            return WqT, im
        if im < 40:
            return WkT, im - 32
        return WvT, im - 40

    with nc.Block() as block:
        @block.gpsimd
        def _(gpsimd):
            gpsimd.dma_start(out=ub[:].rearrange("p (k t) -> p k t", k=NK),
                             in_=uT.rearrange("(k p) t -> p k t", p=128)).then_inc(uld.h, 16)
            gpsimd.dma_start(out=rope[:], in_=ROPE).then_inc(uld.h, 16)
            gpsimd.dma_start(out=pit[:], in_=PIT).then_inc(uld.h, 16)
            for im in range(MTOT):
                Wd, m = wsrc(im)
                if im >= 2:
                    gpsimd.wait_ge(mmA.h, 4 * (im - 1))
                gpsimd.dma_start(
                    out=wt[im % 2][:].rearrange("p (k f) -> p k f", k=NK),
                    in_=Wd.rearrange("(k p) f -> p k f", p=128)[:, :, m * 128:(m + 1) * 128]
                ).then_inc(wld[im % 2].h, 16)

        @block.tensor
        def _(tensor):
            tensor.wait_ge(uld.h, 48)
            for im in range(MTOT):
                tensor.wait_ge(wld[im % 2].h, 16 * (im // 2 + 1))
                for n in range(4):
                    g = im * 4 + n
                    ps = psA if n % 2 == 0 else psB
                    if g >= 2:
                        if g - 2 >= NQK:
                            tensor.wait_ge(gfin.h, g - 1)
                        else:
                            tensor.wait_ge(veq.h, g - 1)
                    inst = None
                    for k in range(NK):
                        inst = tensor.matmul(
                            ps[:],
                            wt[im % 2][:, k * 128:(k + 1) * 128],
                            ub[:, k * 2048 + n * 512: k * 2048 + (n + 1) * 512],
                            start=(k == 0), stop=(k == NK - 1))
                    inst.then_inc(mmA.h, 1)
                    gp = g - 1
                    if 0 <= gp < NQK and g <= NQK:
                        tensor.wait_ge(veq.h, g + 1)
                        if gp >= 1:
                            tensor.wait_ge(gfin.h, gp)
                        tensor.matmul(psC[:], pit[:], qraw[gp % 2][:],
                                      start=True, stop=True).then_inc(rot.h, 1)

        @block.vector
        def _(vector):
            nrope = 0

            def rope_emit(gp):
                nonlocal nrope
                imp, np_ = divmod(gp, 4)
                knd = kind(imp)
                if np_ == 0 and imp >= 2:
                    vector.wait_ge(sto[imp % 2].h, 16 * (imp // 2))
                off = 0 if knd == "q" else 2 * 2048
                cos = rope[:, off + np_ * 512: off + (np_ + 1) * 512]
                sin = rope[:, off + 2048 + np_ * 512: off + 2048 + (np_ + 1) * 512]
                vector.tensor_tensor(t1[:], qraw[gp % 2][:], cos, ALU.mult)
                nrope += 1
                vector.wait_ge(rot.h, nrope)
                vector.tensor_tensor(t2[:], psC[:], sin, ALU.mult)
                vector.tensor_add(stage[imp % 2][:, np_ * 512:(np_ + 1) * 512],
                                  t1[:], t2[:]).then_inc(gfin.h, 1)
                if np_ == 3:
                    vector.tensor_copy(dum2[0:1, 0:8], pit[0:1, 0:8]).then_inc(sfl.h, 1)

            for g in range(NQK):
                vector.wait_ge(mmA.h, g + 1)
                ps = psA if g % 2 == 0 else psB
                vector.tensor_copy(qraw[g % 2][:], ps[:]).then_inc(veq.h, 1)
                if g == NQK - 1:
                    vector.tensor_copy(dum2[0:1, 0:8], pit[0:1, 0:8]).then_inc(veq.h, 1)
                if g >= 1:
                    rope_emit(g - 1)
            rope_emit(NQK - 1)
            for g in range(NQK, 192):
                im, n = divmod(g, 4)
                if n == 0 and im >= 2:
                    vector.wait_ge(sto[im % 2].h, 16 * (im // 2))
                vector.wait_ge(mmA.h, g + 1)
                ps = psA if g % 2 == 0 else psB
                vector.tensor_copy(stage[im % 2][:, n * 512:(n + 1) * 512], ps[:]
                                   ).then_inc(gfin.h, 1)
                if n == 3:
                    vector.tensor_copy(dum2[0:1, 0:8], pit[0:1, 0:8]).then_inc(sfl.h, 1)

        @block.sync
        def _(sync):
            for im in range(MTOT):
                knd = kind(im)
                sync.wait_ge(gfin.h, 4 * (im + 1))
                sync.wait_ge(sfl.h, im + 1)
                if knd == "q":
                    dst = Qo.rearrange("(m p) t -> p m t", p=128)[:, im]
                elif knd == "k":
                    dst = Ko.rearrange("(m p) t -> p m t", p=128)[:, im - 32]
                else:
                    dst = Vo.rearrange("(m p) t -> p m t", p=128)[:, im - 40]
                sync.dma_start(out=dst, in_=stage[im % 2][:]).then_inc(sto[im % 2].h, 16)
            sync.wait_ge(sto[0].h, 16 * (MTOT // 2))
            sync.wait_ge(sto[1].h, 16 * (MTOT // 2))

    return nc



def attn_tiles():
    tiles = []
    for g in range(4):
        for kt in range(NKT[g]):
            masked = kt >= NKT[g] - 8
            j = kt - (NKT[g] - 8)
            tiles.append((g, kt, masked, j))
    assert len(tiles) == 80
    return tiles




DBG_NH = 32
DBG_NOPDUM = False


def build_l0_attn():
    nc = bass.Bass("TRN2", target_bir_lowering=False, debug=False)
    Q = nc.dram_tensor("Q", [4096, 2048], BF16, kind="ExternalInput").ap()
    K = nc.dram_tensor("K", [1024, 4096], BF16, kind="ExternalInput").ap()
    Vt = nc.dram_tensor("Vt", [128, 8 * 32 * 128], BF16, kind="ExternalInput").ap()
    MSK = nc.dram_tensor("MSK", [128, 16 * 512], BF16, kind="ExternalInput").ap()
    ONE = nc.dram_tensor("ONE", [128, 1], BF16, kind="ExternalInput").ap()
    Oo = nc.dram_tensor("Oo", [4096, 2048], BF16, kind="ExternalOutput").ap()
    scr = nc.dram_tensor("scr", [4, 2048], F32, kind="Internal").ap()

    kb = nc.alloc_sbuf_tensor("kb", [128, 8 * 4096], BF16)
    vb = nc.alloc_sbuf_tensor("vb", [128, 8 * 32 * 128], BF16)
    msk = nc.alloc_sbuf_tensor("msk", [128, 16 * 512], BF16)
    one = nc.alloc_sbuf_tensor("one", [128, 1], BF16)
    qhb = [nc.alloc_sbuf_tensor(f"qhb{i}", [128, 2048], BF16) for i in range(2)]
    pt = [nc.alloc_sbuf_tensor(f"pt{i}", [128, 512], BF16) for i in range(4)]
    ptm = [nc.alloc_sbuf_tensor(f"ptm{i}", [128, 512], BF16) for i in range(4)]
    ost = [nc.alloc_sbuf_tensor(f"ost{i}", [128, 2048], BF16) for i in range(2)]
    rs = nc.alloc_sbuf_tensor("rs", [1, 2048], F32)
    rcpr = nc.alloc_sbuf_tensor("rcpr", [1, 2048], F32)
    warm = nc.alloc_sbuf_tensor("warm", [1, 32], F32)
    dum = nc.alloc_sbuf_tensor("dum", [1, 8], F32)
    dum2 = nc.alloc_sbuf_tensor("dum2", [1, 8], F32)
    recipb = [nc.alloc_sbuf_tensor(f"recipb{i}", [128, 2048], F32) for i in range(2)]
    psS = [nc.alloc_psum_tensor(f"psS{i}", [128, 512], F32) for i in range(3)]
    psO = [nc.alloc_psum_tensor(f"psO{i}", [128, 512], F32) for i in range(4)]
    psC = [nc.alloc_psum_tensor(f"psC{i}", [1, 512], F32) for i in range(1)]

    ld = Sem(nc, "ld"); qld = [Sem(nc, "qld0"), Sem(nc, "qld1")]; smm = Sem(nc, "smm"); wse = Sem(nc, "wse")
    exq = Sem(nc, "exq"); mskd = Sem(nc, "mskd"); mskp = Sem(nc, "mskp"); avm = Sem(nc, "avm")
    rcq = Sem(nc, "rcq")
    avq = Sem(nc, "avq"); cse = Sem(nc, "cse"); rcp = Sem(nc, "rcp")
    rw = Sem(nc, "rw"); rb = Sem(nc, "rb"); oev = Sem(nc, "oev")
    sto = [Sem(nc, "sto0"), Sem(nc, "sto1")]

    NH = DBG_NH
    tiles = attn_tiles()
    mreal_of = {}
    mreal = 0
    for h in range(NH):
        for i, (g, kt, masked, j) in enumerate(tiles):
            if masked:
                mreal_of[(h, i)] = mreal
                mreal += 1

    with nc.Block() as block:
        @block.gpsimd
        def _(gpsimd):
            gpsimd.dma_start(out=kb[:].rearrange("p (h t) -> p h t", h=8),
                             in_=K.rearrange("(h p) t -> p h t", p=128)).then_inc(ld.h, 16)
            gpsimd.dma_start(out=vb[:], in_=Vt).then_inc(ld.h, 16)
            gpsimd.dma_start(out=msk[:], in_=MSK).then_inc(ld.h, 16)
            gpsimd.dma_start(out=one[:], in_=ONE).then_inc(ld.h, 16)
            for h in range(NH):
                if h >= 2:
                    gpsimd.wait_ge(avq.h, (h - 1) * 4)
                gpsimd.dma_start(out=qhb[h % 2][:],
                                 in_=Q.rearrange("(m p) t -> p m t", p=128)[:, h]
                                 ).then_inc(qld[h % 2].h, 16)

        @block.tensor
        def _(tensor):
            tensor.wait_ge(ld.h, 64)
            LA = 3
            for h in range(NH):
                kvh = h // 4
                tensor.wait_ge(qld[h % 2].h, 16 * (h // 2 + 1))
                for i in range(80 + LA):
                    if i < 80:
                        g, kt, masked, j = tiles[i]
                        tot = h * 80 + i
                        if tot >= LA:
                            tp = tot - LA
                            tensor.wait_ge(exq.h, 81 * (tp // 80) + (tp % 80) + 1)
                        tensor.matmul(psS[tot % 3][:],
                                      kb[:, kvh * 4096 + kt * 128: kvh * 4096 + (kt + 1) * 128],
                                      qhb[h % 2][:, g * 512:(g + 1) * 512],
                                      start=True, stop=True).then_inc(smm.h, 1)
                    if i >= LA:
                        g, kt, masked, j = tiles[i - LA]
                        tot = h * 80 + i - LA
                        gi = h * 4 + g
                        if masked:
                            tensor.wait_ge(mskp.h, mreal_of[(h, i - LA)] + 1)
                            tensor.wait_ge(mskd.h, mreal_of[(h, i - LA)] + 1)
                            src = ptm[mreal_of[(h, i - LA)] % 4]
                        else:
                            tensor.wait_ge(exq.h, 81 * h + min((i - LA) + 2, 81))
                            src = pt[tot % 4]
                        if kt == 0 and h >= 1:
                            tensor.wait_ge(oev.h, 5 * h)
                        av = tensor.matmul(psO[g][:],
                                           vb[:, (kvh * 32 + kt) * 128:(kvh * 32 + kt + 1) * 128],
                                           src[:], start=(kt == 0), stop=(kt == NKT[g] - 1))
                        if masked:
                            av.then_inc(avm.h, 1)
                        if kt == 0 and gi >= 1:
                            tensor.wait_ge(cse.h, gi)
                        ci = tensor.matmul(psC[0][:], one[:], src[:],
                                           start=(kt == 0), stop=(kt == NKT[g] - 1))
                        if kt == NKT[g] - 1:
                            ci.then_inc(avq.h, 1)

        @block.scalar
        def _(scalar):
            wn = [0]

            def wdummy():
                scalar.wait_ge(wse.h, wn[0])
                wn[0] += 1
                if DBG_NOPDUM:
                    return scalar.nop().then_inc(wse.h, 1)
                return scalar.activation(warm[:, 0:16], warm[:, 16:32], AF.Exp
                                         ).then_inc(wse.h, 1)

            scalar.memzero(warm[:]).then_inc(wse.h, 1)
            wn[0] = 1
            wdummy()
            wdummy()
            for h in range(NH):
                for i in range(80):
                    tot = h * 80 + i
                    scalar.wait_ge(smm.h, tot + 1)
                    if tot >= 4:
                        h2, i2 = divmod(tot - 4, 80)
                        if tiles[i2][2]:
                            scalar.wait_ge(mskp.h, mreal_of[(h2, i2)] + 1)
                    scalar.activation(pt[tot % 4][:], psS[tot % 3][:], AF.Exp
                                      ).then_inc(exq.h, 1)
                scalar.wait_ge(wse.h, 3)
                scalar.wait_ge(exq.h, 81 * h + 80)
                if DBG_NOPDUM:
                    scalar.nop().then_inc(exq.h, 1)
                else:
                    scalar.activation(warm[:, 0:16], warm[:, 16:32], AF.Exp
                                      ).then_inc(exq.h, 1)

        @block.vector
        def _(vector):
            for h in range(NH):
                for i, (g, kt, masked, j) in enumerate(tiles):
                    tot = h * 80 + i
                    if masked:
                        mr = mreal_of[(h, i)]
                        vector.wait_ge(exq.h, 81 * h + min(i + 2, 81))
                        if mr >= 4:
                            vector.wait_ge(avm.h, mr - 3)
                        vector.tensor_tensor(ptm[mr % 4][:], pt[tot % 4][:],
                                             msk[:, ((g % 2) * 8 + j) * 512:((g % 2) * 8 + j + 1) * 512],
                                             ALU.mult).then_inc(mskp.h, 1)
                        (vector.nop() if DBG_NOPDUM else vector.tensor_copy(dum2[0:1, 0:1], one[0:1, 0:1])).then_inc(mskd.h, 1)
                    if kt == NKT[g] - 1:
                        gi = h * 4 + g
                        vector.wait_ge(avq.h, gi + 1)
                        vector.tensor_copy(rs[:, g * 512:(g + 1) * 512],
                                           psC[0][:]).then_inc(cse.h, 1)
                vector.wait_ge(cse.h, 4 * (h + 1))
                if h >= 1:
                    vector.wait_ge(rw.h, 16 * h)
                vector.reciprocal(rcpr[:], rs[:]).then_inc(rcq.h, 1)
                (vector.nop() if DBG_NOPDUM else vector.tensor_copy(dum2[0:1, 0:1], one[0:1, 0:1])).then_inc(rcp.h, 1)
                if h >= 2:
                    vector.wait_ge(sto[h % 2].h, 16 * (h // 2))
                vector.wait_ge(rb.h, 16 * (h + 1))
                for g in range(4):
                    vector.tensor_tensor(ost[h % 2][:, g * 512:(g + 1) * 512],
                                         psO[g][:], recipb[h % 2][:, g * 512:(g + 1) * 512],
                                         ALU.mult).then_inc(oev.h, 1)
                (vector.nop() if DBG_NOPDUM else vector.tensor_copy(dum2[0:1, 0:1], one[0:1, 0:1])).then_inc(oev.h, 1)

        @block.sync
        def _(sync):
            for h in range(NH):
                sync.wait_ge(rcq.h, h + 1)
                sync.wait_ge(rcp.h, h + 1)
                sync.dma_start(out=scr[h % 4], in_=rcpr[:]).then_inc(rw.h, 16)
                sync.wait_ge(rw.h, 16 * (h + 1))
                if h >= 2:
                    sync.wait_ge(oev.h, 5 * (h - 1))
                sync.dma_start(out=recipb[h % 2][:], in_=scr[h % 4].partition_broadcast(128)
                               ).then_inc(rb.h, 16)
                sync.wait_ge(oev.h, 5 * (h + 1))
                sync.dma_start(out=Oo.rearrange("(m p) t -> p m t", p=128)[:, h],
                               in_=ost[h % 2][:]).then_inc(sto[h % 2].h, 16)
            sync.wait_ge(sto[0].h, 16 * ((NH + 1) // 2))
            sync.wait_ge(sto[1].h, 16 * (NH // 2))

    return nc




DBG_NSL = 4


def build_l0_tail():
    nc = bass.Bass("TRN2", target_bir_lowering=False, debug=False)
    O = nc.dram_tensor("O", [4096, 2048], BF16, kind="ExternalInput").ap()
    WOB = nc.dram_tensor("WOB", [16, 32, 128, 128], BF16, kind="ExternalInput").ap()
    UT = nc.dram_tensor("UT", [D, 2048], F32, kind="ExternalInput").ap()
    GWB = nc.dram_tensor("GWB", [44, 16, 128, 128], BF16, kind="ExternalInput").ap()
    UWB = nc.dram_tensor("UWB", [44, 16, 128, 128], BF16, kind="ExternalInput").ap()
    DWB = nc.dram_tensor("DWB", [16, 44, 128, 128], BF16, kind="ExternalInput").ap()
    GS = nc.dram_tensor("GS", [128, 44], F32, kind="ExternalInput").ap()
    US = nc.dram_tensor("US", [128, 44], F32, kind="ExternalInput").ap()
    DS = nc.dram_tensor("DS", [128, 16], F32, kind="ExternalInput").ap()
    ONE = nc.dram_tensor("ONE", [128, 1], BF16, kind="ExternalInput").ap()
    X2 = nc.dram_tensor("X2", [D, 2048], F32, kind="ExternalOutput").ap()
    X1F = nc.dram_tensor("X1F", [D, 2048], F32, kind="Internal").ap()
    scr = nc.dram_tensor("scr", [8, 1024], F32, kind="Internal").ap()

    NSL = DBG_NSL

    onb = nc.alloc_sbuf_tensor("onb", [128, 32 * 512], BF16)
    wob = [nc.alloc_sbuf_tensor("wob0", [128, 32 * 128], BF16)] * 2
    y1raw = nc.alloc_sbuf_tensor("y1raw", [128, 16 * 512], F32)
    x1bn = nc.alloc_sbuf_tensor("x1bn", [128, 16 * 512], BF16)
    gwb = [nc.alloc_sbuf_tensor(f"gwb{i}", [128, 16 * 128], BF16) for i in range(2)]
    uwb = [nc.alloc_sbuf_tensor(f"uwb{i}", [128, 16 * 128], BF16) for i in range(2)]
    dwb = [nc.alloc_sbuf_tensor("dwb0", [128, 44 * 128], BF16)] * 2
    hdn = nc.alloc_sbuf_tensor("hdn", [128, 44 * 512], BF16)
    sumb = x1bn
    sqb = hdn
    gm = nc.alloc_sbuf_tensor("gm", [128, 512], BF16)
    um = nc.alloc_sbuf_tensor("um", [128, 512], BF16)
    x1fn = nc.alloc_sbuf_tensor("x1fn", [128, 16 * 512], F32)
    utn = x1fn
    y2raw = y1raw
    gs = nc.alloc_sbuf_tensor("gs", [128, 44], F32)
    us = nc.alloc_sbuf_tensor("us", [128, 44], F32)
    ds = nc.alloc_sbuf_tensor("ds", [128, 16], F32)
    one = nc.alloc_sbuf_tensor("one", [128, 1], BF16)
    mb = nc.alloc_sbuf_tensor("mb", [128, 1024], F32)
    srow = x1fn[0:1, 0:1024]
    tmps = onb[:, 0:1024].bitcast(F32)
    tmp5 = mb[0:1, 0:512]
    tmp6 = mb[0:1, 512:1024]
    warm = nc.alloc_sbuf_tensor("warm", [1, 32], F32)
    dum2 = nc.alloc_sbuf_tensor("dum2", [1, 8], F32)
    psA = nc.alloc_psum_tensor("psA", [128, 512], F32)
    psB = nc.alloc_psum_tensor("psB", [128, 512], F32)
    psM = nc.alloc_psum_tensor("psM", [1, 512], F32)
    psQ = nc.alloc_psum_tensor("psQ", [1, 512], F32)

    lres = Sem(nc, "lres")
    lon = Sem(nc, "lon"); lut = Sem(nc, "lut"); lx1 = Sem(nc, "lx1")
    lwm = [Sem(nc, "lwm0"), Sem(nc, "lwm1")]
    lgw = [Sem(nc, "lgw0"), Sem(nc, "lgw1")]
    ldw = [Sem(nc, "ldw0"), Sem(nc, "ldw1")]
    mmA = Sem(nc, "mmA"); mmB = Sem(nc, "mmB"); mmC = Sem(nc, "mmC"); mmS = Sem(nc, "mmS")
    evA = Sem(nc, "evA"); evAd = Sem(nc, "evAd")
    seB = Sem(nc, "seB"); evB = Sem(nc, "evB"); evBd = Sem(nc, "evBd")
    evC = Sem(nc, "evC"); evCd = Sem(nc, "evCd")
    cst = Sem(nc, "cst"); cstd = Sem(nc, "cstd"); vch = Sem(nc, "vch")
    sqq = Sem(nc, "sqq"); wse = Sem(nc, "wse")
    stq = Sem(nc, "stq"); rwq = Sem(nc, "rwq"); rbq = Sem(nc, "rbq")
    apq = Sem(nc, "apq"); apqd = Sem(nc, "apqd")
    stX1 = Sem(nc, "stX1"); stX2 = Sem(nc, "stX2")

    with nc.Block() as block:
        @block.gpsimd
        def _(gpsimd):
            gpsimd.dma_start(out=gs[:], in_=GS).then_inc(lres.h, 16)
            gpsimd.dma_start(out=us[:], in_=US).then_inc(lres.h, 16)
            gpsimd.dma_start(out=ds[:], in_=DS).then_inc(lres.h, 16)
            gpsimd.dma_start(out=one[:], in_=ONE).then_inc(lres.h, 16)
            for n in range(NSL):
                if n >= 1:
                    gpsimd.wait_ge(stX2.h, 16 * n)
                gpsimd.dma_start(out=onb[:].rearrange("p (k t) -> p k t", k=32),
                                 in_=O.rearrange("(k p) t -> p k t", p=128)[:, :, n * 512:(n + 1) * 512]
                                 ).then_inc(lon.h, 16)
                gpsimd.dma_start(out=utn[:].rearrange("p (k t) -> p k t", k=16),
                                 in_=UT.rearrange("(k p) t -> p k t", p=128)[:, :, n * 512:(n + 1) * 512]
                                 ).then_inc(lut.h, 16)
                for m in range(16):
                    if n * 16 + m >= 1:
                        gpsimd.wait_ge(mmA.h, 16 * n + m)
                    gpsimd.dma_start(out=wob[0][:].rearrange("p (k f) -> p k f", k=32),
                                     in_=WOB[m].rearrange("k p f -> p k f")
                                     ).then_inc(lwm[0].h, 16)
                for m in range(44):
                    if n * 44 + m >= 2:
                        gpsimd.wait_ge(mmB.h, 2 * (44 * n + m - 1))
                    gpsimd.dma_start(out=gwb[m % 2][:].rearrange("p (k f) -> p k f", k=16),
                                     in_=GWB[m].rearrange("k p f -> p k f")
                                     ).then_inc(lgw[m % 2].h, 16)
                    gpsimd.dma_start(out=uwb[m % 2][:].rearrange("p (k f) -> p k f", k=16),
                                     in_=UWB[m].rearrange("k p f -> p k f")
                                     ).then_inc(lgw[m % 2].h, 16)
                gpsimd.wait_ge(stX1.h, 16 * (n + 1))
                gpsimd.dma_start(out=x1fn[:].rearrange("p (k t) -> p k t", k=16),
                                 in_=X1F.rearrange("(k p) t -> p k t", p=128)[:, :, n * 512:(n + 1) * 512]
                                 ).then_inc(lx1.h, 16)

        @block.tensor
        def _(tensor):
            tensor.wait_ge(lres.h, 64)
            for n in range(NSL):
                tensor.wait_ge(lon.h, 16 * (n + 1))
                for m in range(16):
                    tensor.wait_ge(lwm[0].h, 16 * (16 * n + m + 1))
                    if m >= 2:
                        tensor.wait_ge(evA.h, 16 * n + m - 1)
                    inst = None
                    for k in range(32):
                        ps = psA if m % 2 == 0 else psB
                        inst = tensor.matmul(ps[:], wob[m % 2][:, k * 128:(k + 1) * 128],
                                             onb[:, k * 512:(k + 1) * 512],
                                             start=(k == 0), stop=(k == 31))
                    inst.then_inc(mmA.h, 1)
                ln = 2 * n
                tensor.wait_ge(cst.h, ln + 1)
                tensor.wait_ge(cstd.h, ln + 1)
                tensor.wait_ge(sqq.h, ln + 1)
                tensor.wait_ge(wse.h, 3 + 2 * ln + 1)
                inst = None
                for k in range(16):
                    inst = tensor.matmul(psM[:], one[:], sumb[:, k * 512:(k + 1) * 512],
                                         start=(k == 0), stop=(k == 15))
                inst.then_inc(mmS.h, 1)
                inst = None
                for k in range(16):
                    inst = tensor.matmul(psQ[:], one[:], sqb[:, k * 512:(k + 1) * 512],
                                         start=(k == 0), stop=(k == 15))
                inst.then_inc(mmS.h, 1)
                tensor.wait_ge(apq.h, 16 * (ln + 1))
                tensor.wait_ge(apqd.h, ln + 1)
                for m in range(44):
                    tensor.wait_ge(lgw[m % 2].h, 32 * ((44 * n + m) // 2 + 1))
                    if m >= 1:
                        tensor.wait_ge(seB.h, 44 * n + m)
                    if m >= 1:
                        tensor.wait_ge(evB.h, 2 * (44 * n + m) - 1)
                    inst = None
                    for k in range(16):
                        inst = tensor.matmul(psA[:], gwb[m % 2][:, k * 128:(k + 1) * 128],
                                             x1bn[:, k * 512:(k + 1) * 512],
                                             start=(k == 0), stop=(k == 15))
                    inst.then_inc(mmB.h, 1)
                    inst = None
                    for k in range(16):
                        inst = tensor.matmul(psB[:], uwb[m % 2][:, k * 128:(k + 1) * 128],
                                             x1bn[:, k * 512:(k + 1) * 512],
                                             start=(k == 0), stop=(k == 15))
                    inst.then_inc(mmB.h, 1)
                tensor.wait_ge(evB.h, 88 * n + 88)
                tensor.wait_ge(evBd.h, n + 1)
                for m in range(16):
                    tensor.wait_ge(ldw[0].h, 16 * (16 * n + m + 1))
                    if m >= 2:
                        tensor.wait_ge(evC.h, 16 * n + m - 1)
                    inst = None
                    for k in range(44):
                        ps = psA if m % 2 == 0 else psB
                        inst = tensor.matmul(ps[:], dwb[m % 2][:, k * 128:(k + 1) * 128],
                                             hdn[:, k * 512:(k + 1) * 512],
                                             start=(k == 0), stop=(k == 43))
                    inst.then_inc(mmC.h, 1)
                ln = 2 * n + 1
                tensor.wait_ge(cst.h, ln + 1)
                tensor.wait_ge(cstd.h, ln + 1)
                tensor.wait_ge(sqq.h, ln + 1)
                tensor.wait_ge(wse.h, 3 + 2 * ln + 1)
                inst = None
                for k in range(16):
                    inst = tensor.matmul(psM[:], one[:], sumb[:, k * 512:(k + 1) * 512],
                                         start=(k == 0), stop=(k == 15))
                inst.then_inc(mmS.h, 1)
                inst = None
                for k in range(16):
                    inst = tensor.matmul(psQ[:], one[:], sqb[:, k * 512:(k + 1) * 512],
                                         start=(k == 0), stop=(k == 15))
                inst.then_inc(mmS.h, 1)

        @block.scalar
        def _(scalar):
            wn = [0]

            def wdummy():
                scalar.wait_ge(wse.h, wn[0])
                wn[0] += 1
                return scalar.activation(warm[:, 0:16], warm[:, 16:32], AF.Exp
                                         ).then_inc(wse.h, 1)

            scalar.memzero(warm[:]).then_inc(wse.h, 1)
            wn[0] = 1
            wdummy()
            wdummy()
            scalar.wait_ge(lres.h, 64)
            for n in range(NSL):
                for ph in range(2):
                    ln = 2 * n + ph
                    if ph == 1:
                        for m in range(44):
                            scalar.wait_ge(mmB.h, 2 * (44 * n + m) + 1)
                            scalar.wait_ge(evB.h, 2 * (44 * n + m))
                            scalar.activation(gm[:], psA[:], AF.Silu, scale=gs[:, m:m + 1]
                                              ).then_inc(seB.h, 1)
                    src_ = y1raw if ph == 0 else y2raw
                    if ph == 0:
                        scalar.wait_ge(evA.h, 16 * (n + 1))
                        scalar.wait_ge(evAd.h, n + 1)
                    else:
                        scalar.wait_ge(evC.h, 16 * (n + 1))
                        scalar.wait_ge(evCd.h, n + 1)
                    scalar.activation(sqb[:, 0:8192], src_[:], AF.Square).then_inc(sqq.h, 1)
                    wdummy()
                    scalar.wait_ge(stq.h, 5 * ln + 2)
                    scalar.activation(tmp6, tmp5, AF.Sqrt).then_inc(stq.h, 1)
                    wdummy()

        @block.vector
        def _(vector):
            vn = [0]

            def vdummy():
                if DBG_NOPDUM:
                    return vector.nop()
                return vector.tensor_copy(dum2[0:1, 0:1], one[0:1, 0:1])

            vector.wait_ge(lres.h, 64)
            for n in range(NSL):
                for ph in range(2):
                    ln = 2 * n + ph
                    if ph == 0:
                        vector.wait_ge(lut.h, 16 * (n + 1))
                        for m in range(16):
                            vector.wait_ge(mmA.h, 16 * n + m + 1)
                            ps = psA if m % 2 == 0 else psB
                            vector.tensor_add(y1raw[:, m * 512:(m + 1) * 512], ps[:],
                                              utn[:, m * 512:(m + 1) * 512]).then_inc(evA.h, 1)
                        vdummy().then_inc(evAd.h, 1)
                        src_ = y1raw
                    else:
                        for m in range(44):
                            vector.wait_ge(mmB.h, 2 * (44 * n + m) + 2)
                            vector.wait_ge(evB.h, 2 * (44 * n + m))
                            vector.tensor_scalar(um[:], psB[:], us[:, m:m + 1], None, ALU.mult
                                                 ).then_inc(evB.h, 1)
                            vector.wait_ge(seB.h, 44 * n + m + 1)
                            vector.wait_ge(evB.h, 2 * (44 * n + m) + 1)
                            vector.tensor_tensor(hdn[:, m * 512:(m + 1) * 512], gm[:], um[:],
                                                 ALU.mult).then_inc(evB.h, 1)
                        vdummy().then_inc(evBd.h, 1)
                        vector.wait_ge(lx1.h, 16 * (n + 1))
                        for m in range(16):
                            vector.wait_ge(mmC.h, 16 * n + m + 1)
                            ps = psA if m % 2 == 0 else psB
                            vector.scalar_tensor_tensor(y2raw[:, m * 512:(m + 1) * 512], ps[:],
                                                        ds[:, m:m + 1], x1fn[:, m * 512:(m + 1) * 512],
                                                        ALU.mult, ALU.add).then_inc(evC.h, 1)
                        vdummy().then_inc(evCd.h, 1)
                        src_ = y2raw
                    if ph == 0:
                        vector.wait_ge(evA.h, 16 * (n + 1))
                    else:
                        vector.wait_ge(evC.h, 16 * (n + 1))
                    vector.tensor_copy(sumb[:], src_[:]).then_inc(cst.h, 1)
                    vdummy().then_inc(cstd.h, 1)
                    vector.wait_ge(mmS.h, 2 * ln + 2)
                    vector.tensor_scalar(srow[:, 0:512], psM[:], 1.0 / 2048.0, None, ALU.mult
                                         ).then_inc(vch.h, 1); vn[0] += 1
                    s1 = vn[0]
                    vector.tensor_scalar(tmp5, psQ[:], 1.0 / 2048.0, None, ALU.mult
                                         ).then_inc(vch.h, 1); vn[0] += 1
                    vector.wait_ge(vch.h, s1)
                    vector.tensor_tensor(tmp6, srow[:, 0:512], srow[:, 0:512], ALU.mult
                                         ).then_inc(vch.h, 1); vn[0] += 1
                    vector.wait_ge(vch.h, vn[0])
                    vector.tensor_sub(tmp5, tmp5, tmp6).then_inc(vch.h, 1); vn[0] += 1
                    vector.wait_ge(vch.h, vn[0])
                    vector.tensor_scalar(tmp5, tmp5, 1e-5, None, ALU.add).then_inc(stq.h, 1)
                    vdummy().then_inc(stq.h, 1)
                    vector.wait_ge(stq.h, 5 * ln + 3)
                    vector.wait_ge(wse.h, 3 + 2 * ln + 2)
                    vector.reciprocal(srow[:, 512:1024], tmp6).then_inc(stq.h, 1)
                    vdummy().then_inc(stq.h, 1)
                    vector.wait_ge(rbq.h, 16 * (ln + 1))
                    for m in range(16):
                        vector.wait_ge(vch.h, vn[0])
                        if ph == 1 and m >= 1:
                            vector.wait_ge(apq.h, 16 * ln + m)
                        vector.tensor_sub(tmps, src_[:, m * 512:(m + 1) * 512], mb[:, 0:512]
                                          ).then_inc(vch.h, 1); vn[0] += 1
                        vector.wait_ge(vch.h, vn[0])
                        if ph == 0:
                            vector.tensor_mul(y1raw[:, m * 512:(m + 1) * 512], tmps, mb[:, 512:1024]
                                              ).then_inc(vch.h, 1); vn[0] += 1
                            vector.wait_ge(vch.h, vn[0])
                            vector.tensor_copy(x1bn[:, m * 512:(m + 1) * 512],
                                               y1raw[:, m * 512:(m + 1) * 512]).then_inc(apq.h, 1)
                        else:
                            vector.tensor_mul(y2raw[:, m * 512:(m + 1) * 512], tmps,
                                              mb[:, 512:1024]).then_inc(apq.h, 1)
                    vdummy().then_inc(apqd.h, 1)

        @block.sync
        def _(sync):
            for n in range(NSL):
                for ph in range(2):
                    ln = 2 * n + ph
                    if ph == 1:
                        for m in range(16):
                            if n * 16 + m >= 1:
                                sync.wait_ge(mmC.h, 16 * n + m)
                            sync.dma_start(out=dwb[0][:].rearrange("p (k f) -> p k f", k=44),
                                           in_=DWB[m].rearrange("k p f -> p k f")
                                           ).then_inc(ldw[0].h, 16)
                    sync.wait_ge(stq.h, 5 * (ln + 1))
                    sync.dma_start(out=scr[ln % 8], in_=srow[:]).then_inc(rwq.h, 16)
                    sync.wait_ge(rwq.h, 16 * (ln + 1))
                    sync.dma_start(out=mb[:], in_=scr[ln % 8].partition_broadcast(128)
                                   ).then_inc(rbq.h, 16)
                    sync.wait_ge(apq.h, 16 * (ln + 1))
                    sync.wait_ge(apqd.h, ln + 1)
                    if ph == 0:
                        sync.dma_start(
                            out=X1F.rearrange("(k p) t -> p k t", p=128)[:, :, n * 512:(n + 1) * 512],
                            in_=y1raw[:].rearrange("p (k t) -> p k t", k=16)).then_inc(stX1.h, 16)
                    else:
                        sync.dma_start(
                            out=X2.rearrange("(k p) t -> p k t", p=128)[:, :, n * 512:(n + 1) * 512],
                            in_=y2raw[:].rearrange("p (k t) -> p k t", k=16)).then_inc(stX2.h, 16)
            sync.wait_ge(stX2.h, 16 * NSL)

    return nc


def host_masks(role):
    r = np.arange(128)[:, None]
    c = np.arange(512)[None, :]
    M = [(c >= 128 * v + r).astype(np.float32) for v in range(4)]
    ones = np.ones((128, 512), dtype=np.float32)
    zeros = np.zeros((128, 512), dtype=np.float32)
    larger = [ones] * 4 + M
    smaller = M + [zeros] * 4
    out = []
    for par in (0, 1):
        q = QUADS[role][par]
        out += larger if q == 2 * par + 1 else smaller
    return np.concatenate(out, axis=1).astype(ml_dtypes.bfloat16)


def run_l0_proj(inputs):
    u = inputs["L0_attn_u"]
    W_in = inputs["L0_attn_W_in"]
    WqT = np.ascontiguousarray(W_in[6144:10240].T)
    WkT = np.ascontiguousarray(W_in[5120:6144].T)
    WvT = np.ascontiguousarray(W_in[4096:5120].T)
    cos, sin = rope_host_tables()
    s = np.float32(1.0 / np.sqrt(128.0))
    PIT = np.ascontiguousarray(pi_matrix().T)
    in_maps = []
    for c in range(8):
        b, r = divmod(c, 2)
        st = slot_tokens(r)
        uT = np.ascontiguousarray(u[b][st].T)
        ROPE = np.concatenate([
            np.ascontiguousarray((cos[st] * s).T),
            np.ascontiguousarray((sin[st] * s).T),
            np.ascontiguousarray(cos[st].T),
            np.ascontiguousarray(sin[st].T)], axis=1).astype(np.float32)
        in_maps.append({"uT": uT, "WqT": WqT, "WkT": WkT, "WvT": WvT,
                        "ROPE": np.ascontiguousarray(ROPE), "PIT": PIT})
    nc = build_l0_proj()
    res = run_bass_kernel_spmd(nc, in_maps, core_ids=list(range(8)))
    return res.results


def run_l0_attn(inputs, proj):
    in_maps = []
    ONE = np.ones((128, 1), dtype=ml_dtypes.bfloat16)
    for c in range(8):
        b, r = divmod(c, 2)
        p0, p1 = 2 * b, 2 * b + 1
        Knat = np.zeros((1024, 4096), dtype=ml_dtypes.bfloat16)
        Vnat = np.zeros((1024, 4096), dtype=ml_dtypes.bfloat16)
        Knat[:, slot_tokens(0)] = proj[p0]["Ko"]
        Knat[:, slot_tokens(1)] = proj[p1]["Ko"]
        Vnat[:, slot_tokens(0)] = proj[p0]["Vo"]
        Vnat[:, slot_tokens(1)] = proj[p1]["Vo"]
        Vt = np.ascontiguousarray(
            Vnat.reshape(8, 128, 32, 128).transpose(3, 0, 2, 1).reshape(128, 8 * 32 * 128))
        in_maps.append({"Q": proj[c]["Qo"], "K": np.ascontiguousarray(Knat),
                        "Vt": Vt, "MSK": host_masks(r), "ONE": ONE})
    nc = build_l0_attn()
    res = run_bass_kernel_spmd(nc, in_maps, core_ids=list(range(8)))
    return res.results




def run_l0_tail(inputs, att):
    W_out = inputs["L0_attn_W_out"]
    WoutT = np.ascontiguousarray(W_out.T)
    WOB = np.ascontiguousarray(
        WoutT.reshape(32, 128, 16, 128).transpose(2, 0, 1, 3).astype(ml_dtypes.bfloat16))
    gwT = inputs["L0_ffn_gate_w"].T
    uwT = inputs["L0_ffn_up_w"].T
    dwT = inputs["L0_ffn_down_w"].T
    GWB = np.ascontiguousarray(
        gwT.reshape(16, 128, 44, 128).transpose(2, 0, 1, 3).astype(ml_dtypes.bfloat16))
    UWB = np.ascontiguousarray(
        uwT.reshape(16, 128, 44, 128).transpose(2, 0, 1, 3).astype(ml_dtypes.bfloat16))
    DWB = np.ascontiguousarray(
        dwT.reshape(44, 128, 16, 128).transpose(2, 0, 1, 3).astype(ml_dtypes.bfloat16))
    GS = np.ascontiguousarray(inputs["L0_ffn_gate_s"].reshape(44, 128).T)
    US = np.ascontiguousarray(inputs["L0_ffn_up_s"].reshape(44, 128).T)
    DS = np.ascontiguousarray(inputs["L0_ffn_down_s"].reshape(16, 128).T)
    ONE = np.ones((128, 1), dtype=ml_dtypes.bfloat16)
    u = inputs["L0_attn_u"]
    in_maps = []
    for c in range(8):
        b, r = divmod(c, 2)
        st = slot_tokens(r)
        uT = np.ascontiguousarray(u[b][st].T)
        in_maps.append({"O": att[c]["Oo"], "WOB": WOB, "UT": uT, "GWB": GWB,
                        "UWB": UWB, "DWB": DWB, "GS": GS, "US": US, "DS": DS,
                        "ONE": ONE})
    nc = build_l0_tail()
    res = run_bass_kernel_spmd(nc, in_maps, core_ids=list(range(8)))
    return res.results





def build_quant():
    nc = bass.Bass("TRN2", target_bir_lowering=False, debug=False)
    WGU = nc.dram_tensor("WGU", [2048, 2048], F32, kind="ExternalInput").ap()
    WP = nc.dram_tensor("WP", [256, 8192], F32, kind="ExternalInput").ap()
    QGU = nc.dram_tensor("QGU", [2048, 2048], BF16, kind="ExternalOutput").ap()
    QP = nc.dram_tensor("QP", [256, 8192], BF16, kind="ExternalOutput").ap()

    FMAX = 8192
    wb = nc.alloc_sbuf_tensor("wb", [128, FMAX], BF16)
    aw = nc.alloc_sbuf_tensor("aw", [128, FMAX], F32)
    qv = nc.alloc_sbuf_tensor("qv", [128, FMAX], F32)
    tb = nc.alloc_sbuf_tensor("tb", [128, FMAX], F32)
    rv = nc.alloc_sbuf_tensor("rv", [128, FMAX], BF16)
    whb = nc.alloc_sbuf_tensor("whb", [128, FMAX], BF16)
    outb = nc.alloc_sbuf_tensor("outb", [128, FMAX], BF16)
    gsum = nc.alloc_sbuf_tensor("gsum", [128, FMAX // 64], F32)
    scb = nc.alloc_sbuf_tensor("scb", [128, FMAX // 64], BF16)
    scf = nc.alloc_sbuf_tensor("scf", [128, FMAX // 64], F32)

    lq = Sem(nc, "lq"); qch = Sem(nc, "qch"); stq = Sem(nc, "stq")

    tiles = [("WGU", i, 2048) for i in range(16)] + [("WP", i, 8192) for i in range(2)]

    with nc.Block() as block:
        @block.gpsimd
        def _(gpsimd):
            for ti, (src, i, F) in enumerate(tiles):
                if ti >= 1:
                    gpsimd.wait_ge(stq.h, 16 * ti)
                d = WGU if src == "WGU" else WP
                gpsimd.dma_start(out=wb[:, 0:F], in_=d[i * 128:(i + 1) * 128, :]
                                 ).then_inc(lq.h, 16)

        @block.vector
        def _(vector):
            qn = [0]

            def op(inst):
                inst.then_inc(qch.h, 1)
                qn[0] += 1

            for ti, (src, i, F) in enumerate(tiles):
                G = F // 64
                vector.wait_ge(lq.h, 16 * (ti + 1))
                op(vector.tensor_scalar(tb[:, 0:F], wb[:, 0:F], -1.0, None, ALU.mult))
                vector.wait_ge(qch.h, qn[0])
                op(vector.tensor_tensor(aw[:, 0:F], wb[:, 0:F], tb[:, 0:F], ALU.max))
                vector.wait_ge(qch.h, qn[0])
                op(vector.reduce_sum(gsum[:, 0:G].unsqueeze(2),
                                     aw[:, 0:F].rearrange("p (g e) -> p g e", e=64),
                                     axis=AX.X))
                vector.wait_ge(qch.h, qn[0])
                op(vector.tensor_scalar(scf[:, 0:G], gsum[:, 0:G], 1.0 / 64.0, None, ALU.mult))
                vector.wait_ge(qch.h, qn[0])
                op(vector.tensor_scalar(scf[:, 0:G], scf[:, 0:G], 1e-8, None, ALU.max))
                vector.wait_ge(qch.h, qn[0])
                op(vector.tensor_copy(scb[:, 0:G], scf[:, 0:G]))
                vector.wait_ge(qch.h, qn[0])
                op(vector.tensor_copy(scf[:, 0:G], scb[:, 0:G]))
                vector.wait_ge(qch.h, qn[0])
                op(vector.reciprocal(gsum[:, 0:G], scf[:, 0:G]))
                vector.wait_ge(qch.h, qn[0])
                op(vector.tensor_tensor(rv[:, 0:F].rearrange("p (g e) -> p g e", e=64),
                                        wb[:, 0:F].rearrange("p (g e) -> p g e", e=64),
                                        gsum[:, 0:G].unsqueeze(2).to_broadcast([128, G, 64]),
                                        ALU.mult))
                vector.wait_ge(qch.h, qn[0])
                op(vector.tensor_scalar(qv[:, 0:F], rv[:, 0:F], 0.5, None, ALU.is_gt))
                vector.wait_ge(qch.h, qn[0])
                op(vector.tensor_scalar(tb[:, 0:F], rv[:, 0:F], -0.5, None, ALU.is_lt))
                vector.wait_ge(qch.h, qn[0])
                op(vector.tensor_sub(qv[:, 0:F], qv[:, 0:F], tb[:, 0:F]))
                vector.wait_ge(qch.h, qn[0])
                op(vector.tensor_tensor(whb[:, 0:F].rearrange("p (g e) -> p g e", e=64),
                                        qv[:, 0:F].rearrange("p (g e) -> p g e", e=64),
                                        scf[:, 0:G].unsqueeze(2).to_broadcast([128, G, 64]),
                                        ALU.mult))
                vector.wait_ge(qch.h, qn[0])
                op(vector.tensor_sub(whb[:, 0:F], whb[:, 0:F], wb[:, 0:F]))
                vector.wait_ge(qch.h, qn[0])
                op(vector.tensor_add(outb[:, 0:F], wb[:, 0:F], whb[:, 0:F]))
                vector.wait_ge(qch.h, qn[0])
                op(vector.tensor_copy(wb[:, 0:F], wb[:, 0:F]))

        @block.sync
        def _(sync):
            nops = 16
            for ti, (src, i, F) in enumerate(tiles):
                sync.wait_ge(qch.h, nops * (ti + 1))
                d = QGU if src == "WGU" else QP
                sync.dma_start(out=d[i * 128:(i + 1) * 128, :], in_=outb[:, 0:F]
                               ).then_inc(stq.h, 16)
            sync.wait_ge(stq.h, 16 * len(tiles))

    return nc


def run_quant(inputs):
    wgu = inputs["L1_ffn_w_gate_up"]
    wp = inputs["L1_ffn_w_proj"]
    in_maps = []
    for c in range(8):
        in_maps.append({"WGU": np.ascontiguousarray(wgu[c * 2048:(c + 1) * 2048]),
                        "WP": np.ascontiguousarray(wp[c * 256:(c + 1) * 256])})
    nc = build_quant()
    res = run_bass_kernel_spmd(nc, in_maps, core_ids=list(range(8)))
    qgu = np.concatenate([res.results[c]["QGU"] for c in range(8)], axis=0)
    qp = np.concatenate([res.results[c]["QP"] for c in range(8)], axis=0)
    return qgu, qp




def build_l1_qkv():
    rec = {}
    for pas in range(2):
        nc = bass.Bass("TRN2", target_bir_lowering=False, debug=False)
        X2B = nc.dram_tensor("X2B", [D, 4096], F32, kind="ExternalInput").ap()
        WQ3 = nc.dram_tensor("WQ3", [D, 3 * 1024], F32, kind="ExternalInput").ap()
        WB8 = nc.dram_tensor("WB8", [D, 8], F32, kind="ExternalInput").ap()
        CONV = nc.dram_tensor("CONV", [128, 12], F32, kind="ExternalInput").ap()
        ONE = nc.dram_tensor("ONE", [128, 1], BF16, kind="ExternalInput").ap()
        QN = nc.dram_tensor("QN", [1024, 4096], BF16, kind="ExternalOutput").ap()
        KN = nc.dram_tensor("KN", [1024, 4096], BF16, kind="ExternalOutput").ap()
        KB = nc.dram_tensor("KB", [1024, 4096], BF16, kind="ExternalOutput").ap()
        VB = nc.dram_tensor("VB", [1024, 4096], BF16, kind="ExternalOutput").ap()
        rscr = nc.dram_tensor("rscr", [2, 4096], F32, kind="Internal").ap()
        bscr = nc.dram_tensor("bscr", [8, 4096], F32, kind="Internal").ap()

        xn = [nc.alloc_sbuf_tensor(f"xn{i}", [128, 16 * 512], BF16) for i in range(2)]
        w3 = nc.alloc_sbuf_tensor("w3", [128, 16 * 128], BF16)
        wb8 = nc.alloc_sbuf_tensor("wb8", [128, 16 * 8], BF16)
        cvw = nc.alloc_sbuf_tensor("cvw", [128, 12], F32)
        one = nc.alloc_sbuf_tensor("one", [128, 1], BF16)
        cbuf = nc.alloc_sbuf_tensor("cbuf", [128, 515], F32)
        c1 = nc.alloc_sbuf_tensor("c1", [128, 512], F32)
        c2 = nc.alloc_sbuf_tensor("c2", [128, 512], F32)
        tail = nc.alloc_sbuf_tensor("tail", [128, 3], F32)
        sraw = nc.alloc_sbuf_tensor("sraw", [128, 4096], F32)
        sqb = nc.alloc_sbuf_tensor("sqb", [128, 512], BF16)
        beta = nc.alloc_sbuf_tensor("beta", [8, 4096], F32)
        rsr = nc.alloc_sbuf_tensor("rsr", [1, 4096], F32)
        rs2 = nc.alloc_sbuf_tensor("rs2", [1, 4096], F32)
        brow = nc.alloc_sbuf_tensor("brow", [1, 4096], F32)
        bc1 = nc.alloc_sbuf_tensor("bc1", [128, 4096], F32)
        bc2 = nc.alloc_sbuf_tensor("bc2", [128, 4096], F32)
        obuf = nc.alloc_sbuf_tensor("obuf", [128, 2 * 4096], BF16)
        warm = nc.alloc_sbuf_tensor("warm", [1, 32], F32)
        dum2 = nc.alloc_sbuf_tensor("dum2", [1, 8], F32)
        psA = nc.alloc_psum_tensor("psA", [128, 512], F32)
        psB = nc.alloc_psum_tensor("psB", [128, 512], F32)
        psM = nc.alloc_psum_tensor("psM", [1, 512], F32)

        lx = [Sem(nc, "lx0"), Sem(nc, "lx1")]
        lw = Sem(nc, "lw"); lcst = Sem(nc, "lcst")
        mmP = Sem(nc, "mmP"); mmS = Sem(nc, "mmS")
        vch = Sem(nc, "vch"); cnvd = Sem(nc, "cnvd")
        sil = Sem(nc, "sil"); sqe = Sem(nc, "sqe"); sqr = Sem(nc, "sqr")
        wse = Sem(nc, "wse"); sse = Sem(nc, "sse")
        rwq = Sem(nc, "rwq"); rbq = Sem(nc, "rbq"); stq = Sem(nc, "stq")
        bst = Sem(nc, "bst"); bld = Sem(nc, "bld")

        NM = 24
        mkind = lambda m: ("q" if m < 8 else ("k" if m < 16 else "v"))
        R = (lambda k, d=0: rec.get(k, d))

        with nc.Block() as block:
            @block.vector
            def _(vector):
                vn = [0]

                def op(inst):
                    inst.then_inc(vch.h, 1)
                    vn[0] += 1
                    return vn[0]

                cnt_cnvd = [0]

                def vdummy():
                    vector.tensor_copy(dum2[0:1, 0:1], one[0:1, 0:1]).then_inc(cnvd.h, 1)
                    cnt_cnvd[0] += 1
                    return cnt_cnvd[0]

                vector.wait_ge(lcst.h, 48)
                nsq = 0
                for m in range(NM):
                    knd = mkind(m)
                    for n in range(8):
                        if n == 0:
                            vector.wait_ge(vch.h, vn[0])
                            op(vector.memset(tail[:], 0.0))
                        vector.wait_ge(vch.h, vn[0])
                        op(vector.tensor_copy(cbuf[:, 0:3], tail[:]))
                        vector.wait_ge(mmP.h, 16 * (8 + m * 8 + n + 1))
                        vector.wait_ge(vch.h, vn[0])
                        ps = psA if n % 2 == 0 else psB
                        rec[("evv", m, n)] = op(vector.tensor_copy(cbuf[:, 3:515], ps[:]))
                        vector.wait_ge(vch.h, vn[0])
                        op(vector.tensor_copy(tail[:], cbuf[:, 512:515]))
                        cw = 4 * (0 if knd == "q" else (1 if knd == "k" else 2))
                        vector.wait_ge(vch.h, vn[0])
                        op(vector.tensor_scalar(c1[:], cbuf[:, 3:515], cvw[:, cw + 3:cw + 4], None, ALU.mult))
                        vector.wait_ge(vch.h, vn[0])
                        op(vector.scalar_tensor_tensor(c2[:], cbuf[:, 2:514], cvw[:, cw + 2:cw + 3], c1[:], ALU.mult, ALU.add))
                        vector.wait_ge(vch.h, vn[0])
                        op(vector.scalar_tensor_tensor(c1[:], cbuf[:, 1:513], cvw[:, cw + 1:cw + 2], c2[:], ALU.mult, ALU.add))
                        vector.wait_ge(vch.h, vn[0])
                        rec[("cnv", m, n)] = op(vector.scalar_tensor_tensor(c2[:], cbuf[:, 0:512], cvw[:, cw + 0:cw + 1], c1[:], ALU.mult, ALU.add))
                        rec[("cnvd", m, n)] = vdummy()
                        if knd != "v":
                            nsq += 1
                            vector.wait_ge(mmS.h, nsq)
                            rec[("psmv", m, n)] = op(vector.tensor_scalar(
                                rsr[:, n * 512:(n + 1) * 512], psM[:], 1e-6, None, ALU.add))
                    if knd != "v":
                        rec[("rmv", m)] = vn[0]
                        rec[("rmvd", m)] = vdummy()
                        vector.wait_ge(sqr.h, R(("sqr", m), 0))
                        vector.wait_ge(wse.h, R(("wsesqr", m), 0))
                        op(vector.reciprocal(rsr[:], rs2[:]))
                        vector.wait_ge(vch.h, vn[0])
                        if knd == "q":
                            op(vector.tensor_scalar(rs2[:], rsr[:], float(128 ** -0.5), None, ALU.mult))
                        else:
                            vector.wait_ge(bld.h, 16 * (m - 8 + 1))
                            op(vector.tensor_tensor(rs2[:], rsr[:], brow[:], ALU.mult))
                    rec[("rowv", m)] = vn[0]
                    rec[("rowd", m)] = vdummy()
                    vector.wait_ge(rbq.h, 16 * R(("rbc", m), 0))
                    vector.wait_ge(sil.h, 8 * (m + 1))
                    vector.wait_ge(wse.h, R(("wsem", m), 0))
                    if m >= 2:
                        vector.wait_ge(stq.h, 16 * R(("stc", m - 2), 0))
                    vector.wait_ge(vch.h, vn[0])
                    op(vector.tensor_tensor(obuf[:, 0:4096], sraw[:], bc1[:], ALU.mult))
                    if knd == "k":
                        vector.wait_ge(vch.h, vn[0])
                        op(vector.tensor_tensor(obuf[:, 4096:8192], sraw[:], bc2[:], ALU.mult))
                    rec[("stv", m)] = vn[0]
                    rec[("std", m)] = vdummy()

            @block.scalar
            def _(scalar):
                wn = [0]

                def wdummy():
                    scalar.wait_ge(wse.h, wn[0])
                    wn[0] += 1
                    return scalar.activation(warm[:, 0:16], warm[:, 16:32], AF.Exp
                                             ).then_inc(wse.h, 1)

                scalar.memzero(warm[:]).then_inc(wse.h, 1)
                wn[0] = 1
                scalar.wait_ge(lcst.h, 48)
                nsq = 0
                nsqr = 0
                for n in range(8):
                    scalar.wait_ge(mmP.h, 16 * (n + 1))
                    scalar.activation(beta[:, n * 512:(n + 1) * 512], psA[0:8, :], AF.Sigmoid
                                      ).then_inc(sse.h, 1)
                wdummy()
                rec["wsebeta"] = wn[0]
                for m in range(NM):
                    knd = mkind(m)
                    for n in range(8):
                        scalar.wait_ge(vch.h, R(("cnv", m, n), 0))
                        scalar.wait_ge(cnvd.h, R(("cnvd", m, n), 0))
                        scalar.activation(sraw[:, n * 512:(n + 1) * 512], c2[:], AF.Silu
                                          ).then_inc(sil.h, 1)
                        if knd != "v":
                            nsq += 1
                            scalar.activation(sqb[:], sraw[:, n * 512:(n + 1) * 512], AF.Square
                                              ).then_inc(sqe.h, 1)
                        wdummy()
                    rec[("wsem", m)] = wn[0]
                    if knd != "v":
                        scalar.wait_ge(vch.h, R(("rmv", m), 0))
                        scalar.wait_ge(cnvd.h, R(("rmvd", m), 0))
                        nsqr += 1
                        scalar.activation(rs2[:], rsr[:], AF.Sqrt).then_inc(sqr.h, 1)
                        rec[("sqr", m)] = nsqr
                        wdummy()
                        rec[("wsesqr", m)] = wn[0]

            @block.tensor
            def _(tensor):
                tensor.wait_ge(lcst.h, 48)
                for n in range(8):
                    tensor.wait_ge(lx[n % 2].h, 16 * (n // 2 + 1))
                    if n >= 1:
                        tensor.wait_ge(sse.h, n)
                    inst = None
                    for k in range(16):
                        inst = tensor.matmul(psA[0:8, :], wb8[:, k * 8:(k + 1) * 8],
                                             xn[n % 2][:, k * 512:(k + 1) * 512],
                                             start=(k == 0), stop=(k == 15))
                    inst.then_inc(mmP.h, 16)
                nsq = 0
                gseq = []
                for m in range(NM):
                    for n in range(8):
                        gseq.append((m, n))
                for m in range(NM):
                    knd = mkind(m)
                    tensor.wait_ge(lw.h, 16 * (m + 1))
                    for n in range(8):
                        gi = 8 + m * 8 + n
                        tensor.wait_ge(lx[gi % 2].h, 16 * (gi // 2 + 1))
                        idx = m * 8 + n
                        if idx >= 2:
                            pm, pn = gseq[idx - 2]
                            tensor.wait_ge(vch.h, R(("evv", pm, pn), 0))
                        if idx < 2:
                            tensor.wait_ge(sse.h, 8)
                        ps = psA if n % 2 == 0 else psB
                        inst = None
                        for k in range(16):
                            inst = tensor.matmul(ps[:], w3[:, k * 128:(k + 1) * 128],
                                                 xn[gi % 2][:, k * 512:(k + 1) * 512],
                                                 start=(k == 0), stop=(k == 15))
                        inst.then_inc(mmP.h, 16)
                        if knd != "v":
                            nsq += 1
                            tensor.wait_ge(sqe.h, nsq)
                            tensor.wait_ge(wse.h, R("wsebeta", 0) + m * 8 + n + 1)
                            if nsq >= 2:
                                tensor.wait_ge(vch.h, R(("psmvprev", m, n), 0))
                            tensor.matmul(psM[:], one[:], sqb[:], start=True, stop=True
                                          ).then_inc(mmS.h, 1)

            @block.sync
            def _(sync):
                nrw = 0
                nrb = 0
                nst = 0
                for m in range(NM):
                    knd = mkind(m)
                    sync.wait_ge(vch.h, R(("rowv", m), 0))
                    sync.wait_ge(cnvd.h, R(("rowd", m), 0))
                    src1 = rs2 if knd == "q" else (rsr if knd == "k" else beta[m - 16:m - 15, :])
                    if knd == "v":
                        sync.dma_start(out=rscr[0], in_=src1).then_inc(rwq.h, 16)
                    else:
                        sync.dma_start(out=rscr[0], in_=(rs2[:] if knd == "q" else rsr[:])).then_inc(rwq.h, 16)
                    nrw += 1
                    if knd == "k":
                        sync.dma_start(out=rscr[1], in_=rs2[:]).then_inc(rwq.h, 16)
                        nrw += 1
                    sync.wait_ge(rwq.h, 16 * nrw)
                    if m >= 1:
                        sync.wait_ge(vch.h, R(("stv", m - 1), 0))
                        sync.wait_ge(cnvd.h, R(("std", m - 1), 0))
                    sync.dma_start(out=bc1[:], in_=rscr[0].partition_broadcast(128)).then_inc(rbq.h, 16)
                    nrb += 1
                    if knd == "k":
                        sync.dma_start(out=bc2[:], in_=rscr[1].partition_broadcast(128)).then_inc(rbq.h, 16)
                        nrb += 1
                    rec[("rbc", m)] = nrb
                    sync.wait_ge(vch.h, R(("stv", m), 0))
                    sync.wait_ge(cnvd.h, R(("std", m), 0))
                    dst1 = {"q": QN, "k": KN, "v": VB}[knd]
                    mm = m % 8
                    sync.dma_start(out=dst1.rearrange("(h p) t -> p h t", p=128)[:, mm],
                                   in_=obuf[:, 0:4096]).then_inc(stq.h, 16)
                    nst += 1
                    if knd == "k":
                        sync.dma_start(out=KB.rearrange("(h p) t -> p h t", p=128)[:, mm],
                                       in_=obuf[:, 4096:8192]).then_inc(stq.h, 16)
                        nst += 1
                    rec[("stc", m)] = nst
                sync.wait_ge(stq.h, 16 * nst)

            @block.gpsimd
            def _(gpsimd):
                gpsimd.dma_start(out=wb8[:].rearrange("p (k f) -> p k f", k=16),
                                 in_=WB8.rearrange("(k p) f -> p k f", p=128)).then_inc(lcst.h, 16)
                gpsimd.dma_start(out=cvw[:], in_=CONV).then_inc(lcst.h, 16)
                gpsimd.dma_start(out=one[:], in_=ONE).then_inc(lcst.h, 16)
                gi = 0
                for n in range(8):
                    if gi >= 2:
                        gpsimd.wait_ge(mmP.h, 16 * (gi - 1))
                    gpsimd.dma_start(out=xn[gi % 2][:].rearrange("p (k t) -> p k t", k=16),
                                     in_=X2B.rearrange("(k p) t -> p k t", p=128)[:, :, n * 512:(n + 1) * 512]
                                     ).then_inc(lx[gi % 2].h, 16)
                    gi += 1
                gpsimd.wait_ge(sse.h, 8)
                gpsimd.wait_ge(wse.h, 2)
                gpsimd.dma_start(out=bscr, in_=beta[:]).then_inc(bst.h, 16)
                for m in range(NM):
                    if m == 8:
                        gpsimd.wait_ge(bst.h, 16)
                    if 8 <= m < 16:
                        mk = m - 8
                        if mk >= 1:
                            gpsimd.wait_ge(vch.h, R(("rowv", 8 + mk - 1), 0))
                        gpsimd.dma_start(out=brow[:], in_=bscr[mk]).then_inc(bld.h, 16)
                    if m >= 1:
                        gpsimd.wait_ge(mmP.h, 16 * (8 + m * 8))
                    gpsimd.dma_start(out=w3[:].rearrange("p (k f) -> p k f", k=16),
                                     in_=WQ3.rearrange("(k p) f -> p k f", p=128)[:, :, m * 128:(m + 1) * 128]
                                     ).then_inc(lw.h, 16)
                    for n in range(8):
                        if gi >= 2:
                            gpsimd.wait_ge(mmP.h, 16 * (gi - 1))
                        gpsimd.dma_start(out=xn[gi % 2][:].rearrange("p (k t) -> p k t", k=16),
                                         in_=X2B.rearrange("(k p) t -> p k t", p=128)[:, :, n * 512:(n + 1) * 512]
                                         ).then_inc(lx[gi % 2].h, 16)
                        gi += 1

        if pas == 0:
            prev = 0
            for m in range(NM):
                if mkind(m) == "v":
                    continue
                for n in range(8):
                    rec[("psmvprev", m, n)] = prev
                    prev = rec.get(("psmv", m, n), 0)
    return nc


def run_l1_qkv(inputs, x2_results):
    Wq = inputs["L1_attn_Wq"]; Wk = inputs["L1_attn_Wk"]; Wv = inputs["L1_attn_Wv"]
    Wb = inputs["L1_attn_Wb"]
    ONE = np.ones((128, 1), dtype=ml_dtypes.bfloat16)
    in_maps = []
    for c in range(8):
        b, r = divmod(c, 2)
        X2nat = np.zeros((D, 4096), dtype=np.float32)
        X2nat[:, slot_tokens(0)] = x2_results[2 * b]["X2"]
        X2nat[:, slot_tokens(1)] = x2_results[2 * b + 1]["X2"]
        cols = slice(r * 1024, (r + 1) * 1024)
        WQ3 = np.ascontiguousarray(np.concatenate([Wq[:, cols], Wk[:, cols], Wv[:, cols]], axis=1))
        WB8 = np.ascontiguousarray(Wb[:, r * 8:(r + 1) * 8])
        CONV = np.ascontiguousarray(np.tile(np.concatenate(
            [inputs["L1_attn_conv_q"], inputs["L1_attn_conv_k"], inputs["L1_attn_conv_v"]]),
            (128, 1)).astype(np.float32))
        in_maps.append({"X2B": np.ascontiguousarray(X2nat), "WQ3": WQ3, "WB8": WB8,
                        "CONV": CONV, "ONE": ONE})
    nc = build_l1_qkv()
    res = run_bass_kernel_spmd(nc, in_maps, core_ids=list(range(8)))
    return res.results




def build_l1_scan(NH=8):
    nc = bass.Bass("TRN2", target_bir_lowering=False, debug=False)
    QN = nc.dram_tensor("QN", [1024, 4096], BF16, kind="ExternalInput").ap()
    KN = nc.dram_tensor("KN", [1024, 4096], BF16, kind="ExternalInput").ap()
    KB = nc.dram_tensor("KB", [1024, 4096], BF16, kind="ExternalInput").ap()
    KNT = nc.dram_tensor("KNT", [128, 8 * 32 * 128], BF16, kind="ExternalInput").ap()
    KBT = nc.dram_tensor("KBT", [128, 8 * 32 * 128], BF16, kind="ExternalInput").ap()
    VBT = nc.dram_tensor("VBT", [128, 8 * 32 * 128], BF16, kind="ExternalInput").ap()
    MSL = nc.dram_tensor("MSL", [128, 384], F32, kind="ExternalInput").ap()
    IDN = nc.dram_tensor("IDN", [128, 128], F32, kind="ExternalInput").ap()
    ONW = nc.dram_tensor("ONW", [128, 1], F32, kind="ExternalInput").ap()
    ONE = nc.dram_tensor("ONE", [128, 1], BF16, kind="ExternalInput").ap()
    O1 = nc.dram_tensor("O1", [1024, 4096], BF16, kind="ExternalOutput").ap()
    rscr = nc.dram_tensor("rscr", [1, 4096], F32, kind="Internal").ap()

    qn = nc.alloc_sbuf_tensor("qn", [128, 4096], BF16)
    kn = nc.alloc_sbuf_tensor("kn", [128, 4096], BF16)
    kb = nc.alloc_sbuf_tensor("kb", [128, 4096], BF16)
    knt = nc.alloc_sbuf_tensor("knt", [128, 4096], BF16)
    kbt = nc.alloc_sbuf_tensor("kbt", [128, 4096], BF16)
    vbt = nc.alloc_sbuf_tensor("vbt", [128, 4096], BF16)
    msl = nc.alloc_sbuf_tensor("msl", [128, 384], F32)
    idn = nc.alloc_sbuf_tensor("idn", [128, 128], F32)
    onw = nc.alloc_sbuf_tensor("onw", [128, 1], F32)
    one = nc.alloc_sbuf_tensor("one", [128, 1], BF16)
    Sf = nc.alloc_sbuf_tensor("Sf", [128, 128], F32)
    Sb = nc.alloc_sbuf_tensor("Sb", [128, 128], BF16)
    Lt1 = nc.alloc_sbuf_tensor("Lt1", [128, 128], F32)
    Lt2 = nc.alloc_sbuf_tensor("Lt2", [128, 128], F32)
    Px = nc.alloc_sbuf_tensor("Px", [128, 128], F32)
    PxT = nc.alloc_sbuf_tensor("PxT", [128, 128], F32)
    Gx = nc.alloc_sbuf_tensor("Gx", [128, 128], F32)
    GxT = nc.alloc_sbuf_tensor("GxT", [128, 128], F32)
    GxTb = nc.alloc_sbuf_tensor("GxTb", [128, 128], BF16)
    ut = nc.alloc_sbuf_tensor("ut", [128, 128], BF16)
    upt = nc.alloc_sbuf_tensor("upt", [128, 128], BF16)
    wf = nc.alloc_sbuf_tensor("wf", [128, 128], BF16)
    att = nc.alloc_sbuf_tensor("att", [128, 128], BF16)
    ost = nc.alloc_sbuf_tensor("ost", [128, 4096], BF16)
    osq = nc.alloc_sbuf_tensor("osq", [128, 512], BF16)
    rsr = nc.alloc_sbuf_tensor("rsr", [1, 4096], F32)
    rs2 = nc.alloc_sbuf_tensor("rs2", [1, 4096], F32)
    bc1 = nc.alloc_sbuf_tensor("bc1", [128, 4096], F32)
    tmo = nc.alloc_sbuf_tensor("tmo", [128, 512], F32)
    warm = nc.alloc_sbuf_tensor("warm", [1, 32], F32)
    dum2 = nc.alloc_sbuf_tensor("dum2", [1, 8], F32)
    psU = nc.alloc_psum_tensor("psU", [128, 512], F32)
    psV = nc.alloc_psum_tensor("psV", [128, 512], F32)
    psM = nc.alloc_psum_tensor("psM", [1, 512], F32)

    lds = Sem(nc, "lds"); tch = Sem(nc, "tch"); vchd = Sem(nc, "vchd")
    sqe = Sem(nc, "sqe"); wse = Sem(nc, "wse")
    rwq = Sem(nc, "rwq"); rbq = Sem(nc, "rbq"); stq = Sem(nc, "stq")

    NLEV = 6
    prog = []

    def T(fn):
        prog.append(("t", fn))

    def V(fn):
        prog.append(("v", fn))

    def S(fn):
        prog.append(("s", fn))

    def DMA(kind, fn):
        prog.append((kind, fn))

    for h in range(NH):
        hh = h
        V(lambda v, hh=hh: (v.wait_ge(stq.h, 16 * hh), v.memset(Sf[:], 0.0)))
        V(lambda v: v.memset(Sb[:], 0.0))
        for c in range(32):
            sl_ = slice(c * 128, (c + 1) * 128)
            T(lambda t, sl_=sl_: t.matmul(psU[:, 0:128], kb[:, sl_], kn[:, sl_], start=True, stop=True))
            T(lambda t, sl_=sl_: t.matmul(psU[:, 128:256], kn[:, sl_], kb[:, sl_], start=True, stop=True))
            V(lambda v: v.tensor_tensor(Lt1[:], psU[:, 0:128], msl[:, 0:128], ALU.mult))
            V(lambda v: v.tensor_tensor(Lt2[:], psU[:, 128:256], msl[:, 128:256], ALU.mult))
            V(lambda v: v.tensor_sub(GxT[:], idn[:], Lt2[:]))
            for lev in range(NLEV):
                Xs, XsT = (Lt1, Lt2) if lev == 0 else (Px, PxT)
                T(lambda t, XsT=XsT, Xs=Xs: t.matmul(psU[:, 0:128], XsT[:], Xs[:], start=True, stop=True))
                if lev < NLEV - 1:
                    T(lambda t, XsT=XsT, Xs=Xs: t.matmul(psU[:, 128:256], Xs[:], XsT[:], start=True, stop=True))
                V(lambda v: v.tensor_copy(Px[:], psU[:, 0:128]))
                if lev < NLEV - 1:
                    V(lambda v: v.tensor_copy(PxT[:], psU[:, 128:256]))
                T(lambda t: t.matmul(psV[:, 128:256], idn[:], GxT[:], start=True, stop=False))
                T(lambda t: t.matmul(psV[:, 128:256], Px[:], GxT[:], start=False, stop=True))
                V(lambda v: v.tensor_copy(GxT[:], psV[:, 128:256]))
            V(lambda v: v.tensor_copy(GxTb[:], GxT[:]))
            T(lambda t, sl_=sl_: t.matmul(psU[:, 0:128], GxTb[:], vbt[:, sl_], start=True, stop=True))
            T(lambda t, sl_=sl_: t.matmul(psU[:, 128:256], kbt[:, sl_], GxTb[:], start=True, stop=True))
            V(lambda v: v.tensor_copy(ut[:], psU[:, 0:128]))
            V(lambda v: v.tensor_copy(wf[:], psU[:, 128:256]))
            T(lambda t: t.matmul(psV[:, 0:128], wf[:], Sb[:], start=True, stop=True))
            V(lambda v: v.tensor_sub(upt[:], ut[:], psV[:, 0:128]))
            T(lambda t, sl_=sl_: t.matmul(psV[:, 128:256], kn[:, sl_], qn[:, sl_], start=True, stop=True))
            V(lambda v: v.tensor_tensor(att[:], psV[:, 128:256], msl[:, 256:384], ALU.mult))
            T(lambda t, sl_=sl_: t.matmul(psU[:, 256:384], Sb[:], qn[:, sl_], start=True, stop=False))
            T(lambda t: t.matmul(psU[:, 256:384], upt[:], att[:], start=False, stop=True))
            V(lambda v, sl_=sl_: v.tensor_copy(ost[:, sl_], psU[:, 256:384]))
            T(lambda t, sl_=sl_: t.matmul(psV[:, 256:384], knt[:, sl_], upt[:], start=True, stop=True))
            V(lambda v: v.tensor_add(Sf[:], Sf[:], psV[:, 256:384]))
            V(lambda v: v.tensor_copy(Sb[:], Sf[:]))
        for s8 in range(8):
            so = slice(s8 * 512, (s8 + 1) * 512)
            S(lambda s, so=so: s.activation(osq[:], ost[:, so], AF.Square))
            T(lambda t: t.matmul(psM[:], one[:], osq[:], start=True, stop=True))
            V(lambda v, so=so: v.tensor_scalar(rsr[0:1, so], psM[:], 1.0 / 128.0, 1e-5,
                                               ALU.mult, ALU.add))
        S(lambda s: s.activation(rs2[:], rsr[:], AF.Sqrt))
        V(lambda v: v.reciprocal(rsr[:], rs2[:]))
        DMA("dmaw", lambda sy, hh=hh: sy.dma_start(out=rscr, in_=rsr[:]).then_inc(rwq.h, 16))
        DMA("dmab", lambda sy, hh=hh: (sy.wait_ge(rwq.h, 16 * (hh + 1)),
                                       sy.dma_start(out=bc1[:], in_=rscr.partition_broadcast(128)
                                                    ).then_inc(rbq.h, 16)))
        for s8 in range(8):
            so = slice(s8 * 512, (s8 + 1) * 512)
            V(lambda v, so=so, hh=hh: (v.wait_ge(rbq.h, 16 * (hh + 1)),
                                       v.tensor_scalar(tmo[:], ost[:, so], onw[:, 0:1], None, ALU.mult)))
            V(lambda v, so=so: v.tensor_tensor(ost[:, so], tmo[:], bc1[:, so], ALU.mult))
        DMA("dmas", lambda sy, hh=hh: sy.dma_start(
            out=O1.rearrange("(g p) t -> p g t", p=128)[:, hh], in_=ost[:]).then_inc(stq.h, 16))

    nT = [0]; nV = [0]; nS = [0]
    pref = []
    for eng, fn in prog:
        pref.append((nT[0], nV[0], nS[0]))
        if eng == "t":
            nT[0] += 1
        elif eng == "v":
            nV[0] += 1
        elif eng == "s":
            nS[0] += 1
    OPS_PER_HEAD = len(prog) // NH

    with nc.Block() as block:
        @block.tensor
        def _(tensor):
            lastv = -1
            lasts = -1
            lasth = -1
            for i, (eng, fn) in enumerate(prog):
                if eng != "t":
                    continue
                h = i // OPS_PER_HEAD
                tb, vb, sb = pref[i]
                if h != lasth:
                    tensor.wait_ge(lds.h, 16 * (4 + 6 * (h + 1)))
                    lasth = h
                if vb != lastv:
                    tensor.wait_ge(vchd.h, vb)
                    lastv = vb
                if sb != lasts:
                    tensor.wait_ge(sqe.h, sb)
                    tensor.wait_ge(wse.h, 1 + sb)
                    lasts = sb
                fn(tensor).then_inc(tch.h, 1)

        @block.vector
        def _(vector):
            lastt = -1
            lasts = -1
            for i, (eng, fn) in enumerate(prog):
                if eng != "v":
                    continue
                tb, vb, sb = pref[i]
                if tb != lastt:
                    vector.wait_ge(tch.h, tb)
                    lastt = tb
                if sb != lasts:
                    vector.wait_ge(sqe.h, sb)
                    vector.wait_ge(wse.h, 1 + sb)
                    lasts = sb
                fn(vector)
                vector.tensor_copy(dum2[0:1, 0:1], one[0:1, 0:1]).then_inc(vchd.h, 1)

        @block.scalar
        def _(scalar):
            wn = [0]
            scalar.memzero(warm[:]).then_inc(wse.h, 1)
            wn[0] = 1
            for i, (eng, fn) in enumerate(prog):
                if eng != "s":
                    continue
                tb, vb, sb = pref[i]
                scalar.wait_ge(vchd.h, vb)
                fn(scalar).then_inc(sqe.h, 1)
                scalar.wait_ge(wse.h, wn[0])
                wn[0] += 1
                scalar.activation(warm[:, 0:16], warm[:, 16:32], AF.Exp).then_inc(wse.h, 1)

        @block.sync
        def _(sync):
            for i, (eng, fn) in enumerate(prog):
                if eng not in ("dmaw", "dmab", "dmas"):
                    continue
                tb, vb, sb = pref[i]
                sync.wait_ge(vchd.h, vb)
                fn(sync)
            sync.wait_ge(stq.h, 16 * NH)

        @block.gpsimd
        def _(gpsimd):
            gpsimd.dma_start(out=msl[:], in_=MSL).then_inc(lds.h, 16)
            gpsimd.dma_start(out=idn[:], in_=IDN).then_inc(lds.h, 16)
            gpsimd.dma_start(out=onw[:], in_=ONW).then_inc(lds.h, 16)
            gpsimd.dma_start(out=one[:], in_=ONE).then_inc(lds.h, 16)
            for h in range(NH):
                if h >= 1:
                    gpsimd.wait_ge(stq.h, 16 * h)
                for src_d, dst in ((QN, qn), (KN, kn), (KB, kb)):
                    gpsimd.dma_start(out=dst[:], in_=src_d.rearrange("(g p) t -> p g t", p=128)[:, h]
                                     ).then_inc(lds.h, 16)
                for src_d, dst in ((KNT, knt), (KBT, kbt), (VBT, vbt)):
                    gpsimd.dma_start(out=dst[:], in_=src_d[:, h * 4096:(h + 1) * 4096]
                                     ).then_inc(lds.h, 16)

    return nc


def run_l1_scan(inputs, qkv):
    cosn = None
    ONE = np.ones((128, 1), dtype=ml_dtypes.bfloat16)
    IDN = np.eye(128, dtype=np.float32)
    i_ = np.arange(128)[:, None]; j_ = np.arange(128)[None, :]
    stril_cm = (j_ < i_).astype(np.float32)
    striu_mc = (j_ > i_).astype(np.float32)
    inclT = (j_ >= i_).astype(np.float32)
    MSL = np.concatenate([stril_cm, striu_mc, inclT], axis=1)
    ONW = np.ascontiguousarray(inputs["L1_attn_onorm_w"].astype(np.float32)[:, None])
    in_maps = []
    for c in range(8):
        def tmize(a):
            return np.ascontiguousarray(
                np.asarray(a).reshape(8, 128, 32, 128).transpose(3, 0, 2, 1).reshape(128, 8 * 32 * 128))
        in_maps.append({"QN": qkv[c]["QN"], "KN": qkv[c]["KN"], "KB": qkv[c]["KB"],
                        "KNT": tmize(qkv[c]["KN"]), "KBT": tmize(qkv[c]["KB"]),
                        "VBT": tmize(qkv[c]["VB"]), "MSL": MSL, "IDN": IDN,
                        "ONW": ONW, "ONE": ONE})
    nc = build_l1_scan()
    res = run_bass_kernel_spmd(nc, in_maps, core_ids=list(range(8)))
    return res.results




def build_l1_tail():
    nc = bass.Bass("TRN2", target_bir_lowering=False, debug=False)
    O = nc.dram_tensor("O", [D, 2048], BF16, kind="ExternalInput").ap()
    X2IN = nc.dram_tensor("X2IN", [D, 2048], F32, kind="ExternalInput").ap()
    WOB = nc.dram_tensor("WOB", [16, 16, 128, 128], BF16, kind="ExternalInput").ap()
    GWB = nc.dram_tensor("GWB", [64, 16, 128, 128], BF16, kind="ExternalInput").ap()
    UWB = nc.dram_tensor("UWB", [64, 16, 128, 128], BF16, kind="ExternalInput").ap()
    WPB = nc.dram_tensor("WPB", [16, 64, 128, 128], BF16, kind="ExternalInput").ap()
    ONE = nc.dram_tensor("ONE", [128, 1], BF16, kind="ExternalInput").ap()
    OUT = nc.dram_tensor("OUT", [D, 2048], F32, kind="ExternalOutput").ap()
    X3F = nc.dram_tensor("X3F", [D, 2048], F32, kind="Internal").ap()
    scr = nc.dram_tensor("scr", [8, 1024], F32, kind="Internal").ap()
    scr2 = nc.dram_tensor("scr2", [1, 512], F32, kind="Internal").ap()

    NSL = 4
    RMS_EPS2 = 1.1920929e-07

    hdn = nc.alloc_sbuf_tensor("hdn", [128, 64 * 512], BF16)
    wob = nc.alloc_sbuf_tensor("wob0", [128, 16 * 128], BF16)
    y1raw = nc.alloc_sbuf_tensor("y1raw", [128, 16 * 512], F32)
    x1bn = nc.alloc_sbuf_tensor("x1bn", [128, 16 * 512], BF16)
    gwb = [nc.alloc_sbuf_tensor(f"gwb{i}", [128, 16 * 128], BF16) for i in range(2)]
    uwb = [nc.alloc_sbuf_tensor(f"uwb{i}", [128, 16 * 128], BF16) for i in range(2)]
    wpb = nc.alloc_sbuf_tensor("wpb0", [128, 64 * 128], BF16)
    gm = nc.alloc_sbuf_tensor("gm", [128, 512], BF16)
    um = nc.alloc_sbuf_tensor("um", [128, 512], BF16)
    x1fn = nc.alloc_sbuf_tensor("x1fn", [128, 16 * 512], F32)
    utn = x1fn
    y2raw = y1raw
    sumb = x1bn
    one = nc.alloc_sbuf_tensor("one", [128, 1], BF16)
    mb = nc.alloc_sbuf_tensor("mb", [128, 1024], F32)
    mb2 = nc.alloc_sbuf_tensor("mb2", [128, 512], F32)
    rrow = nc.alloc_sbuf_tensor("rrow", [1, 1536], F32)
    y1q = y1raw[:].bitcast(BF16)
    utq = x1fn[:].bitcast(BF16)
    srow = x1fn[0:1, 0:1024]
    tmp5 = mb[0:1, 0:512]
    tmp6 = mb[0:1, 512:1024]
    warm = nc.alloc_sbuf_tensor("warm", [1, 32], F32)
    dum2 = nc.alloc_sbuf_tensor("dum2", [1, 8], F32)
    tmpa = nc.alloc_sbuf_tensor("tmpa", [128, 512], F32)
    psA = nc.alloc_psum_tensor("psA", [128, 512], F32)
    psB = nc.alloc_psum_tensor("psB", [128, 512], F32)
    psM = nc.alloc_psum_tensor("psM", [1, 512], F32)
    psQ = nc.alloc_psum_tensor("psQ", [1, 512], F32)
    psQ2 = nc.alloc_psum_tensor("psQ2", [1, 512], F32)

    lres = Sem(nc, "lres")
    lon = Sem(nc, "lon"); lut = Sem(nc, "lut"); lx1 = Sem(nc, "lx1")
    lwm = Sem(nc, "lwm0")
    lgw = [Sem(nc, "lgw0"), Sem(nc, "lgw1")]
    ldw = Sem(nc, "ldw0")
    mmA = Sem(nc, "mmA"); mmB = Sem(nc, "mmB"); mmC = Sem(nc, "mmC"); mmS = Sem(nc, "mmS")
    mmS2 = Sem(nc, "mmS2")
    evA = Sem(nc, "evA"); evAd = Sem(nc, "evAd")
    seB = Sem(nc, "seB"); evB = Sem(nc, "evB"); evBd = Sem(nc, "evBd")
    evC = Sem(nc, "evC"); evCd = Sem(nc, "evCd")
    cst = Sem(nc, "cst"); cstd = Sem(nc, "cstd"); vch = Sem(nc, "vch")
    sqq = Sem(nc, "sqq"); wse = Sem(nc, "wse")
    sqq2 = Sem(nc, "sqq2"); wse2 = Sem(nc, "wse2")
    rq = Sem(nc, "rq"); rqd = Sem(nc, "rqd"); rq2 = Sem(nc, "rq2")
    rq3 = Sem(nc, "rq3"); rq3d = Sem(nc, "rq3d")
    rw2 = Sem(nc, "rw2"); rb2 = Sem(nc, "rb2")
    ap2 = Sem(nc, "ap2"); ap2d = Sem(nc, "ap2d")
    stq = Sem(nc, "stq"); rwq = Sem(nc, "rwq"); rbq = Sem(nc, "rbq")
    apq = Sem(nc, "apq"); apqd = Sem(nc, "apqd")
    stX3 = Sem(nc, "stX3"); stX2 = Sem(nc, "stX2")

    with nc.Block() as block:
        @block.gpsimd
        def _(gpsimd):
            gpsimd.dma_start(out=one[:], in_=ONE).then_inc(lres.h, 16)
            for n in range(NSL):
                if n >= 1:
                    gpsimd.wait_ge(stX2.h, 16 * n)
                gpsimd.dma_start(out=hdn[:].rearrange("p (k t) -> p k t", k=64)[:, 0:16],
                                 in_=O.rearrange("(k p) t -> p k t", p=128)[:, :, n * 512:(n + 1) * 512]
                                 ).then_inc(lon.h, 16)
                gpsimd.dma_start(out=utn[:].rearrange("p (k t) -> p k t", k=16),
                                 in_=X2IN.rearrange("(k p) t -> p k t", p=128)[:, :, n * 512:(n + 1) * 512]
                                 ).then_inc(lut.h, 16)
                for m in range(16):
                    if n * 16 + m >= 1:
                        gpsimd.wait_ge(mmA.h, 16 * n + m)
                    gpsimd.dma_start(out=wob[:].rearrange("p (k f) -> p k f", k=16),
                                     in_=WOB[m].rearrange("k p f -> p k f")
                                     ).then_inc(lwm.h, 16)
                for m in range(64):
                    if n * 64 + m >= 2:
                        gpsimd.wait_ge(mmB.h, 2 * (64 * n + m - 1))
                    gpsimd.dma_start(out=gwb[m % 2][:].rearrange("p (k f) -> p k f", k=16),
                                     in_=GWB[m].rearrange("k p f -> p k f")
                                     ).then_inc(lgw[m % 2].h, 16)
                    gpsimd.dma_start(out=uwb[m % 2][:].rearrange("p (k f) -> p k f", k=16),
                                     in_=UWB[m].rearrange("k p f -> p k f")
                                     ).then_inc(lgw[m % 2].h, 16)
                gpsimd.wait_ge(stX3.h, 16 * (n + 1))
                gpsimd.wait_ge(mmS2.h, n + 1)
                gpsimd.dma_start(out=x1fn[:].rearrange("p (k t) -> p k t", k=16),
                                 in_=X3F.rearrange("(k p) t -> p k t", p=128)[:, :, n * 512:(n + 1) * 512]
                                 ).then_inc(lx1.h, 16)

        @block.tensor
        def _(tensor):
            tensor.wait_ge(lres.h, 16)
            for n in range(NSL):
                tensor.wait_ge(lon.h, 16 * (n + 1))
                for m in range(16):
                    tensor.wait_ge(lwm.h, 16 * (16 * n + m + 1))
                    if m >= 2:
                        tensor.wait_ge(evA.h, 16 * n + m - 1)
                    inst = None
                    for k in range(16):
                        ps = psA if m % 2 == 0 else psB
                        inst = tensor.matmul(ps[:], wob[:, k * 128:(k + 1) * 128],
                                             hdn[:, k * 512:(k + 1) * 512],
                                             start=(k == 0), stop=(k == 15))
                    inst.then_inc(mmA.h, 1)
                ln = 2 * n
                tensor.wait_ge(cst.h, ln + 1)
                tensor.wait_ge(cstd.h, ln + 1)
                tensor.wait_ge(sqq.h, ln + 1)
                tensor.wait_ge(wse.h, 3 + 2 * ln + 1)
                inst = None
                for k in range(16):
                    inst = tensor.matmul(psM[:], one[:], sumb[:, k * 512:(k + 1) * 512],
                                         start=(k == 0), stop=(k == 15))
                inst.then_inc(mmS.h, 1)
                inst = None
                for k in range(16):
                    inst = tensor.matmul(psQ[:], one[:], hdn[:, k * 512:(k + 1) * 512],
                                         start=(k == 0), stop=(k == 15))
                inst.then_inc(mmS.h, 1)
                tensor.wait_ge(apq.h, 16 * (ln + 1))
                tensor.wait_ge(apqd.h, ln + 1)
                for m in range(64):
                    tensor.wait_ge(lgw[m % 2].h, 32 * ((64 * n + m) // 2 + 1))
                    if m >= 1:
                        tensor.wait_ge(seB.h, 64 * n + m)
                        tensor.wait_ge(evB.h, 2 * (64 * n + m) - 1)
                    inst = None
                    for k in range(16):
                        inst = tensor.matmul(psA[:], gwb[m % 2][:, k * 128:(k + 1) * 128],
                                             x1bn[:, k * 512:(k + 1) * 512],
                                             start=(k == 0), stop=(k == 15))
                    inst.then_inc(mmB.h, 1)
                    inst = None
                    for k in range(16):
                        inst = tensor.matmul(psB[:], uwb[m % 2][:, k * 128:(k + 1) * 128],
                                             x1bn[:, k * 512:(k + 1) * 512],
                                             start=(k == 0), stop=(k == 15))
                    inst.then_inc(mmB.h, 1)
                tensor.wait_ge(sqq2.h, 2 * (n + 1))
                tensor.wait_ge(wse2.h, 3 * n + 2)
                inst = None
                for k in range(64):
                    src = y1q[:, k * 512:(k + 1) * 512] if k < 32 \
                        else utq[:, (k - 32) * 512:(k - 31) * 512]
                    inst = tensor.matmul(psQ2[:], one[:], src,
                                         start=(k == 0), stop=(k == 63))
                inst.then_inc(mmS2.h, 1)
                tensor.wait_ge(ap2.h, 64 * (n + 1))
                tensor.wait_ge(ap2d.h, n + 1)
                for m in range(16):
                    tensor.wait_ge(ldw.h, 16 * (16 * n + m + 1))
                    if m >= 2:
                        tensor.wait_ge(evC.h, 16 * n + m - 1)
                    inst = None
                    for k in range(64):
                        ps = psA if m % 2 == 0 else psB
                        inst = tensor.matmul(ps[:], wpb[:, k * 128:(k + 1) * 128],
                                             hdn[:, k * 512:(k + 1) * 512],
                                             start=(k == 0), stop=(k == 63))
                    inst.then_inc(mmC.h, 1)
                ln = 2 * n + 1
                tensor.wait_ge(cst.h, ln + 1)
                tensor.wait_ge(cstd.h, ln + 1)
                tensor.wait_ge(sqq.h, ln + 1)
                tensor.wait_ge(wse.h, 3 + 2 * ln + 1)
                inst = None
                for k in range(16):
                    inst = tensor.matmul(psM[:], one[:], sumb[:, k * 512:(k + 1) * 512],
                                         start=(k == 0), stop=(k == 15))
                inst.then_inc(mmS.h, 1)
                inst = None
                for k in range(16):
                    inst = tensor.matmul(psQ[:], one[:], hdn[:, k * 512:(k + 1) * 512],
                                         start=(k == 0), stop=(k == 15))
                inst.then_inc(mmS.h, 1)

        @block.scalar
        def _(scalar):
            wn = [0]
            w2n = [0]

            def wdummy():
                scalar.wait_ge(wse.h, wn[0])
                wn[0] += 1
                return scalar.activation(warm[:, 0:16], warm[:, 16:32], AF.Exp
                                         ).then_inc(wse.h, 1)

            def w2dummy():
                scalar.wait_ge(wse2.h, w2n[0])
                w2n[0] += 1
                return scalar.activation(warm[:, 0:16], warm[:, 16:32], AF.Exp
                                         ).then_inc(wse2.h, 1)

            scalar.memzero(warm[:]).then_inc(wse.h, 1)
            wn[0] = 1
            wdummy()
            wdummy()
            scalar.wait_ge(lres.h, 16)
            for n in range(NSL):
                ln = 2 * n
                scalar.wait_ge(evA.h, 16 * (n + 1))
                scalar.wait_ge(evAd.h, n + 1)
                scalar.activation(hdn[:, 0:8192], y1raw[:], AF.Square).then_inc(sqq.h, 1)
                wdummy()
                scalar.wait_ge(stq.h, 5 * ln + 2)
                scalar.activation(tmp6, tmp5, AF.Sqrt).then_inc(stq.h, 1)
                wdummy()
                for m in range(64):
                    scalar.wait_ge(mmB.h, 2 * (64 * n + m) + 1)
                    scalar.wait_ge(evB.h, 2 * (64 * n + m))
                    scalar.activation(gm[:], psA[:], AF.Silu).then_inc(seB.h, 1)
                scalar.wait_ge(evB.h, 128 * (n + 1))
                scalar.wait_ge(evBd.h, n + 1)
                scalar.wait_ge(stX3.h, 16 * (n + 1))
                scalar.activation(y1q[:], hdn[:, 0:16384], AF.Square).then_inc(sqq2.h, 1)
                w2dummy()
                scalar.activation(utq[:], hdn[:, 16384:32768], AF.Square).then_inc(sqq2.h, 1)
                w2dummy()
                scalar.wait_ge(rq.h, n + 1)
                scalar.wait_ge(rqd.h, n + 1)
                scalar.activation(rrow[0:1, 512:1024], rrow[0:1, 0:512], AF.Sqrt
                                  ).then_inc(rq2.h, 1)
                w2dummy()
                ln = 2 * n + 1
                scalar.wait_ge(evC.h, 16 * (n + 1))
                scalar.wait_ge(evCd.h, n + 1)
                scalar.activation(hdn[:, 0:8192], y2raw[:], AF.Square).then_inc(sqq.h, 1)
                wdummy()
                scalar.wait_ge(stq.h, 5 * ln + 2)
                scalar.activation(tmp6, tmp5, AF.Sqrt).then_inc(stq.h, 1)
                wdummy()

        @block.vector
        def _(vector):
            vn = [0]

            def vdummy():
                return vector.tensor_copy(dum2[0:1, 0:1], one[0:1, 0:1])

            def ln_stats_apply(n, ph, src_):
                ln = 2 * n + ph
                vector.wait_ge(mmS.h, 2 * ln + 2)
                vector.tensor_scalar(srow[:, 0:512], psM[:], 1.0 / 2048.0, None, ALU.mult
                                     ).then_inc(vch.h, 1); vn[0] += 1
                s1 = vn[0]
                vector.tensor_scalar(tmp5, psQ[:], 1.0 / 2048.0, None, ALU.mult
                                     ).then_inc(vch.h, 1); vn[0] += 1
                vector.wait_ge(vch.h, s1)
                vector.tensor_tensor(tmp6, srow[:, 0:512], srow[:, 0:512], ALU.mult
                                     ).then_inc(vch.h, 1); vn[0] += 1
                vector.wait_ge(vch.h, vn[0])
                vector.tensor_sub(tmp5, tmp5, tmp6).then_inc(vch.h, 1); vn[0] += 1
                vector.wait_ge(vch.h, vn[0])
                vector.tensor_scalar(tmp5, tmp5, 1e-5, None, ALU.add).then_inc(stq.h, 1)
                vdummy().then_inc(stq.h, 1)
                vector.wait_ge(stq.h, 5 * ln + 3)
                vector.wait_ge(wse.h, 3 + 2 * ln + 2)
                vector.reciprocal(srow[:, 512:1024], tmp6).then_inc(stq.h, 1)
                vdummy().then_inc(stq.h, 1)
                vector.wait_ge(rbq.h, 16 * (ln + 1))
                for m in range(16):
                    vector.wait_ge(vch.h, vn[0])
                    if ph == 1 and m >= 1:
                        vector.wait_ge(apq.h, 16 * ln + m)
                    vector.tensor_sub(tmpa[:], src_[:, m * 512:(m + 1) * 512], mb[:, 0:512]
                                      ).then_inc(vch.h, 1); vn[0] += 1
                    vector.wait_ge(vch.h, vn[0])
                    if ph == 0:
                        vector.tensor_mul(y1raw[:, m * 512:(m + 1) * 512], tmpa[:],
                                          mb[:, 512:1024]).then_inc(vch.h, 1); vn[0] += 1
                        vector.wait_ge(vch.h, vn[0])
                        vector.tensor_copy(x1bn[:, m * 512:(m + 1) * 512],
                                           y1raw[:, m * 512:(m + 1) * 512]).then_inc(apq.h, 1)
                    else:
                        vector.tensor_mul(y2raw[:, m * 512:(m + 1) * 512], tmpa[:],
                                          mb[:, 512:1024]).then_inc(apq.h, 1)
                vdummy().then_inc(apqd.h, 1)

            vector.wait_ge(lres.h, 16)
            for n in range(NSL):
                vector.wait_ge(lut.h, 16 * (n + 1))
                for m in range(16):
                    vector.wait_ge(mmA.h, 16 * n + m + 1)
                    ps = psA if m % 2 == 0 else psB
                    vector.tensor_add(y1raw[:, m * 512:(m + 1) * 512], ps[:],
                                      utn[:, m * 512:(m + 1) * 512]).then_inc(evA.h, 1)
                vdummy().then_inc(evAd.h, 1)
                vector.wait_ge(evA.h, 16 * (n + 1))
                vector.tensor_copy(sumb[:], y1raw[:]).then_inc(cst.h, 1)
                vdummy().then_inc(cstd.h, 1)
                ln_stats_apply(n, 0, y1raw)
                for m in range(64):
                    vector.wait_ge(mmB.h, 2 * (64 * n + m) + 2)
                    vector.wait_ge(evB.h, 2 * (64 * n + m))
                    vector.tensor_copy(um[:], psB[:]).then_inc(evB.h, 1)
                    vector.wait_ge(seB.h, 64 * n + m + 1)
                    vector.wait_ge(evB.h, 2 * (64 * n + m) + 1)
                    vector.tensor_tensor(hdn[:, m * 512:(m + 1) * 512], gm[:], um[:],
                                         ALU.mult).then_inc(evB.h, 1)
                vdummy().then_inc(evBd.h, 1)
                vector.wait_ge(mmS2.h, n + 1)
                vector.tensor_scalar(rrow[0:1, 0:512], psQ2[:], 1.0 / 8192.0, RMS_EPS2,
                                     ALU.mult, ALU.add).then_inc(rq.h, 1)
                vdummy().then_inc(rqd.h, 1)
                vector.wait_ge(rq2.h, n + 1)
                vector.wait_ge(wse2.h, 3 * (n + 1))
                vector.reciprocal(rrow[0:1, 1024:1536], rrow[0:1, 512:1024]).then_inc(rq3.h, 1)
                vdummy().then_inc(rq3d.h, 1)
                vector.wait_ge(rb2.h, 16 * (n + 1))
                for m in range(64):
                    vector.tensor_mul(hdn[:, m * 512:(m + 1) * 512],
                                      hdn[:, m * 512:(m + 1) * 512], mb2[:]).then_inc(ap2.h, 1)
                vdummy().then_inc(ap2d.h, 1)
                vector.wait_ge(lx1.h, 16 * (n + 1))
                for m in range(16):
                    vector.wait_ge(mmC.h, 16 * n + m + 1)
                    ps = psA if m % 2 == 0 else psB
                    vector.tensor_add(y2raw[:, m * 512:(m + 1) * 512], ps[:],
                                      x1fn[:, m * 512:(m + 1) * 512]).then_inc(evC.h, 1)
                vdummy().then_inc(evCd.h, 1)
                vector.wait_ge(evC.h, 16 * (n + 1))
                vector.tensor_copy(sumb[:], y2raw[:]).then_inc(cst.h, 1)
                vdummy().then_inc(cstd.h, 1)
                ln_stats_apply(n, 1, y2raw)

        @block.sync
        def _(sync):
            for n in range(NSL):
                ln = 2 * n
                sync.wait_ge(stq.h, 5 * (ln + 1))
                sync.dma_start(out=scr[ln % 8], in_=srow[:]).then_inc(rwq.h, 16)
                sync.wait_ge(rwq.h, 16 * (ln + 1))
                sync.dma_start(out=mb[:], in_=scr[ln % 8].partition_broadcast(128)
                               ).then_inc(rbq.h, 16)
                sync.wait_ge(apq.h, 16 * (ln + 1))
                sync.wait_ge(apqd.h, ln + 1)
                sync.dma_start(
                    out=X3F.rearrange("(k p) t -> p k t", p=128)[:, :, n * 512:(n + 1) * 512],
                    in_=y1raw[:].rearrange("p (k t) -> p k t", k=16)).then_inc(stX3.h, 16)
                sync.wait_ge(rq3.h, n + 1)
                sync.wait_ge(rq3d.h, n + 1)
                sync.dma_start(out=scr2[0], in_=rrow[0:1, 1024:1536]).then_inc(rw2.h, 16)
                sync.wait_ge(rw2.h, 16 * (n + 1))
                sync.dma_start(out=mb2[:], in_=scr2[0].partition_broadcast(128)
                               ).then_inc(rb2.h, 16)
                for m in range(16):
                    if n * 16 + m >= 1:
                        sync.wait_ge(mmC.h, 16 * n + m)
                    sync.dma_start(out=wpb[:].rearrange("p (k f) -> p k f", k=64),
                                   in_=WPB[m].rearrange("k p f -> p k f")
                                   ).then_inc(ldw.h, 16)
                ln = 2 * n + 1
                sync.wait_ge(stq.h, 5 * (ln + 1))
                sync.dma_start(out=scr[ln % 8], in_=srow[:]).then_inc(rwq.h, 16)
                sync.wait_ge(rwq.h, 16 * (ln + 1))
                sync.dma_start(out=mb[:], in_=scr[ln % 8].partition_broadcast(128)
                               ).then_inc(rbq.h, 16)
                sync.wait_ge(apq.h, 16 * (ln + 1))
                sync.wait_ge(apqd.h, ln + 1)
                sync.dma_start(
                    out=OUT.rearrange("(k p) t -> p k t", p=128)[:, :, n * 512:(n + 1) * 512],
                    in_=y2raw[:].rearrange("p (k t) -> p k t", k=16)).then_inc(stX2.h, 16)
            sync.wait_ge(stX2.h, 16 * NSL)

    return nc


def run_l1_tail(inputs, x2_results, scan_results, qgu, qp):
    Wo = np.asarray(inputs["L1_attn_Wo"], np.float32)
    WOB1 = np.ascontiguousarray(
        Wo.reshape(16, 128, 16, 128).transpose(2, 0, 1, 3).astype(ml_dtypes.bfloat16))
    WGUT = np.ascontiguousarray(np.asarray(qgu).T)
    GW = WGUT.reshape(16, 128, 128, 128).transpose(2, 0, 1, 3)
    GWB1 = np.ascontiguousarray(GW[0:64])
    UWB1 = np.ascontiguousarray(GW[64:128])
    WPT = np.ascontiguousarray(np.asarray(qp).T)
    WPB1 = np.ascontiguousarray(WPT.reshape(64, 128, 16, 128).transpose(2, 0, 1, 3))
    ONE = np.ones((128, 1), dtype=ml_dtypes.bfloat16)
    ofull = {}
    for b in range(4):
        ofull[b] = np.concatenate(
            [np.asarray(scan_results[2 * b]["O1"]).T,
             np.asarray(scan_results[2 * b + 1]["O1"]).T], axis=1)
    in_maps = []
    for c in range(8):
        b, role = divmod(c, 2)
        of = np.ascontiguousarray(ofull[b][slot_tokens(role)].T)
        in_maps.append({"O": of, "X2IN": np.ascontiguousarray(x2_results[c]["X2"]),
                        "WOB": WOB1, "GWB": GWB1, "UWB": UWB1, "WPB": WPB1, "ONE": ONE})
    nc = build_l1_tail()
    res = run_bass_kernel_spmd(nc, in_maps, core_ids=list(range(8)))
    return res.results


def kernel(**inputs):
    qgu, qp = run_quant(inputs)
    proj = run_l0_proj(inputs)
    att = run_l0_attn(inputs, proj)
    tail = run_l0_tail(inputs, att)
    qkv = run_l1_qkv(inputs, tail)
    scan = run_l1_scan(inputs, qkv)
    fin = run_l1_tail(inputs, tail, scan, qgu, qp)
    out = np.zeros((B, S, D), dtype=np.float32)
    for c in range(8):
        b, role = divmod(c, 2)
        out[b, slot_tokens(role), :] = np.asarray(fin[c]["OUT"], np.float32).T
    return out
